# version 07
# baseline (speedup 1.0000x reference)
import numpy as np
import ml_dtypes
import concourse.bass as bass
import concourse.mybir as mybir
from concourse.bass_utils import run_bass_kernel_spmd

F32 = mybir.dt.float32
BF16 = mybir.dt.bfloat16
AF = mybir.ActivationFunctionType
ALU = mybir.AluOpType
AX = mybir.AxisListType
bfloat16 = ml_dtypes.bfloat16

N_CORES = 8
B_LOC = 2
S = 2048
D = 1024
NT = S // 128
PHASES = 4
DBG_TAP = None
FFN_DBG_STOP = None

class Ctr:
    def __init__(self, name):
        self.name, self.n, self.h = name, 0, None

class Plan:
    def __init__(self):
        self.items = {k: [] for k in "TSVGY"}
        self.ctrs = []
        self.queues = {}
    def ctr(self, name):
        c = Ctr(name); self.ctrs.append(c); return c
    def mkqueue(self, eng, n=8):
        self.queues[eng] = [self.ctr(f"d{eng}{i}") for i in range(n)], [0]
    def dma(self, eng, fn, waits=()):
        cs, kbox = self.queues[eng]
        c = cs[kbox[0] % len(cs)]
        kbox[0] += 1
        w = list(waits)
        if c.n > 0:
            w.append((c, c.n))
        self.op(eng, fn, waits=w, inc=c, by=16, _dma=True)
        return (c, c.n)
    def dma_all(self):
        toks = []
        for cs, _ in self.queues.values():
            toks += [(c, c.n) for c in cs if c.n > 0]
        return toks
    def op(self, eng, fn, waits=(), inc=None, by=1, _dma=False):
        if eng in ("V", "S", "G") and not _dma:
            c = self.auto[eng]
            waits = list(waits) + [(c, c.n)]
            if inc is None:
                inc, by = c, 1
            else:
                assert inc is c, f"{eng} op with foreign inc"
        for (c, v) in waits:
            assert v <= c.n, f"wait on future: {c.name} {v} > {c.n}"
        waits = tuple(waits)
        if inc is not None:
            inc.n += by
        def run(e, fn=fn, waits=waits, inc=inc, by=by):
            for (c, v) in waits:
                if v > 0:
                    e.wait_ge(c.h, v)
            ins = fn(e)
            if inc is not None:
                ins.then_inc(inc.h, by)
            return ins
        self.items[eng].append(run)
        return inc.n if inc is not None else None
    def grp(self, eng, ctr, by=1):
        ctr.n += by
        prev_run = self.items[eng].pop()
        def run2(e, prev_run=prev_run, ctr=ctr, by=by):
            ins = prev_run(e)
            ins.then_inc(ctr.h, by)
            return ins
        self.items[eng].append(run2)
        return ctr.n
    def wait(self, eng, waits):
        for (c, v) in waits:
            assert v <= c.n, f"wait on future: {c.name} {v} > {c.n}"
        waits = tuple(waits)
        def run(e, waits=waits):
            for (c, v) in waits:
                if v > 0:
                    e.wait_ge(c.h, v)
        self.items[eng].append(run)
    def join(self, engs, ctrs, dmas=True):
        for eng in engs:
            w = [(c, c.n) for c in ctrs]
            if dmas:
                w += self.dma_all()
            self.wait(eng, w)

class Mem:
    def __init__(self, nc):
        self.size = (nc.sbuf_bytes_remaining - 4096) & ~63
        self.nc = nc
        self.handle = nc.alloc_sbuf_tensor("pool", [128, self.size], mybir.dt.uint8)
        self.pap = self.handle.ap() if hasattr(self.handle, "ap") else self.handle[:]
        self.cur = 0
        self.reg = {}
        self.offs = {}
    def at(self, name, shape, dtype, off):
        nbytes = int(np.prod(shape[1:])) * mybir.dt.size(dtype)
        assert off + nbytes <= self.size, f"{name}: {off}+{nbytes} > {self.size}"
        ap = self.pap[:, off:off + nbytes].bitcast(dtype)
        if shape[0] != 128:
            ap = ap[0:shape[0], :]
        self.reg[name] = ap
        self.offs[name] = (off, tuple(shape), dtype)
        return ap
    def seq(self, name, shape, dtype):
        off = (self.cur + 63) & ~63
        t = self.at(name, shape, dtype, off)
        self.cur = off + int(np.prod(shape[1:])) * mybir.dt.size(dtype)
        return t
    def mark(self):
        return self.cur
    def reset(self, mark):
        self.cur = mark

def _rotary_tables():
    DIM_HEAD, WINDOW = 64, 512
    half_idx = np.arange(0, DIM_HEAD, 2, dtype=np.float64)
    inv_freq = 1.0 / 10000.0 ** (half_idx / DIM_HEAD)
    t = np.arange(S, dtype=np.float64)
    freqs = t[:, None] * inv_freq[None, :]
    freqs = np.concatenate([freqs, freqs], axis=-1)
    base = (half_idx + 0.4 * DIM_HEAD) / (1.4 * DIM_HEAD)
    power = (t - S // 2) / float(WINDOW // 2)
    scale = base[None, :] ** power[:, None]
    scale = np.concatenate([scale, scale], axis=-1)
    cos, sin = np.cos(freqs), np.sin(freqs)
    return ((cos * scale).astype(np.float32), (sin * scale).astype(np.float32),
            (cos / scale).astype(np.float32), (sin / scale).astype(np.float32))

def prepare_host(inp):
    f32 = np.float32
    g = lambda k: np.asarray(inp[k])
    d = {}
    d["aWqT"] = np.ascontiguousarray(g("L0_attn_a_Wq").T.astype(f32))
    d["aWkT"] = np.ascontiguousarray(g("L0_attn_a_Wk").T.astype(f32))
    d["aWvT"] = np.ascontiguousarray(g("L0_attn_a_Wv").T.astype(f32))
    d["aWoT"] = np.ascontiguousarray(g("L0_attn_a_Wo").T.astype(f32))
    a_bqk = np.zeros((128, 16), f32)
    a_bqk[:, 0::2] = g("L0_attn_a_bq").astype(f32).reshape(8, 128).T
    a_bqk[:, 1::2] = g("L0_attn_a_bk").astype(f32).reshape(8, 128).T
    d["a_bqk"] = a_bqk
    d["a_bv"] = g("L0_attn_a_bv").astype(f32).reshape(1, D)
    d["a_bo"] = g("L0_attn_a_bo").astype(f32).reshape(1, D)
    d["bWqT"] = np.ascontiguousarray(g("L0_attn_b_Wq").T.astype(f32))
    d["bWkT"] = np.ascontiguousarray(g("L0_attn_b_Wk").T.astype(f32))
    d["bWvT"] = np.ascontiguousarray(g("L0_attn_b_Wv").T.astype(f32))
    d["bWpT"] = np.ascontiguousarray(g("L0_attn_b_Wp").T.astype(f32))
    d["bW1T"] = np.ascontiguousarray(g("L0_attn_b_W1").T.astype(f32))
    d["b_b1"] = g("L0_attn_b_b1").astype(f32).reshape(64, 1)
    d["bW2aug"] = np.ascontiguousarray(np.concatenate(
        [g("L0_attn_b_W2").T.astype(f32), g("L0_attn_b_b2").astype(f32).reshape(1, D)], 0))
    HID, NHC = 2730, 22
    d["f_wqkv"] = np.ascontiguousarray(g("L0_ffn_wqkv").astype(f32))
    d["f_wo"] = np.ascontiguousarray(g("L0_ffn_wo").astype(f32))
    w1 = g("L0_ffn_w1").astype(f32)
    w1p = np.zeros((2, D, NHC, 256), f32)
    for c in range(NHC):
        w = min(128, HID - c * 128)
        w1p[:, :, c, 0:w] = w1[:, :, c*128:c*128+w]
        w1p[:, :, c, 128:128+w] = w1[:, :, HID + c*128:HID + c*128+w]
    d["f_w1p"] = np.ascontiguousarray(w1p.reshape(2, D, NHC * 256))
    b1 = g("L0_ffn_b1").astype(f32)
    b1c = np.zeros((2, 128, 2 * NHC), f32)
    for c in range(NHC):
        w = min(128, HID - c * 128)
        b1c[:, 0:w, 2*c] = b1[:, c*128:c*128+w]
        b1c[:, 0:w, 2*c+1] = b1[:, HID + c*128:HID + c*128+w]
    d["f_b1c"] = np.ascontiguousarray(b1c)
    w2 = g("L0_ffn_w2").astype(f32)
    w2p = np.zeros((2, NHC, 128, D), f32)
    for c in range(NHC):
        w = min(128, HID - c * 128)
        w2p[:, c, 0:w, :] = w2[:, c*128:c*128+w, :]
    d["f_w2p"] = np.ascontiguousarray(w2p)
    d["f_b2"] = g("L0_ffn_b2").astype(f32).reshape(2, 1, D)
    d["f_lng"] = g("L0_ffn_ln_g").astype(f32).reshape(2, 1, D)
    d["f_lnb"] = g("L0_ffn_ln_b").astype(f32).reshape(2, 1, D)
    d["f_qs8"] = np.ascontiguousarray(np.tile(g("L0_ffn_q_scale").astype(f32), (1, 8)))
    d["f_ks8"] = np.ascontiguousarray(np.tile(g("L0_ffn_k_scale").astype(f32), (1, 8)))
    d["ident"] = np.eye(128, dtype=bfloat16)
    r = np.arange(128)
    d["maskP12"] = np.concatenate(
        [(r[:, None] <= r[None, :]).astype(bfloat16),
         (r[:, None] >= r[None, :]).astype(bfloat16)], axis=1)
    d["onesbf"] = np.ones((1, 128), dtype=bfloat16)
    r2 = np.arange(128)
    maskadd = np.zeros((128, 256), np.float32)
    maskadd[:, 0:128][r2[:, None] > r2[None, :]] = -1e30
    maskadd[:, 128:256][r2[:, None] < r2[None, :]] = -1e30
    d["maskadd"] = maskadda = maskadd
    d["onesf32"] = np.ones((1, 128), dtype=np.float32)
    d["negrow"] = np.full((1, S), -1.0, dtype=bfloat16)
    ind8 = np.zeros((128, 8), dtype=bfloat16)
    for rr in range(128):
        ind8[rr, rr // 16] = 1
    d["ind8"] = ind8
    d["rot_qc"], d["rot_qs"], d["rot_kc"], d["rot_ks"] = _rotary_tables()
    return d

def ln_tile(ctx, x_sb, outs, slot, scratch, pre_waits=()):
    p, C, sb = ctx["p"], ctx["C"], ctx["sb"]
    st = sb["stats"]
    o = slot * 8
    ssum = st[:, o:o+1]; ssq = st[:, o+1:o+2]; mu = st[:, o+2:o+3]
    m2 = st[:, o+3:o+4]; var = st[:, o+4:o+5]; istd = st[:, o+5:o+6]
    nb = st[:, o+6:o+7]; lnv = st[:, o+7:o+8]
    vC, sC = C["vC"], C["sC"]
    p.op("V", lambda e: e.reduce_sum(ssum, x_sb, axis=AX.X), waits=list(pre_waits), inc=vC)
    p.op("S", lambda e: e.activation(scratch, x_sb, AF.Square, accum_out=ssq),
         waits=list(pre_waits), inc=sC)
    svv = sC.n
    p.op("V", lambda e: e.tensor_scalar_mul(mu, ssum, 1.0 / D))
    p.op("V", lambda e: e.tensor_scalar_mul(m2, ssq, 1.0 / D), waits=[(sC, svv)])
    p.op("V", lambda e: e.tensor_mul(var, mu, mu))
    p.op("V", lambda e: e.tensor_sub(var, m2, var))
    p.op("V", lambda e: e.tensor_scalar_add(var, var, 1e-5), inc=vC)
    vstat = vC.n
    p.op("S", lambda e: e.activation(lnv, var, AF.Ln), waits=[(vC, vstat)])
    p.op("S", lambda e: e.activation(istd, lnv, AF.Exp, scale=-0.5), inc=sC)
    sistd = sC.n
    p.op("V", lambda e: e.tensor_mul(nb, mu, istd), waits=[(sC, sistd)])
    p.op("V", lambda e: e.tensor_scalar_mul(nb, nb, -1.0), inc=vC)
    vnb = vC.n
    for ap in outs:
        p.op("V", lambda e, ap=ap: e.tensor_scalar(
            ap, x_sb, istd, nb, ALU.mult, ALU.add),
            waits=[(vC, vnb), (sC, sistd)], inc=vC)
    return sC.n, vC.n

def transpose_tile(ctx, src_bf, dst_fn, ncol, pre_wait=(), bank=7):
    p, C, sb, PS = ctx["p"], ctx["C"], ctx["sb"], ctx["PS"]
    tC, vC = C["tC"], C["vC"]
    done = 0
    j0 = 0
    while j0 < ncol:
        w = min(4, ncol - j0)
        for j in range(w):
            jj = j0 + j
            wl = list(pre_wait) if jj == 0 else []
            if j == 0:
                wl.append((vC, vC.n))
            p.op("T", lambda e, jj=jj, j=j: e.matmul(
                PS(bank)[:, 128*j:128*(j+1)], src_bf[:, 128*jj:128*(jj+1)],
                sb["ident"][:], start=True, stop=True), waits=wl)
        p.grp("T", tC)
        tv = tC.n
        p.op("V", lambda e, j0=j0, w=w, tv=tv: e.tensor_copy(
            dst_fn(j0, w), PS(bank)[:, 0:128*w].rearrange("p (c k) -> p c k", k=128)),
            waits=[(tC, tv)], inc=vC)
        done = tC.n
        j0 += w
    return done

def emit_attn_a(ctx, src, dst):
    nc, io, p, C, sb, mem, PS = (ctx[k] for k in ["nc", "io", "p", "C", "sb", "mem", "PS"])
    dY, dG, tC, sC, vC, gC = (C[k] for k in ["dY", "dG", "tC", "sC", "vC", "gC"])
    mem.reset(ctx["PHASE0"])
    yF = mem.seq("a_yF", [128, 8 * 2048], BF16)
    QF = mem.seq("a_QF", [128, 8 * 2048], BF16)
    KF = mem.seq("a_KF", [128, 8 * 2048], BF16)
    VT = mem.seq("a_VT", [128, 16 * 1024], BF16)
    wqk = mem.seq("a_wqk", [128, 2 * 1024], BF16)
    wvo = mem.seq("a_wvo", [128, 8 * 1024], BF16)
    xt = mem.seq("a_xt", [128, 4 * 1024], F32)
    ybf = mem.seq("a_ybf", [128, 2 * 1024], BF16)
    sinS = mem.seq("a_sinS", [128, 2 * 1024], BF16)
    scr = mem.seq("a_scr", [128, 1024], F32)
    res = mem.seq("a_res", [128, 2 * 1024], F32)
    rowbf0 = mem.seq("a_rowbf0", [1, D], BF16)
    rowbf1 = mem.seq("a_rowbf1", [1, D], BF16)
    sb["rowbf0"], sb["rowbf1"] = rowbf0, rowbf1
    yFr = lambda: yF[:].rearrange("p (c k) -> p c k", k=2048)

    r0 = p.dma("G", lambda e: e.dma_start(out=sb["rowbf0"][:], in_=io["a_bv"][:]))
    r1 = p.dma("G", lambda e: e.dma_start(out=sb["rowbf1"][:], in_=io["a_bo"][:]))
    ldrows = [r0, r1]

    for b in range(B_LOC):
        ln_s, tr_done = {}, {}
        for t in range(NT):
            x_sb = xt[:, 1024*(t % 4):1024*(t % 4 + 1)]
            w = [(sC, ln_s[t-4][0]), (vC, ln_s[t-4][1])] if t >= 4 else []
            ld = p.dma("Y", lambda e, b=b, t=t, x_sb=x_sb: e.dma_start(
                out=x_sb, in_=src[b, 128*t:128*(t+1), :]), waits=w)
            out_bf = ybf[:, 1024*(t % 2):1024*(t % 2 + 1)]
            pw = [ld]
            if t >= 2:
                pw.append((tC, tr_done[t-2]))
            sN, vN = ln_tile(ctx, x_sb, [out_bf], slot=t % 8, scratch=scr[:], pre_waits=pw)
            ln_s[t] = (sN, vN)
            if DBG_TAP == "attn_a" and b == 0:
                p.dma("Y", lambda e, t=t, out_bf=out_bf: e.dma_start(
                    out=io["dbg2"][:, 1024*t:1024*(t+1)], in_=out_bf), waits=[(vC, vN)])
            tr_done[t] = transpose_tile(
                ctx, out_bf, lambda j0, w_, t=t: yFr()[:, j0:j0+w_, 128*t:128*(t+1)],
                ncol=8, pre_wait=[(vC, vN)])
        p.join(["T", "S", "G"], [vC])
        if DBG_TAP == "attn_a" and b == 0:
            p.wait("Y", [(vC, vC.n)])
            p.dma("Y", lambda e: e.dma_start(out=io["dbg"][:], in_=yF[:]))


        wq_hist = []
        for wi, (Wn, OUT, bw) in enumerate((("aWqT", QF, 0), ("aWkT", KF, 1))):
            for oc in range(8):
                idx = wi * 8 + oc
                wb = wqk[:, 1024*(idx % 2):1024*(idx % 2 + 1)]
                w = [(tC, wq_hist[idx-2])] if idx >= 2 else []
                ldw = p.dma("G", lambda e, Wn=Wn, oc=oc, wb=wb: e.dma_start(
                    out=wb.rearrange("p (c k) -> p c k", k=128),
                    in_=io[Wn][:, 128*oc:128*(oc+1)].rearrange("(c p) k -> p c k", p=128)),
                    waits=w)
                for tg in range(4):
                    bank = tg % 2
                    for kc in range(8):
                        w2 = [ldw, (vC, vC.n)] if kc == 0 else []
                        p.op("T", lambda e, wb=wb, tg=tg, kc=kc, bank=bank: e.matmul(
                            PS(bank), wb[:, 128*kc:128*(kc+1)],
                            yFr()[:, kc, 512*tg:512*(tg+1)],
                            start=(kc == 0), stop=(kc == 7)), waits=w2)
                    p.grp("T", tC)
                    mmv = tC.n
                    p.op("V", lambda e, OUT=OUT, oc=oc, tg=tg, bank=bank, bw=bw:
                         e.tensor_scalar_add(
                            OUT[:].rearrange("p (c k) -> p c k", k=2048)[:, oc,
                                                                        512*tg:512*(tg+1)],
                            PS(bank), sb["a_bqk"][:, 2*oc+bw:2*oc+bw+1]),
                        waits=[(tC, mmv)], inc=vC)
                wq_hist.append(tC.n)
        p.join(["T"], [vC])


        ldv = p.dma("G", lambda e: e.dma_start(
            out=wvo[:].rearrange("p (c k) -> p c k", k=1024),
            in_=io["aWvT"][:].rearrange("(c p) k -> p c k", p=128)))
        for t in range(NT):
            for half in range(2):
                bank = (2 * t + half) % 2
                for kc in range(8):
                    w2 = [ldv, (vC, vC.n)] if kc == 0 else []
                    p.op("T", lambda e, t=t, half=half, kc=kc, bank=bank: e.matmul(
                        PS(bank), yFr()[:, kc, 128*t:128*(t+1)],
                        wvo[:, 1024*kc + 512*half: 1024*kc + 512*(half+1)],
                        start=(kc == 0), stop=False), waits=w2)
                p.op("T", lambda e, half=half, bank=bank: e.matmul(
                    PS(bank), sb["onesbf"][:, 0:128], sb["rowbf0"][:, 512*half:512*(half+1)],
                    start=False, stop=True),
                    waits=(ldrows if (t == 0 and half == 0) else []))
                p.grp("T", tC)
                mmv = tC.n
                p.op("V", lambda e, t=t, half=half, bank=bank: e.tensor_copy(
                    VT[:, 1024*t + 512*half: 1024*t + 512*(half+1)], PS(bank)),
                    waits=[(tC, mmv)], inc=vC)
        p.join(["T"], [vC])
        if DBG_TAP == "attn_a" and b == 0:
            p.wait("Y", [(vC, vC.n)])
            p.dma("Y", lambda e: e.dma_start(out=io["dbg3"][:], in_=VT[:]))

        ldwo = p.dma("G", lambda e: e.dma_start(
            out=wvo[:].rearrange("p (c k) -> p c k", k=1024),
            in_=io["aWoT"][:].rearrange("(c p) k -> p c k", p=128)),
            waits=[(vC, vC.n)])
        sin_done, av_done = {}, {}
        for hp in range(8):
            for qg in range(4):
                for kc in range(16):
                    par = kc % 2
                    for which in range(2):
                        bank = 2 * par + which
                        base = 64 * which
                        w = ([(sC, sin_done[(hp, qg, kc - 2)])]
                             if (hp, qg, kc - 2) in sin_done else [])
                        if kc == 0 and which == 0 and (hp, qg) == (0, 0):
                            w = [(sC, sC.n)]
                        p.op("T", lambda e, hp=hp, qg=qg, kc=kc, bank=bank, base=base: e.matmul(
                            PS(bank),
                            KF[base:base+64, 2048*hp + 128*kc: 2048*hp + 128*(kc+1)],
                            QF[base:base+64, 2048*hp + 512*qg: 2048*hp + 512*(qg+1)],
                            start=True, stop=True), waits=w)
                    p.grp("T", tC)
                    smm = tC.n
                    slot = par
                    w = [(tC, smm)]
                    if (hp, qg, kc - 2) in av_done:
                        w.append((tC, av_done[(hp, qg, kc - 2)]))
                    elif kc < 2 and (hp, qg) != (0, 0):
                        w.append((tC, av_prev))
                    p.op("S", lambda e, par=par, slot=slot: e.activation(
                        sinS[:, 1024*slot:1024*(slot+1)], PS(2*par, 2), AF.Sin, scale=3.75),
                        waits=w, inc=sC)
                    sin_done[(hp, qg, kc)] = sC.n
                    for which in range(2):
                        w = [(sC, sin_done[(hp, qg, kc)])] if which == 0 else []
                        if kc == 0 and which == 0:
                            w.append((vC, vC.n))
                        p.op("T", lambda e, hp=hp, kc=kc, which=which, slot=slot: e.matmul(
                            PS(4)[64*which:64*(which+1), :],
                            VT[:, 1024*kc + 128*hp + 64*which: 1024*kc + 128*hp + 64*(which+1)],
                            sinS[:, 1024*slot + 512*which: 1024*slot + 512*(which+1)],
                            start=(kc == 0), stop=(kc == 15)), waits=w)
                    p.grp("T", tC)
                    av_done[(hp, qg, kc)] = tC.n
                av_prev = av_done[(hp, qg, 15)]
                p.op("V", lambda e, hp=hp, qg=qg, v=av_prev: e.tensor_copy(
                    yF[:, 2048*hp + 512*qg: 2048*hp + 512*(qg+1)], PS(4)),
                    waits=[(tC, av_prev)], inc=vC)
        p.join(["T"], [vC])

        add_done = 0
        st_tok = {}
        for t in range(NT):
            x_sb = xt[:, 1024*(t % 2):1024*(t % 2 + 1)]
            w2 = [(vC, add_done)] if t >= 2 else [(vC, vC.n), (sC, sC.n)]
            ldx = p.dma("Y", lambda e, b=b, t=t, x_sb=x_sb: e.dma_start(
                out=x_sb, in_=src[b, 128*t:128*(t+1), :]), waits=w2)
            rslot = t % 2
            for half in range(2):
                bank = 5 + half
                for kc in range(8):
                    w3 = [(vC, vC.n)] if kc == 0 else []
                    if kc == 0 and t == 0 and half == 0:
                        w3.append(ldwo)
                    p.op("T", lambda e, t=t, kc=kc, half=half, bank=bank: e.matmul(
                        PS(bank), yF[:, 2048*kc + 128*t: 2048*kc + 128*(t+1)],
                        wvo[:, 1024*kc + 512*half: 1024*kc + 512*(half+1)],
                        start=(kc == 0), stop=False), waits=w3)
                p.op("T", lambda e, half=half, bank=bank: e.matmul(
                    PS(bank), sb["onesbf"][:, 0:128], sb["rowbf1"][:, 512*half:512*(half+1)],
                    start=False, stop=True))
                p.grp("T", tC)
                mmv = tC.n
                wv = [(tC, mmv), ldx]
                if half == 0 and t >= 2:
                    wv.append(st_tok[t-2])
                p.op("V", lambda e, rslot=rslot, half=half, bank=bank, x_sb=x_sb: e.tensor_add(
                    res[:, 1024*rslot + 512*half: 1024*rslot + 512*(half+1)],
                    x_sb[:, 512*half:512*(half+1)], PS(bank)),
                    waits=wv, inc=vC)
            add_done = vC.n
            st_tok[t] = p.dma("Y", lambda e, b=b, t=t, rslot=rslot: e.dma_start(
                out=dst[b, 128*t:128*(t+1), :], in_=res[:, 1024*rslot:1024*(rslot+1)]),
                waits=[(vC, add_done)])
        p.join(["T", "S", "V", "G", "Y"], [vC, sC, tC])

def emit_ffn(ctx, stream, dst):
    nc, io, p, C, sb, mem, PS = (ctx[k] for k in ["nc", "io", "p", "C", "sb", "mem", "PS"])
    dY, dG, tC, sC, vC, gC = (C[k] for k in ["dY", "dG", "tC", "sC", "vC", "gC"])
    zs = io["zs"]
    NHC = 22
    CLIP = [(0, 128), (0, 256), (0, 384), (0, 512), (0, 512), (128, 512), (256, 512), (384, 512)]

    mem.reset(ctx["PHASE0"])
    hF = mem.seq("f_hF", [128, 8 * 1024], BF16)
    hF_off = mem.mark() - 8 * 1024 * 2
    qpk = mem.seq("f_qpk", [128, 4 * 2048], BF16)
    kpk = mem.seq("f_kpk", [128, 4 * 2048], BF16)
    kpk_off = mem.mark() - 4 * 2048 * 2
    vaug = mem.seq("f_vaug", [128, 16 * 520], BF16)
    wbuf = mem.seq("f_wbuf", [128, 8 * 1536], BF16)
    tabs = mem.seq("f_tabs", [128, 4 * 1024], F32)
    pexp = mem.seq("f_pexp", [128, 2 * 640], BF16)
    ptb = mem.seq("f_ptb", [128, 2 * 640], BF16)
    expSoff_ = mem.mark() - 4 * 640 * 2
    hbf = mem.seq("f_hbf", [128, 2 * 1024], BF16)
    hbf_off = mem.mark() - 2 * 1024 * 2
    hf32 = mem.seq("f_hf32", [128, 3 * 1024], F32)
    hf32_off = mem.mark() - 3 * 1024 * 4
    qkT = mem.seq("f_qkT", [128, 4 * 512], F32)
    qkT_off = mem.mark() - 4 * 512 * 4
    qkbf = mem.seq("f_qkbf", [128, 512], BF16)
    qkbf_off = mem.mark() - 512 * 2
    zupd = mem.seq("f_zupd", [128, 2 * 1024], F32)
    zupd_off = mem.mark() - 2 * 1024 * 4
    scr = mem.seq("f_scr", [128, 1024], F32)
    nrm = mem.seq("f_nrm", [128, 128], F32)
    oF = mem.at("f_oF", [128, 4 * 2048], BF16, hF_off)
    hlnF = mem.at("fB_hlnF", [128, 8 * 2048], BF16, hF_off)
    ohalf = mem.at("fB_ohalf", [128, 11 * 2048], BF16, kpk_off)
    w2buf = mem.at("fB_w2buf", [128, 11 * 1024], BF16, kpk_off + 45056)
    w1bufB = mem.at("fB_w1buf", [128, 2 * 2048], BF16, kpk_off + 45056 + 22528)
    gelt = mem.at("fB_gelt", [128, 512], F32, kpk_off + 45056 + 22528 + 8192)
    hlnbf = mem.at("fB_hlnbf", [128, 1024], BF16, kpk_off + 45056 + 22528 + 10240)

    lngB = mem.at("fB_lng", [128, D], F32, qkT_off)
    lnbB = mem.at("fB_lnb", [128, D], F32, qkT_off + 4096)
    b1cB = mem.at("fB_b1c", [128, 44], F32, qkbf_off)
    row2bfB = sb["row2bf"]
    assert kpk_off + 45056 + 22528 + 10240 + 2048 <= mem.offs["f_hf32"][0], "B alias overlap"
    hFr = lambda: hF[:].rearrange("p (c k) -> p c k", k=1024)
    hlnFr = lambda: hlnF[:].rearrange("p (c k) -> p c k", k=2048)
    tbr = lambda ti: tabs[:, 1024*ti:1024*(ti+1)].rearrange("p (t k) -> p t k", k=64)

    final_phase = dst is not zs and dst is not None

    for b in range(B_LOC):
        zwr, ln_s = {}, {}
        for t in range(NT):
            x_sb = hf32[:, 1024*(t % 3):1024*(t % 3 + 1)]
            w = [(sC, ln_s[t-3][0]), (vC, ln_s[t-3][1])] if t >= 3 else []
            ld = p.dma("Y", lambda e, b=b, t=t, x_sb=x_sb: e.dma_start(
                out=x_sb, in_=stream[b, 128*t:128*(t+1), :]), waits=w)
            z_sb = zupd[:, 1024*(t % 2):1024*(t % 2 + 1)]
            pw = [ld]
            if t >= 2:
                pw.append(zwr[t-2])
            sN, vN = ln_tile(ctx, x_sb, [z_sb], slot=t % 8, scratch=scr[:], pre_waits=pw)
            ln_s[t] = (sN, vN)
            zwr[t] = p.dma("Y", lambda e, b=b, t=t, z_sb=z_sb: e.dma_start(
                out=zs[b, 128*t:128*(t+1), :], in_=z_sb), waits=[(vC, vN)])

        for layer in range(2):
            final = final_phase and layer == 1
            tabtoks = []
            tabtoks.append(p.dma("Y", lambda e, layer=layer: e.dma_start(
                out=sb["qs8"][:], in_=io["f_qs8"][layer:layer+1, :].broadcast_to([128, 512])),
                waits=[(vC, vC.n)]))
            tabtoks.append(p.dma("Y", lambda e, layer=layer: e.dma_start(
                out=sb["ks8"][:], in_=io["f_ks8"][layer:layer+1, :].broadcast_to([128, 512]))))
            for (ti, rawname) in ((0, "rot_qc"), (1, "rot_qs"), (2, "rot_kc"), (3, "rot_ks")):
                tabtoks.append(p.dma("Y", lambda e, rawname=rawname, ti=ti: e.dma_start(
                    out=tbr(ti), in_=io[rawname][:].rearrange("(t p) k -> p t k", p=128)),
                    waits=[(vC, vC.n)] if ti == 0 else []))
            for ti, sc in ((0, "qs8"), (2, "ks8")):
                p.op("V", lambda e, ti=ti, sc=sc: e.tensor_tensor(
                    tbr(ti), tbr(ti),
                    sb[sc][:, 0:64].unsqueeze(1).broadcast_to([128, 16, 64]), ALU.mult),
                    waits=tabtoks, inc=vC)
            for ti, sc in ((1, "qs8"), (3, "ks8")):
                p.op("V", lambda e, ti=ti, sc=sc: e.tensor_tensor(
                    tbr(ti)[:, :, 0:32], tbr(ti)[:, :, 0:32],
                    sb[sc][:, 32:64].unsqueeze(1).broadcast_to([128, 16, 32]), ALU.mult),
                    waits=tabtoks)
                p.op("V", lambda e, ti=ti, sc=sc: e.tensor_tensor(
                    tbr(ti)[:, :, 32:64], tbr(ti)[:, :, 32:64],
                    sb[sc][:, 0:32].unsqueeze(1).broadcast_to([128, 16, 32]), ALU.mult),
                    inc=vC)
            tab_done = vC.n
            ld_wqkv = p.dma("G", lambda e, layer=layer: e.dma_start(
                out=wbuf[:].rearrange("p (c k) -> p c k", k=1536),
                in_=io["f_wqkv"][layer].rearrange("(c p) k -> p c k", p=128)),
                waits=[(tC, tC.n)])
            for t in range(NT):
                p.op("G", lambda e, t=t: e.memset(
                    vaug[:, 520*t:520*(t+1)].rearrange("p (g k) -> p g k", k=65)[:, :, 64:65],
                    1.0), inc=gC)
            neg_done = gC.n

            tr_done = {}
            for half in range(2):
                for tl in range(8):
                    t = 8 * half + tl
                    h_sb = hbf[:, 1024*(t % 2):1024*(t % 2 + 1)]
                    w = [(tC, tr_done[t-2])] if t >= 2 else [(vC, tab_done)]
                    w += [zwr[t], zwr[max(t-1, 0)]]
                    ldh1 = p.dma("G", lambda e, b=b, t=t, h_sb=h_sb: e.dma_start(
                        out=h_sb[:, 0:512], in_=zs[b, 128*t:128*(t+1), 0:512]), waits=w)
                    if t == 0:
                        p.op("G", lambda e, h_sb=h_sb: e.memset(h_sb[0:1, 512:1024], 0.0),
                             inc=gC)
                        ldh2 = p.dma("G", lambda e, b=b, h_sb=h_sb: e.dma_start(
                            out=h_sb[1:128, 512:1024], in_=zs[b, 0:127, 512:1024]))
                    else:
                        ldh2 = p.dma("G", lambda e, b=b, t=t, h_sb=h_sb: e.dma_start(
                            out=h_sb[:, 512:1024], in_=zs[b, 128*t-1:128*(t+1)-1, 512:1024]))
                    tr_done[t] = transpose_tile(
                        ctx, h_sb, lambda j0, w_, tl=tl: hFr()[:, j0:j0+w_, 128*tl:128*(tl+1)],
                        ncol=8, pre_wait=[ldh1, ldh2, (gC, gC.n)])
                if FFN_DBG_STOP == f"A1h{half}":
                    p.join(["T", "S", "V", "G", "Y"], [tC, sC, vC, gC])
                    return
                for tl in range(8):
                    t = 8 * half + tl
                    for pi in range(3):
                        bank = pi
                        for kc in range(8):
                            w2 = []
                            if kc == 0:
                                w2 = [(vC, vC.n), (sC, sC.n)]
                                if tl == 0 and pi == 0:
                                    w2 += [ld_wqkv, (tC, tr_done[8*half+7])]
                            p.op("T", lambda e, tl=tl, pi=pi, kc=kc, bank=bank: e.matmul(
                                PS(bank), hFr()[:, kc, 128*tl:128*(tl+1)],
                                wbuf[:, 1536*kc + 512*pi: 1536*kc + 512*(pi+1)],
                                start=(kc == 0), stop=(kc == 7)), waits=w2)
                        p.grp("T", tC)
                        mmv = tC.n
                        if pi == 2:
                            p.op("V", lambda e, t=t, bank=bank: e.tensor_copy(
                                vaug[:, 520*t:520*(t+1)].rearrange(
                                    "p (g k) -> p g k", k=65)[:, :, 0:64],
                                PS(bank).rearrange("p (g k) -> p g k", k=64)),
                                waits=[(tC, mmv)], inc=vC)
                        else:
                            p.op("V", lambda e, pi=pi, bank=bank: e.tensor_copy(
                                qkT[:, 512*pi:512*(pi+1)], PS(bank)),
                                waits=[(tC, mmv)], inc=vC)
                    qk_ready = vC.n
                    if FFN_DBG_STOP == f"A2h{half}t{tl}":
                        p.join(["T", "S", "V", "G", "Y"], [tC, sC, vC, gC])
                        return
                    for (pi, TA, TB, dstc) in ((0, 0, 1, qpk), (1, 2, 3, kpk)):
                        src3 = qkT[:, 512*pi:512*(pi+1)].rearrange("p (h k) -> p h k", k=64)
                        u3 = qkT[:, 1024:1536].rearrange("p (h k) -> p h k", k=64)
                        w3 = qkT[:, 1536:2048].rearrange("p (h k) -> p h k", k=64)
                        ss = nrm[:, (t % 4) * 16 + 8 * pi:(t % 4) * 16 + 8 * pi + 8]
                        p.op("S", lambda e, pi=pi: e.activation(
                            scr[:, 0:512], qkT[:, 512*pi:512*(pi+1)], AF.Square),
                            waits=[(vC, qk_ready)], inc=sC)
                        sqv = sC.n
                        p.op("V", lambda e, ss=ss: e.reduce_sum(
                            ss, scr[:, 0:512].rearrange("p (h k) -> p h k", k=64), axis=AX.X),
                            waits=[(sC, sqv)])
                        p.op("V", lambda e, ss=ss: e.tensor_scalar_max(ss, ss, 1e-24), inc=vC)
                        ssv = vC.n
                        p.op("S", lambda e, ss=ss: e.activation(ss, ss, AF.Ln),
                             waits=[(vC, ssv)])
                        p.op("S", lambda e, ss=ss: e.activation(
                            ss, ss, AF.Exp, scale=-0.5), inc=sC)
                        sexp = sC.n
                        if pi == 0:
                            p.op("V", lambda e, ss=ss: e.tensor_scalar_mul(ss, ss, 8.0),
                                 waits=[(sC, sexp)], inc=vC)
                            inv_ctr, inv_done = vC, vC.n
                        else:
                            inv_ctr, inv_done = sC, sexp
                        TAv = tbr(TA)[:, t % 16, :]
                        TBv = tbr(TB)[:, t % 16, :]
                        p.op("V", lambda e, u3=u3, src3=src3, TAv=TAv: e.tensor_tensor(
                            u3, src3, TAv.unsqueeze(1).broadcast_to([128, 8, 64]), ALU.mult))
                        p.op("V", lambda e, w3=w3, src3=src3, TBv=TBv: e.tensor_tensor(
                            w3[:, :, 0:32], src3[:, :, 32:64],
                            TBv[:, 0:32].unsqueeze(1).broadcast_to([128, 8, 32]), ALU.mult))
                        p.op("V", lambda e, w3=w3, src3=src3, TBv=TBv: e.tensor_tensor(
                            w3[:, :, 32:64], src3[:, :, 0:32],
                            TBv[:, 32:64].unsqueeze(1).broadcast_to([128, 8, 32]), ALU.mult))
                        p.op("V", lambda e, u3=u3, w3=w3: e.tensor_sub(
                            u3[:, :, 0:32], u3[:, :, 0:32], w3[:, :, 0:32]))
                        p.op("V", lambda e, u3=u3, w3=w3: e.tensor_add(
                            u3[:, :, 32:64], u3[:, :, 32:64], w3[:, :, 32:64]))
                        p.op("V", lambda e, u3=u3, ss=ss: e.tensor_tensor(
                            u3, u3, ss.unsqueeze(2).broadcast_to([128, 8, 64]), ALU.mult),
                            waits=[(inv_ctr, inv_done)], inc=vC)
                        rot_v = vC.n
                        p.op("V", lambda e: e.tensor_copy(qkbf[:], qkT[:, 1024:1536]),
                             waits=[(vC, rot_v), (tC, tC.n)], inc=vC)
                        cpv = vC.n
                        for j in range(4):
                            w4 = [(vC, cpv), (sC, sC.n)] if j == 0 else []
                            p.op("T", lambda e, j=j: e.matmul(
                                PS(6)[:, 128*j:128*(j+1)], qkbf[:, 128*j:128*(j+1)],
                                sb["ident"][:], start=True, stop=True), waits=w4)
                        p.grp("T", tC)
                        trv = tC.n
                        p.op("S", lambda e, dstc=dstc, t=t: e.activation(
                            dstc[:].rearrange("p (c k) -> p c k", k=2048)[:, :, 128*t:128*(t+1)],
                            PS(6)[:].rearrange("p (c k) -> p c k", k=128), AF.Copy),
                            waits=[(tC, trv)], inc=sC)
            p.join(["T"], [sC, vC, gC])
            if FFN_DBG_STOP == "A3":
                return

            for h in range(8):
                hb, hc2 = 64 * (h % 2), h // 2
                prev_exp = 0
                state = {}
                ptev_hist = {-1: (vC, 0)}
                oev_hist = {-1: 0}

                def emit_S(qc, h=h, hb=hb, hc2=hc2):
                    kcs = list(range(max(0, qc - 4), qc + 1))
                    nk = len(kcs)
                    W_ = 128 * nk
                    sb0 = 2 * (qc % 2)
                    pslot = qc % 2
                    stat = nrm[:, 16*pslot:16*pslot+4]
                    st_ = state[qc] = dict(kcs=kcs, nk=nk, W_=W_, sb0=sb0, pslot=pslot,
                                           stat=stat)
                    w = [(sC, state[qc-2]["exp"])] if qc >= 2 else [(sC, sC.n)]
                    for ji, kc in enumerate(kcs):
                        p.op("T", lambda e, hb=hb, hc2=hc2, qc=qc, kc=kc, ji=ji,
                             sb0=sb0: e.matmul(
                            PS(sb0, 2)[:, 128*ji:128*(ji+1)],
                            qpk[hb:hb+64, 2048*hc2 + 128*qc: 2048*hc2 + 128*(qc+1)],
                            kpk[hb:hb+64, 2048*hc2 + 128*kc: 2048*hc2 + 128*(kc+1)],
                            start=True, stop=True),
                            waits=(w if ji == 0 else []))
                        p.grp("T", tC)
                    st_["smm"] = tC.n

                def emit_softmax(qc):
                    st_ = state[qc]
                    sb0, W_, nk, pslot, stat = (st_[k] for k in
                                                ("sb0", "W_", "nk", "pslot", "stat"))
                    smm = st_["smm"]
                    if qc >= 4:
                        p.op("V", lambda e, sb0=sb0: e.tensor_add(
                            PS(sb0, 2)[:, 0:128], PS(sb0, 2)[:, 0:128],
                            sb["maskadd"][:, 0:128]), waits=[(tC, smm)])
                    p.op("V", lambda e, sb0=sb0, nk=nk: e.tensor_add(
                        PS(sb0, 2)[:, 128*(nk-1):128*nk], PS(sb0, 2)[:, 128*(nk-1):128*nk],
                        sb["maskadd"][:, 128:256]), waits=[(tC, smm)])
                    p.op("V", lambda e, sb0=sb0, W_=W_, stat=stat: e.reduce_max(
                        stat[:, 0:1], PS(sb0, 2)[:, 0:W_], axis=AX.X),
                         waits=[(sC, sC.n)])
                    p.op("V", lambda e, stat=stat: e.tensor_scalar_mul(
                        stat[:, 1:2], stat[:, 0:1], -1.0), inc=vC)
                    nmv = vC.n
                    ptw = state[qc-2]["ptv"] if qc >= 2 else 0
                    p.op("S", lambda e, sb0=sb0, W_=W_, stat=stat, pslot=pslot: e.activation(
                        pexp[:, 640*pslot:640*pslot + W_], PS(sb0, 2)[:, 0:W_], AF.Exp,
                        bias=stat[:, 1:2], accum_out=stat[:, 2:3]),
                        waits=[(vC, nmv), (tC, ptw)], inc=sC)
                    st_["exp"] = sC.n
                    p.op("V", lambda e, stat=stat: e.reciprocal(stat[:, 3:4], stat[:, 2:3]),
                         waits=[(sC, st_["exp"])], inc=vC)
                    rcv = vC.n
                    p.op("S", lambda e, W_=W_, stat=stat, pslot=pslot: e.activation(
                        pexp[:, 640*pslot:640*pslot + W_], pexp[:, 640*pslot:640*pslot + W_],
                        AF.Copy, scale=stat[:, 3:4]), waits=[(vC, rcv)], inc=sC)
                    st_["pnorm"] = (sC, sC.n)

                def emit_tail(qc, h=h):
                    st_ = state[qc]
                    nk, W_, pslot, kcs = (st_[k] for k in ("nk", "W_", "pslot", "kcs"))
                    for ji in range(nk):
                        w5 = ([st_["pnorm"], ptev_hist[qc-1]]
                              if ji == 0 else [])
                        p.op("T", lambda e, ji=ji, pslot=pslot: e.matmul(
                            PS(6, 2)[:, 128*ji:128*(ji+1)],
                            pexp[:, 640*pslot + 128*ji: 640*pslot + 128*(ji+1)],
                            sb["ident"][:], start=True, stop=True), waits=w5)
                        p.grp("T", tC)
                    st_["ptv"] = tC.n
                    if qc % 2 == 0:
                        p.op("V", lambda e, W_=W_, pslot=pslot: e.tensor_copy(
                            ptb[:, 640*pslot:640*pslot + W_], PS(6, 2)[:, 0:W_]),
                            waits=[(tC, st_["ptv"])], inc=vC)
                        ptev_hist[qc] = (vC, vC.n)
                    else:
                        p.op("S", lambda e, W_=W_, pslot=pslot: e.activation(
                            ptb[:, 640*pslot:640*pslot + W_], PS(6, 2)[:, 0:W_], AF.Copy),
                            waits=[(tC, st_["ptv"])], inc=sC)
                        ptev_hist[qc] = (sC, sC.n)
                    for ji, kc in enumerate(kcs):
                        w6 = ([ptev_hist[qc], (vC, oev_hist[qc-1])]
                              if ji == 0 else [])
                        p.op("T", lambda e, kc=kc, h=h, ji=ji, pslot=pslot,
                             first=(ji == 0), last=(ji == nk - 1): e.matmul(
                            PS(5)[0:64, 0:128],
                            vaug[:, 520*kc + 65*h: 520*kc + 65*h + 64],
                            ptb[:, 640*pslot + 128*ji: 640*pslot + 128*(ji+1)],
                            start=first, stop=last), waits=w6)
                        p.grp("T", tC)
                    avv = tC.n
                    p.op("V", lambda e, h=h, qc=qc: e.tensor_copy(
                        oF[64*(h % 2):64*(h % 2)+64,
                           2048*(h // 2) + 128*qc: 2048*(h // 2) + 128*(qc+1)],
                        PS(5)[0:64, 0:128]), waits=[(tC, avv)], inc=vC)
                    oev_hist[qc] = vC.n

                for qc in range(NT):
                    emit_S(qc)
                    emit_softmax(qc)
                    if qc >= 1:
                        emit_tail(qc - 1)
                emit_tail(NT - 1)
            p.join(["T"], [vC, sC])

            ldwo = p.dma("G", lambda e, layer=layer: e.dma_start(
                out=wbuf[:, 0:4*1024].rearrange("p (c k) -> p c k", k=1024),
                in_=io["f_wo"][layer].rearrange("(c p) k -> p c k", p=128)),
                waits=[(tC, tC.n)])
            zw2 = {}
            upd_v = 0
            for t in range(NT):
                z_sb = zupd[:, 1024*(t % 2):1024*(t % 2 + 1)]
                w = ([(vC, upd_v), zw2[t-2]] if t >= 2
                     else [(sC, sC.n)] + p.dma_all()) + [zwr[t]]
                ldz = p.dma("Y", lambda e, b=b, t=t, z_sb=z_sb: e.dma_start(
                    out=z_sb, in_=zs[b, 128*t:128*(t+1), :]), waits=w)
                for half in range(2):
                    bank = half
                    for kc in range(4):
                        w2 = [(vC, vC.n)] if kc == 0 else []
                        if kc == 0 and t == 0 and half == 0:
                            w2.append(ldwo)
                        p.op("T", lambda e, t=t, kc=kc, half=half, bank=bank: e.matmul(
                            PS(bank), oF[:, 2048*kc + 128*t: 2048*kc + 128*(t+1)],
                            wbuf[:, 1024*kc + 512*half: 1024*kc + 512*(half+1)],
                            start=(kc == 0), stop=(kc == 3)), waits=w2)
                    p.grp("T", tC)
                    mmv = tC.n
                    p.op("V", lambda e, z_sb=z_sb, half=half, bank=bank: e.tensor_add(
                        z_sb[:, 512*half:512*(half+1)], z_sb[:, 512*half:512*(half+1)],
                        PS(bank)), waits=[(tC, mmv), ldz], inc=vC)
                upd_v = vC.n
                zw2[t] = p.dma("Y", lambda e, b=b, t=t, z_sb=z_sb: e.dma_start(
                    out=zs[b, 128*t:128*(t+1), :], in_=z_sb), waits=[(vC, upd_v)])
            p.join(["T", "S", "V", "G"], [])

            lnctoks = []
            lnctoks.append(p.dma("Y", lambda e, layer=layer: e.dma_start(
                out=lngB[:], in_=io["f_lng"][layer].broadcast_to([128, D])),
                waits=[(sC, sC.n), (vC, vC.n)]))
            lnctoks.append(p.dma("Y", lambda e, layer=layer: e.dma_start(
                out=lnbB[:], in_=io["f_lnb"][layer].broadcast_to([128, D]))))
            lnctoks.append(p.dma("Y", lambda e, layer=layer: e.dma_start(
                out=b1cB[:], in_=io["f_b1c"][layer])))
            ldb2 = p.dma("G", lambda e, layer=layer: e.dma_start(
                out=row2bfB[:], in_=io["f_b2"][layer]), waits=[(tC, tC.n)])
            trB = {}
            hlv = 0
            for t in range(NT):
                h_sb = hf32[:, 1024*(t % 3):1024*(t % 3 + 1)]
                w = [(tC, trB[t-3])] if t >= 3 else []
                w += [zw2[t], zw2[max(t-1, 0)]]
                ldh1 = p.dma("G", lambda e, b=b, t=t, h_sb=h_sb: e.dma_start(
                    out=h_sb[:, 0:512], in_=zs[b, 128*t:128*(t+1), 0:512]), waits=w)
                if t == 0:
                    p.op("G", lambda e, h_sb=h_sb: e.memset(h_sb[0:1, 512:1024], 0.0), inc=gC)
                    ldh2 = p.dma("G", lambda e, b=b, h_sb=h_sb: e.dma_start(
                        out=h_sb[1:128, 512:1024], in_=zs[b, 0:127, 512:1024]))
                else:
                    ldh2 = p.dma("G", lambda e, b=b, t=t, h_sb=h_sb: e.dma_start(
                        out=h_sb[:, 512:1024], in_=zs[b, 128*t-1:128*(t+1)-1, 512:1024]))
                pw = [ldh1, ldh2, (gC, gC.n), (vC, hlv)] + lnctoks
                sN, vN = ln_tile(ctx, h_sb, [scr[:]], slot=t % 8,
                                 scratch=zupd[:, 0:1024], pre_waits=pw)
                p.op("V", lambda e: e.tensor_mul(scr[:], scr[:], lngB[:]))
                p.op("V", lambda e: e.tensor_add(scr[:], scr[:], lnbB[:]))
                p.op("V", lambda e: e.tensor_copy(hlnbf[:], scr[:]), inc=vC)
                hlv = vC.n
                trB[t] = transpose_tile(
                    ctx, hlnbf, lambda j0, w_, t=t: hlnFr()[:, j0:j0+w_, 128*t:128*(t+1)],
                    ncol=8, pre_wait=[(vC, hlv)])
            p.join(["T"], [vC])

            zw3 = dict(zw2)
            for hhalf in range(2):
                chunks = list(range(11 * hhalf, min(11 * (hhalf + 1), NHC)))
                g_done = 0
                for ci, c in enumerate(chunks):
                    w1b = w1bufB[:, 2048*(ci % 2):2048*(ci % 2 + 1)]
                    w = [(tC, g_done)] if ci >= 2 else [(tC, tC.n)]
                    ldw1 = p.dma("G", lambda e, c=c, layer=layer, w1b=w1b: e.dma_start(
                        out=w1b.rearrange("p (ic k) -> p ic k", k=256),
                        in_=io["f_w1p"][layer][:, 256*c:256*(c+1)].rearrange(
                            "(ic p) k -> p ic k", p=128)), waits=w)
                    for tg in range(4):
                        abank = tg % 2
                        gbank = 2 + tg % 2
                        for kc in range(8):
                            w2 = [(vC, vC.n), ldw1] if kc == 0 else []
                            p.op("T", lambda e, kc=kc, tg=tg, abank=abank, w1b=w1b: e.matmul(
                                PS(abank), w1b[:, 256*kc:256*kc+128],
                                hlnFr()[:, kc, 512*tg:512*(tg+1)],
                                start=(kc == 0), stop=(kc == 7)), waits=w2)
                        for kc in range(8):
                            w2 = [(sC, sC.n)] if kc == 0 else []
                            p.op("T", lambda e, kc=kc, tg=tg, gbank=gbank, w1b=w1b: e.matmul(
                                PS(gbank), w1b[:, 256*kc+128:256*kc+256],
                                hlnFr()[:, kc, 512*tg:512*(tg+1)],
                                start=(kc == 0), stop=(kc == 7)), waits=w2)
                        p.grp("T", tC)
                        mmv = tC.n
                        p.op("S", lambda e, gbank=gbank, c=c: e.activation(
                            gelt[:], PS(gbank), AF.Gelu, bias=b1cB[:, 2*c+1:2*c+2]),
                            waits=[(tC, mmv)], inc=sC)
                        gel = sC.n
                        p.op("V", lambda e, abank=abank, c=c: e.tensor_scalar_add(
                            scr[:, 0:512], PS(abank), b1cB[:, 2*c:2*c+1]),
                            waits=[(tC, mmv)])
                        p.op("V", lambda e, ci=ci, tg=tg: e.tensor_mul(
                            ohalf[:, 2048*ci + 512*tg: 2048*ci + 512*(tg+1)],
                            scr[:, 0:512], gelt[:]), waits=[(sC, gel)], inc=vC)
                    g_done = tC.n
                ldw2 = p.dma("G", lambda e, layer=layer, hhalf=hhalf, nch=len(chunks):
                             e.dma_start(
                    out=w2buf[:, 0:1024*nch].rearrange("p (c k) -> p c k", k=1024),
                    in_=io["f_w2p"][layer, 11*hhalf:11*hhalf+nch].rearrange("c p k -> p c k")),
                    waits=[(tC, tC.n)])
                p.join(["T"], [vC])
                updB = 0
                st3 = {}
                for t in range(NT):
                    z_sb = zupd[:, 1024*(t % 2):1024*(t % 2 + 1)]
                    w = ([(vC, updB), st3[t-2]] if t >= 2
                         else [(sC, sC.n)] + p.dma_all())
                    ldz = p.dma("Y", lambda e, b=b, t=t, z_sb=z_sb: e.dma_start(
                        out=z_sb, in_=zs[b, 128*t:128*(t+1), :]),
                        waits=w + [zw3[t]])
                    ldx = None
                    if final and hhalf == 1:
                        x_sb = hf32[:, 1024*(t % 2):1024*(t % 2 + 1)]
                        ldx = p.dma("Y", lambda e, b=b, t=t, x_sb=x_sb: e.dma_start(
                            out=x_sb, in_=stream[b, 128*t:128*(t+1), :]),
                            waits=([(sC, sC.n)] if t < 3 else []))
                    for half in range(2):
                        bank = 4 + half
                        nch = len(chunks)
                        for ci in range(nch):
                            w2 = [(vC, vC.n), ldw2] if ci == 0 else []
                            p.op("T", lambda e, ci=ci, t=t, half=half, bank=bank,
                                 last=(ci == nch - 1 and hhalf == 0): e.matmul(
                                PS(bank), ohalf[:, 2048*ci + 128*t: 2048*ci + 128*(t+1)],
                                w2buf[:, 1024*ci + 512*half: 1024*ci + 512*(half+1)],
                                start=(ci == 0), stop=last), waits=w2)
                        if hhalf == 1:
                            p.op("T", lambda e, half=half, bank=bank: e.matmul(
                                PS(bank), sb["onesbf"][:, 0:128],
                                row2bfB[:, 512*half:512*(half+1)],
                                start=False, stop=True),
                                waits=([ldb2] if (t == 0 and half == 0) else []))
                        p.grp("T", tC)
                        mmv = tC.n
                        if final and hhalf == 1:
                            p.op("V", lambda e, z_sb=z_sb, half=half, bank=bank: e.tensor_add(
                                z_sb[:, 512*half:512*(half+1)],
                                z_sb[:, 512*half:512*(half+1)], PS(bank)),
                                waits=[(tC, mmv), ldz])
                            p.op("V", lambda e, z_sb=z_sb, half=half, x_sb=x_sb: e.tensor_add(
                                z_sb[:, 512*half:512*(half+1)],
                                z_sb[:, 512*half:512*(half+1)],
                                x_sb[:, 512*half:512*(half+1)]),
                                waits=[ldx], inc=vC)
                        else:
                            p.op("V", lambda e, z_sb=z_sb, half=half, bank=bank: e.tensor_add(
                                z_sb[:, 512*half:512*(half+1)],
                                z_sb[:, 512*half:512*(half+1)], PS(bank)),
                                waits=[(tC, mmv), ldz], inc=vC)
                    updB = vC.n
                    outdst = dst if (final and hhalf == 1) else zs
                    zw3[t] = st3[t] = p.dma("Y", lambda e, b=b, t=t, z_sb=z_sb,
                                            outdst=outdst: e.dma_start(
                        out=outdst[b, 128*t:128*(t+1), :], in_=z_sb),
                        waits=[(vC, updB)])
                p.join(["T", "S", "V", "G"], [])
            zwr = dict(zw3)
        p.join(["T", "S", "V", "G", "Y"], [vC, sC, tC, gC])

def emit_attn_b(ctx, src, dst):
    nc, io, p, C, sb, mem, PS = (ctx[k] for k in ["nc", "io", "p", "C", "sb", "mem", "PS"])
    dY, dG, tC, sC, vC, gC = (C[k] for k in ["dY", "dG", "tC", "sC", "vC", "gC"])
    es = io["zs"]
    SCALE = 0.25

    mem.reset(ctx["PHASE0"])
    eF = mem.seq("b_eF", [128, 8 * 2048], BF16)
    eF_off = mem.mark() - 32 * 1024
    qpad = mem.seq("b_qpad", [128, 4 * 2048], BF16)
    kpad = mem.seq("b_kpad", [128, 4 * 2048], BF16)
    vaug = mem.seq("b_vaug", [128, 16 * 1040], BF16)
    wvp = mem.seq("b_wvp", [128, 8 * 1024], BF16)
    qkraw = mem.seq("b_qkraw", [128, 4 * 2048], BF16)
    et = mem.seq("b_et", [128, 4 * 1024], F32)
    qn = mem.seq("b_qn", [8, 4096], BF16)
    kn = mem.seq("b_kn", [8, 4096], BF16)
    sqb = mem.seq("b_sqb", [128, 2048], BF16)
    scr = mem.seq("b_scr", [128, 1024], F32)
    maxk1 = mem.seq("b_maxk1", [1, 16], F32)
    ebf = mem.seq("b_ebf", [128, 1024], BF16)
    wq1 = mem.seq("b_wq1", [128, 2 * 1024], BF16)
    w1buf = mem.seq("b_w1buf", [128, 8 * 64], BF16)
    expS = mem.at("b_expS", [128, 2 * 2048], BF16, eF_off)
    outF = mem.at("b_outF", [128, 8 * 512], BF16, eF_off + 8 * 1024)
    resF = mem.at("b_resF", [128, 8 * 512], BF16, eF_off + 16 * 1024)
    obf = mem.at("b_obf", [128, 2048], BF16, eF_off + 24 * 1024)
    hW = mem.at("b_hW", [65, 512], BF16, eF_off + 28 * 1024)
    rowstb = mem.at("b_rowst", [1, 2048], BF16, eF_off)
    rowstb2 = mem.at("b_rowst2", [1, 2048], BF16, eF_off + 4096)
    eFr = lambda: eF[:].rearrange("p (c k) -> p c k", k=2048)

    for b in range(B_LOC):
        ewr, ln_s = {}, {}
        for t in range(NT):
            x_sb = et[:, 1024*(t % 4):1024*(t % 4 + 1)]
            w = [(sC, ln_s[t-4][0]), (vC, ln_s[t-4][1])] if t >= 4 else []
            ld = p.dma("Y", lambda e, b=b, t=t, x_sb=x_sb: e.dma_start(
                out=x_sb, in_=src[b, 128*t:128*(t+1), :]), waits=w)
            z_sb = scr[:]
            pw = [ld]
            if t >= 1:
                pw.append(ewr[t-1])
            sN, vN = ln_tile(ctx, x_sb, [z_sb], slot=t % 8,
                             scratch=sqb[:].bitcast(F32), pre_waits=pw)
            ln_s[t] = (sN, vN)
            ewr[t] = p.dma("Y", lambda e, b=b, t=t, z_sb=z_sb: e.dma_start(
                out=es[b, 128*t:128*(t+1), :], in_=z_sb), waits=[(vC, vN)])

        negd = []
        for cch in range(16):
            g_, si = cch // 4, cch % 4
            negd.append(p.dma("Y", lambda e, g_=g_, si=si: e.dma_start(
                out=kpad[32*si+16:32*si+17, 2048*g_:2048*(g_+1)], in_=io["negrow"][:])))

        for it in range(2):
            tr_done = {}
            for t in range(NT):
                w = [(tC, tr_done[t-1])] if t >= 1 else list(negd)
                w += [ewr[t]]
                ldt = p.dma("G", lambda e, b=b, t=t: e.dma_start(
                    out=ebf[:], in_=es[b, 128*t:128*(t+1), :]), waits=w)
                tr_done[t] = transpose_tile(
                    ctx, ebf, lambda j0, w_, t=t: eFr()[:, j0:j0+w_, 128*t:128*(t+1)],
                    ncol=8, pre_wait=[ldt])
            p.join(["T"], [vC])

            for wi, Wn in enumerate(("bWqT", "bWkT")):
                for oc in range(2):
                    idx = 2 * wi + oc
                    wb = wq1[:, 1024*(idx % 2):1024*(idx % 2 + 1)]
                    w = [(tC, qk_hist)] if idx >= 2 else [(tC, tC.n)]
                    ldw = p.dma("G", lambda e, Wn=Wn, oc=oc, wb=wb: e.dma_start(
                        out=wb.rearrange("p (c k) -> p c k", k=128),
                        in_=io[Wn][:, 128*oc:128*(oc+1)].rearrange("(c p) k -> p c k", p=128)),
                        waits=w)
                    for tg in range(4):
                        bank = tg % 2
                        for kc in range(8):
                            w2 = [ldw, (sC, sC.n)] if kc == 0 else []
                            p.op("T", lambda e, wb=wb, tg=tg, kc=kc, bank=bank: e.matmul(
                                PS(bank), wb[:, 128*kc:128*(kc+1)],
                                eFr()[:, kc, 512*tg:512*(tg+1)],
                                start=(kc == 0), stop=(kc == 7)), waits=w2)
                        p.grp("T", tC)
                        mmv = tC.n
                        sc_ = SCALE if wi == 0 else 1.0
                        p.op("S", lambda e, wi=wi, oc=oc, tg=tg, bank=bank, sc_=sc_: e.activation(
                            qkraw[:, 2048*(2*wi+oc) + 512*tg: 2048*(2*wi+oc) + 512*(tg+1)],
                            PS(bank), AF.Copy, scale=sc_), waits=[(tC, mmv)], inc=sC)
                    qk_hist = tC.n
            ldv = p.dma("G", lambda e: e.dma_start(
                out=wvp[:].rearrange("p (c k) -> p c k", k=1024),
                in_=io["bWvT"][:].rearrange("(c p) k -> p c k", p=128)),
                waits=[(tC, tC.n)])
            for t in range(NT):
                for half in range(2):
                    bank = (2 * t + half) % 2
                    for kc in range(8):
                        w2 = [ldv, (vC, vC.n), (sC, sC.n)] if kc == 0 else []
                        p.op("T", lambda e, t=t, half=half, kc=kc, bank=bank: e.matmul(
                            PS(bank), eFr()[:, kc, 128*t:128*(t+1)],
                            wvp[:, 1024*kc + 512*half: 1024*kc + 512*(half+1)],
                            start=(kc == 0), stop=(kc == 7)), waits=w2)
                    p.grp("T", tC)
                    mmv = tC.n
                    p.op("V", lambda e, t=t, half=half, bank=bank: e.tensor_copy(
                        vaug[:, 1040*t:1040*(t+1)].rearrange(
                            "p (g k) -> p g k", k=65)[:, 8*half:8*half+8, 0:64],
                        PS(bank).rearrange("p (g k) -> p g k", k=64)),
                        waits=[(tC, mmv)], inc=vC)
            for t in range(NT):
                p.op("G", lambda e, t=t: e.memset(
                    vaug[:, 1040*t:1040*(t+1)].rearrange("p (g k) -> p g k", k=65)[:, :, 64:65],
                    1.0), inc=gC)
            vones = gC.n
            p.join(["T"], [vC, sC])

            for wi, (rawbase, dstn) in enumerate(((0, qn), (2, kn))):
                for ch in range(2):
                    p.op("S", lambda e, rawbase=rawbase, ch=ch: e.activation(
                        sqb[:], qkraw[:, 2048*(rawbase+ch):2048*(rawbase+ch+1)], AF.Square),
                        waits=[(vC, vC.n)], inc=sC)
                    sqv = sC.n
                    for ng in range(4):
                        w4 = [(sC, sqv), (sC, sC.n)] if ng == 0 else []
                        p.op("T", lambda e, ng=ng: e.matmul(
                            PS(2, 4)[0:8, 512*ng:512*(ng+1)], sb["ind8"][:],
                            sqb[:, 512*ng:512*(ng+1)], start=True, stop=True), waits=w4)
                    p.grp("T", tC)
                    mmn = tC.n
                    p.op("S", lambda e, dstn=dstn, ch=ch: e.activation(
                        dstn[0:8, 2048*ch:2048*(ch+1)], PS(2, 4)[0:8, 0:2048], AF.Copy),
                        waits=[(tC, mmn)], inc=sC)
            nrm_s = sC.n
            p.op("V", lambda e: e.reduce_max(
                scr[0:8, 0:2], kn[0:8, :].rearrange("p (c k) -> p c k", k=2048), axis=AX.X),
                waits=[(sC, nrm_s)], inc=vC)
            mkv = vC.n
            mkd = p.dma("Y", lambda e: e.dma_start(out=maxk1[0:1, 0:16], in_=scr[0:8, 0:2],
                                                   allow_slow_non_contiguous=True),
                        waits=[(vC, mkv)])
            for cch in range(16):
                g_, si = cch // 4, cch % 4
                src_ch, src_r = cch // 8, 16 * (cch % 8)
                for (rawbase, dpad) in ((0, qpad), (2, kpad)):
                    p.dma("Y", lambda e, rawbase=rawbase, src_ch=src_ch, src_r=src_r,
                          g_=g_, si=si, dpad=dpad: e.dma_start(
                        out=dpad[32*si:32*si+16, 2048*g_:2048*(g_+1)],
                        in_=qkraw[src_r:src_r+16,
                                  2048*(rawbase+src_ch):2048*(rawbase+src_ch+1)]),
                        waits=[(sC, nrm_s)] if cch == 0 else [])
                rowd = p.dma("Y", lambda e, cch=cch, src_ch=src_ch: e.dma_start(
                    out=rowstb[:], in_=qn[cch % 8:cch % 8 + 1,
                                          2048*src_ch:2048*(src_ch+1)]),
                    waits=[(sC, bsts)] if cch > 0 else [(sC, nrm_s)])
                wS = [rowd, mkd]
                if cch > 0:
                    wS.append(brow_tok)
                p.op("S", lambda e, cch=cch: e.activation(
                    rowstb2[:], rowstb[:], AF.Sqrt,
                    scale=maxk1[0:1, cch:cch+1]), waits=wS, inc=sC)
                bsts = sC.n
                brow_tok = p.dma("Y", lambda e, g_=g_, si=si: e.dma_start(
                    out=qpad[32*si+16:32*si+17, 2048*g_:2048*(g_+1)], in_=rowstb2[:]),
                    waits=[(sC, bsts)])
            p.join(["T"], [vC, gC])

            ldwp = p.dma("G", lambda e: e.dma_start(
                out=wvp[:].rearrange("p (c k) -> p c k", k=1024),
                in_=io["bWpT"][:].rearrange("(c p) k -> p c k", p=128)),
                waits=[(tC, tC.n)])
            ldw1 = p.dma("G", lambda e: e.dma_start(
                out=w1buf[:].rearrange("p (c k) -> p c k", k=64),
                in_=io["bW1T"][:].rearrange("(c p) k -> p c k", p=128)))
            p.op("G", lambda e: e.memset(hW[64:65, :], 1.0), inc=gC)
            hwones = gC.n

            PSF = PS(0, 8)

            def AVAP(i):
                off = 512 * (4 + i // 6) + 65 * (i % 6)
                return PSF[:, off:off + 65]
            av_war = vC.n
            prev_av_last = tC.n
            ptv_hist = {}
            bank_tok = {3: (vC, 0), 7: (vC, 0)}
            for ng in range(4):
                for cg in range(4):
                    gcg = 4 * ng + cg
                    obslot = 1024 * (gcg % 2)
                    avd = {}
                    exp_done = {}
                    for mg in range(16):
                        w = ([(sC, exp_done[mg - 1])] if mg >= 1
                             else [(sC, sC.n), (vC, vC.n)])
                        for j in range(4):
                            for s in range(4):
                                cc = 32 * ((s + j) % 4)
                                p.op("T", lambda e, cg=cg, s=s, cc=cc, mg=mg, ng=ng:
                                     e.matmul(
                                    PSF[cc:cc+32, 512*s:512*s+512],
                                    kpad[32*s:32*s+17,
                                         2048*cg + 128*mg + cc: 2048*cg + 128*mg + cc + 32],
                                    qpad[32*s:32*s+17,
                                         2048*cg + 512*ng: 2048*cg + 512*(ng+1)],
                                    start=True, stop=True, tile_position=(32*s, cc)),
                                    waits=(w if (j == 0 and s == 0) else []))
                        p.grp("T", tC)
                        smm = tC.n
                        eslot = mg % 2
                        w = [(tC, smm)]
                        if mg >= 2:
                            w.append((tC, avd[mg - 2]))
                        elif gcg > 0 or it > 0 or b > 0:
                            w.append((tC, prev_av_last))
                        p.op("S", lambda e, eslot=eslot: e.activation(
                            expS[:, 2048*eslot:2048*(eslot+1)], PS(0, 4), AF.Exp),
                            waits=w, inc=sC)
                        exp_done[mg] = sC.n
                        for s in range(4):
                            for qq in range(4):
                                i = 4 * s + qq
                                w = []
                                if i == 0:
                                    w = [(sC, exp_done[mg])]
                                    if mg == 0:
                                        w += [(gC, vones), (vC, av_war)]
                                p.op("T", lambda e, i=i, s=s, qq=qq, mg=mg, cg=cg,
                                     eslot=eslot,
                                     first=(mg == 0 and i in (0, 6, 12)),
                                     last=(mg == 15 and i in (5, 11, 15)):
                                     e.matmul(
                                    AVAP(i),
                                    expS[:, 2048*eslot + 512*s + 128*qq:
                                         2048*eslot + 512*s + 128*(qq+1)],
                                    vaug[:, 1040*mg + 65*(4*cg+s):
                                         1040*mg + 65*(4*cg+s) + 65],
                                    start=first, stop=last), waits=w)
                        p.grp("T", tC)
                        avd[mg] = tC.n
                    av_last = prev_av_last = avd[15]
                    for jb in range(3):
                        nsl = (6, 6, 4)[jb]
                        p.op("V", lambda e, jb=jb, nsl=nsl: e.reciprocal(
                            scr[:, 8*jb:8*jb+nsl].rearrange("p (g k) -> p g k", k=1),
                            PSF[:, 512*(4+jb):512*(4+jb) + 65*nsl].rearrange(
                                "p (g k) -> p g k", k=65)[:, :, 64:65]),
                            waits=([(tC, av_last)] if jb == 0 else []))
                    rcv = vC.n
                    for i in range(16):
                        w = []
                        if i == 0:
                            w = [(vC, rcv), (tC, ptv_hist.get(gcg - 2, 0))]
                        p.op("V", lambda e, i=i, obslot=obslot: e.tensor_scalar_mul(
                            obf[:, obslot + 64*i: obslot + 64*(i+1)],
                            AVAP(i)[:, 0:64], scr[:, 8*(i//6) + (i % 6):
                                                  8*(i//6) + (i % 6) + 1]), waits=w)
                    nrmv = vC.n
                    av_war = nrmv
                    for s in range(4):
                        tb = 3 if s % 2 == 0 else 7
                        for qq in range(4):
                            w = []
                            if qq == 0:
                                w = [(vC, nrmv), bank_tok[tb]]
                            p.op("T", lambda e, s=s, qq=qq, tb=tb, obslot=obslot:
                                 e.matmul(
                                PS(tb)[0:64, 128*qq:128*(qq+1)],
                                obf[:, obslot + 256*s + 64*qq:
                                    obslot + 256*s + 64*(qq+1)],
                                sb["ident"][:], start=True, stop=True), waits=w)
                        p.grp("T", tC)
                        ptv_s = tC.n
                        cch = 4 * cg + s
                        if s % 2 == 0:
                            p.op("V", lambda e, cch=cch, tb=tb: e.tensor_copy(
                                outF[64*(cch % 2):64*(cch % 2)+64,
                                     512*(cch // 2):512*(cch // 2 + 1)],
                                PS(tb)[0:64, 0:512]), waits=[(tC, ptv_s)], inc=vC)
                            bank_tok[tb] = (vC, vC.n)
                        else:
                            p.op("S", lambda e, cch=cch, tb=tb: e.activation(
                                outF[64*(cch % 2):64*(cch % 2)+64,
                                     512*(cch // 2):512*(cch // 2 + 1)],
                                PS(tb)[0:64, 0:512], AF.Copy), waits=[(tC, ptv_s)], inc=sC)
                            bank_tok[tb] = (sC, sC.n)
                    ptv_hist[gcg] = tC.n
                out_done = vC.n
                wp_v = 0
                for tc in range(4):
                    t = 4 * ng + tc
                    e_sb = et[:, 1024*tc:1024*(tc+1)]
                    prev_t = (4 * (ng - 1)) % 16 + tc
                    lde = p.dma("Y", lambda e, b=b, t=t, e_sb=e_sb: e.dma_start(
                        out=e_sb, in_=es[b, 128*t:128*(t+1), :]),
                        waits=[(vC, out_done), ewr[t], ewr[prev_t]])
                    for half in range(2):
                        bank = 6 + half
                        for kc in range(8):
                            w2 = [(vC, vC.n)] if kc == 0 else []
                            if tc == 0 and half == 0 and kc == 0:
                                w2 += [ldwp, (vC, out_done), (sC, sC.n)]
                            p.op("T", lambda e, kc=kc, tc=tc, half=half, bank=bank: e.matmul(
                                PS(bank), outF[:, 512*kc + 128*tc: 512*kc + 128*(tc+1)],
                                wvp[:, 1024*kc + 512*half: 1024*kc + 512*(half+1)],
                                start=(kc == 0), stop=(kc == 7)), waits=w2)
                        p.grp("T", tC)
                        mmv = tC.n
                        p.op("V", lambda e, e_sb=e_sb, half=half, bank=bank: e.tensor_add(
                            e_sb[:, 512*half:512*(half+1)], e_sb[:, 512*half:512*(half+1)],
                            PS(bank)), waits=[(tC, mmv), lde], inc=vC)
                    wp_v = vC.n
                rtr = 0
                for tc in range(4):
                    e_sb = et[:, 1024*tc:1024*(tc+1)]
                    p.op("V", lambda e, e_sb=e_sb: e.tensor_copy(ebf[:], e_sb),
                         waits=[(vC, wp_v)] if tc == 0 else [(tC, rtr)], inc=vC)
                    cpv = vC.n
                    rtr = transpose_tile(
                        ctx, ebf, lambda j0, w_, tc=tc: resF[:].rearrange(
                            "p (c k) -> p c k", k=512)[:, j0:j0+w_, 128*tc:128*(tc+1)],
                        ncol=8, pre_wait=[(vC, cpv)], bank=0)
                p.join(["T"], [vC])
                for kc in range(8):
                    w2 = [ldw1, (sC, sC.n)] if kc == 0 else []
                    p.op("T", lambda e, kc=kc: e.matmul(
                        PS(1)[0:64, 0:512], w1buf[:, 64*kc:64*(kc+1)],
                        resF[:].rearrange("p (c k) -> p c k", k=512)[:, kc, :],
                        start=(kc == 0), stop=(kc == 7)), waits=w2)
                p.grp("T", tC)
                w1v = tC.n
                p.op("S", lambda e: e.activation(
                    hW[0:64, :], PS(1)[0:64, 0:512], AF.Relu, bias=sb["b_b1"][:]),
                    waits=[(tC, w1v)], inc=sC)
                hWs = sC.n
                lastit = it == 1
                for tc in range(4):
                    t = 4 * ng + tc
                    e_sb = et[:, 1024*tc:1024*(tc+1)]
                    ldx = None
                    if lastit:
                        ldx = p.dma("Y", lambda e, b=b, t=t: e.dma_start(
                            out=scr[:, 0:1024], in_=src[b, 128*t:128*(t+1), :]),
                            waits=[(vC, vC.n)])
                    for half in range(2):
                        bank = 2 + half
                        w2 = [(sC, hWs), (gC, hwones), (vC, vC.n)]
                        p.op("T", lambda e, tc=tc, half=half, bank=bank: e.matmul(
                            PS(bank), hW[0:65, 128*tc:128*(tc+1)],
                            sb["bW2aug"][0:65, 512*half:512*(half+1)],
                            start=True, stop=True), waits=w2, inc=tC)
                        mmv = tC.n
                        if lastit:
                            p.op("V", lambda e, e_sb=e_sb, half=half, bank=bank: e.tensor_add(
                                e_sb[:, 512*half:512*(half+1)],
                                e_sb[:, 512*half:512*(half+1)], PS(bank)),
                                waits=[(tC, mmv)])
                            p.op("V", lambda e, e_sb=e_sb, half=half: e.tensor_add(
                                e_sb[:, 512*half:512*(half+1)],
                                e_sb[:, 512*half:512*(half+1)], scr[:, 512*half:512*(half+1)]),
                                waits=[ldx], inc=vC)
                        else:
                            p.op("V", lambda e, e_sb=e_sb, half=half, bank=bank: e.tensor_add(
                                e_sb[:, 512*half:512*(half+1)],
                                e_sb[:, 512*half:512*(half+1)], PS(bank)),
                                waits=[(tC, mmv)], inc=vC)
                    uv = vC.n
                    outdst = dst if lastit else es
                    ewr[t] = p.dma("Y", lambda e, b=b, t=t, e_sb=e_sb, outdst=outdst:
                                   e.dma_start(
                        out=outdst[b, 128*t:128*(t+1), :], in_=e_sb),
                        waits=[(vC, uv)])
            p.join(["T", "S", "V", "G"], [])
        p.join(["T", "S", "V", "G", "Y"], [vC, sC, tC, gC])

def build_program():
    nc = bass.Bass("TRN2", target_bir_lowering=False)
    io = {}
    def din(name, shape, dtype=F32):
        io[name] = nc.dram_tensor(name, list(shape), dtype, kind="ExternalInput")
    din("x", [B_LOC, S, D])
    for n, s_ in (("aWqT", [D, D]), ("aWkT", [D, D]), ("aWvT", [D, D]), ("aWoT", [D, D]),
                  ("a_bqk", [128, 16]), ("a_bv", [1, D]), ("a_bo", [1, D]),
                  ("bWqT", [D, 256]), ("bWkT", [D, 256]), ("bWvT", [D, D]), ("bWpT", [D, D]),
                  ("bW1T", [D, 64]), ("b_b1", [64, 1]), ("bW2aug", [65, D]),
                  ("f_wqkv", [2, D, 1536]), ("f_wo", [2, 512, D]),
                  ("f_w1p", [2, D, 22 * 256]), ("f_b1c", [2, 128, 44]),
                  ("f_w2p", [2, 22, 128, D]), ("f_b2", [2, 1, D]),
                  ("f_lng", [2, 1, D]), ("f_lnb", [2, 1, D]),
                  ("f_qs8", [2, 512]), ("f_ks8", [2, 512]), ("onesf32", [1, 128]),
                  ("maskadd", [128, 256]),
                  ("rot_qc", [S, 64]), ("rot_qs", [S, 64]), ("rot_kc", [S, 64]),
                  ("rot_ks", [S, 64])):
        din(n, s_)
    for n, s_ in (("ident", [128, 128]), ("maskP12", [128, 256]), ("onesbf", [1, 128]),
                  ("ind8", [128, 8]), ("negrow", [1, S])):
        din(n, s_, BF16)
    io["dbg"] = nc.dram_tensor("dbg", [128, 16384], BF16, kind="ExternalOutput")
    io["dbg2"] = nc.dram_tensor("dbg2", [128, 16384], BF16, kind="ExternalOutput")
    io["dbg3"] = nc.dram_tensor("dbg3", [128, 16384], BF16, kind="ExternalOutput")
    io["xs"] = nc.dram_tensor("xs", [B_LOC, S, D], F32, kind="Internal")
    io["zs"] = nc.dram_tensor("zs", [B_LOC, S, D], F32, kind="Internal")
    io["y"] = nc.dram_tensor("y", [B_LOC, S, D], F32, kind="ExternalOutput")

    p = Plan()
    C = {n: p.ctr(n) for n in ["tC", "sC", "vC", "gC", "dY", "dG"]}
    p.auto = {"V": C["vC"], "S": C["sC"], "G": C["gC"]}
    p.mkqueue("Y", 10)
    p.mkqueue("G", 8)
    mem = Mem(nc)
    sb = {}
    sb["ident"] = mem.seq("c_ident", [128, 128], BF16)
    sb["maskP12"] = mem.seq("c_maskP12", [128, 256], BF16)
    sb["maskadd"] = mem.seq("c_maskadd", [128, 256], F32)
    sb["row2bf"] = mem.seq("c_row2bf", [1, D], BF16)
    sb["onesbf"] = mem.seq("c_onesbf", [1, 128], BF16)
    sb["onesf32"] = mem.seq("c_onesf32", [1, 128], F32)
    sb["ind8"] = mem.seq("c_ind8", [128, 8], BF16)
    sb["a_bqk"] = mem.seq("c_a_bqk", [128, 16], F32)
    sb["stats"] = mem.seq("c_stats", [128, 64], F32)
    sb["b_b1"] = mem.seq("c_b_b1", [64, 1], F32)
    sb["bW2aug"] = mem.seq("c_bW2aug", [65, D], BF16)
    sb["qs8"] = mem.seq("c_qs8", [128, 512], F32)
    sb["ks8"] = mem.seq("c_ks8", [128, 512], F32)
    PHASE0 = mem.mark()

    psum = nc.alloc_psum_tensor("ps", [128, 4096], F32)
    def PS(b_, w=1):
        return psum[:, 512*b_:512*(b_+w)]

    for n in ["ident", "maskP12", "maskadd", "onesbf", "onesf32", "ind8", "a_bqk",
              "b_b1"]:
        p.dma("Y", lambda e, n=n: e.dma_start(out=sb[n][:], in_=io[n][:]))
    p.dma("G", lambda e: e.dma_start(out=sb["bW2aug"][:], in_=io["bW2aug"][:]))
    p.wait("T", p.dma_all())
    p.wait("S", p.dma_all())
    p.wait("V", p.dma_all())
    ctx = dict(nc=nc, io=io, p=p, C=C, sb=sb, mem=mem, PS=PS, PHASE0=PHASE0)

    emit_attn_a(ctx, src=io["x"], dst=(io["xs"] if PHASES > 1 else io["y"]))
    if PHASES > 1:
        emit_ffn(ctx, stream=io["xs"], dst=(io["xs"] if PHASES > 2 else io["y"]))
    if PHASES > 2:
        emit_attn_b(ctx, src=io["xs"], dst=(io["xs"] if PHASES > 3 else io["y"]))
    if PHASES > 3:
        emit_ffn(ctx, stream=io["xs"], dst=io["y"])

    p.wait("Y", p.dma_all())

    import contextlib
    stack = contextlib.ExitStack()
    for c in p.ctrs:
        c.h = stack.enter_context(nc.semaphore(c.name))
    with nc.Block() as block:
        @block.tensor
        def _(e):
            for f in p.items["T"]:
                f(e)
        @block.scalar
        def _(e):
            for f in p.items["S"]:
                f(e)
        @block.vector
        def _(e):
            for f in p.items["V"]:
                f(e)
        @block.gpsimd
        def _(e):
            for f in p.items["G"]:
                f(e)
        @block.sync
        def _(e):
            for f in p.items["Y"]:
                f(e)
    stack.close()
    nc.dbg_mem = mem
    return nc

def kernel(**inputs):
    d = prepare_host(inputs)
    x = np.asarray(inputs["L0_attn_a_x"], dtype=np.float32)
    nc = build_program()
    in_maps = []
    for c in range(N_CORES):
        m = dict(d)
        m["x"] = np.ascontiguousarray(x[2*c:2*c+2])
        in_maps.append(m)
    res = run_bass_kernel_spmd(nc, in_maps, core_ids=list(range(N_CORES)))
    return np.concatenate([r["y"] for r in res.results], axis=0)


# version 08
# speedup vs baseline: 1.1575x; 1.1575x over previous
import numpy as np
import ml_dtypes
import concourse.bass as bass
import concourse.mybir as mybir
from concourse.bass_utils import run_bass_kernel_spmd

F32 = mybir.dt.float32
BF16 = mybir.dt.bfloat16
AF = mybir.ActivationFunctionType
ALU = mybir.AluOpType
AX = mybir.AxisListType
bfloat16 = ml_dtypes.bfloat16

N_CORES = 8
B_LOC = 2
S = 2048
D = 1024
NT = S // 128
PHASES = 4
DBG_TAP = None
FFN_DBG_STOP = None

class Ctr:
    def __init__(self, name):
        self.name, self.n, self.h = name, 0, None

class Plan:
    def __init__(self):
        self.items = {k: [] for k in "TSVGY"}
        self.ctrs = []
        self.queues = {}
    def ctr(self, name):
        c = Ctr(name); self.ctrs.append(c); return c
    def mkqueue(self, eng, n=8):
        self.queues[eng] = [self.ctr(f"d{eng}{i}") for i in range(n)], [0]
    def dma(self, eng, fn, waits=()):
        cs, kbox = self.queues[eng]
        c = cs[kbox[0] % len(cs)]
        kbox[0] += 1
        w = list(waits)
        if c.n > 0:
            w.append((c, c.n))
        self.op(eng, fn, waits=w, inc=c, by=16, _dma=True)
        return (c, c.n)
    def dma_all(self):
        toks = []
        for cs, _ in self.queues.values():
            toks += [(c, c.n) for c in cs if c.n > 0]
        return toks
    def op(self, eng, fn, waits=(), inc=None, by=1, _dma=False):
        if eng in ("V", "S", "G") and not _dma:
            c = self.auto[eng]
            waits = list(waits) + [(c, c.n)]
            if inc is None:
                inc, by = c, 1
            else:
                assert inc is c, f"{eng} op with foreign inc"
        for (c, v) in waits:
            assert v <= c.n, f"wait on future: {c.name} {v} > {c.n}"
        waits = tuple(waits)
        if inc is not None:
            inc.n += by
        def run(e, fn=fn, waits=waits, inc=inc, by=by):
            for (c, v) in waits:
                if v > 0:
                    e.wait_ge(c.h, v)
            ins = fn(e)
            if inc is not None:
                ins.then_inc(inc.h, by)
            return ins
        self.items[eng].append(run)
        return inc.n if inc is not None else None
    def grp(self, eng, ctr, by=1):
        ctr.n += by
        prev_run = self.items[eng].pop()
        def run2(e, prev_run=prev_run, ctr=ctr, by=by):
            ins = prev_run(e)
            ins.then_inc(ctr.h, by)
            return ins
        self.items[eng].append(run2)
        return ctr.n
    def wait(self, eng, waits):
        for (c, v) in waits:
            assert v <= c.n, f"wait on future: {c.name} {v} > {c.n}"
        waits = tuple(waits)
        def run(e, waits=waits):
            for (c, v) in waits:
                if v > 0:
                    e.wait_ge(c.h, v)
        self.items[eng].append(run)
    def join(self, engs, ctrs, dmas=True):
        for eng in engs:
            w = [(c, c.n) for c in ctrs]
            if dmas:
                w += self.dma_all()
            self.wait(eng, w)

class Mem:
    def __init__(self, nc):
        self.size = (nc.sbuf_bytes_remaining - 4096) & ~63
        self.nc = nc
        self.handle = nc.alloc_sbuf_tensor("pool", [128, self.size], mybir.dt.uint8)
        self.pap = self.handle.ap() if hasattr(self.handle, "ap") else self.handle[:]
        self.cur = 0
        self.reg = {}
        self.offs = {}
    def at(self, name, shape, dtype, off):
        nbytes = int(np.prod(shape[1:])) * mybir.dt.size(dtype)
        assert off + nbytes <= self.size, f"{name}: {off}+{nbytes} > {self.size}"
        ap = self.pap[:, off:off + nbytes].bitcast(dtype)
        if shape[0] != 128:
            ap = ap[0:shape[0], :]
        self.reg[name] = ap
        self.offs[name] = (off, tuple(shape), dtype)
        return ap
    def seq(self, name, shape, dtype):
        off = (self.cur + 63) & ~63
        t = self.at(name, shape, dtype, off)
        self.cur = off + int(np.prod(shape[1:])) * mybir.dt.size(dtype)
        return t
    def mark(self):
        return self.cur
    def reset(self, mark):
        self.cur = mark

def _rotary_tables():
    DIM_HEAD, WINDOW = 64, 512
    half_idx = np.arange(0, DIM_HEAD, 2, dtype=np.float64)
    inv_freq = 1.0 / 10000.0 ** (half_idx / DIM_HEAD)
    t = np.arange(S, dtype=np.float64)
    freqs = t[:, None] * inv_freq[None, :]
    freqs = np.concatenate([freqs, freqs], axis=-1)
    base = (half_idx + 0.4 * DIM_HEAD) / (1.4 * DIM_HEAD)
    power = (t - S // 2) / float(WINDOW // 2)
    scale = base[None, :] ** power[:, None]
    scale = np.concatenate([scale, scale], axis=-1)
    cos, sin = np.cos(freqs), np.sin(freqs)
    return ((cos * scale).astype(np.float32), (sin * scale).astype(np.float32),
            (cos / scale).astype(np.float32), (sin / scale).astype(np.float32))

def prepare_host(inp):
    f32 = np.float32
    g = lambda k: np.asarray(inp[k])
    d = {}
    d["aWqT"] = np.ascontiguousarray(g("L0_attn_a_Wq").T.astype(f32))
    d["aWkT"] = np.ascontiguousarray(g("L0_attn_a_Wk").T.astype(f32))
    d["aWvT"] = np.ascontiguousarray(g("L0_attn_a_Wv").T.astype(f32))
    d["aWoT"] = np.ascontiguousarray(g("L0_attn_a_Wo").T.astype(f32))
    a_bqk = np.zeros((128, 16), f32)
    a_bqk[:, 0::2] = g("L0_attn_a_bq").astype(f32).reshape(8, 128).T
    a_bqk[:, 1::2] = g("L0_attn_a_bk").astype(f32).reshape(8, 128).T
    d["a_bqk"] = a_bqk
    d["a_bv"] = g("L0_attn_a_bv").astype(f32).reshape(1, D)
    d["a_bo"] = g("L0_attn_a_bo").astype(f32).reshape(1, D)
    d["bWqT"] = np.ascontiguousarray(g("L0_attn_b_Wq").T.astype(f32))
    d["bWkT"] = np.ascontiguousarray(g("L0_attn_b_Wk").T.astype(f32))
    d["bWvT"] = np.ascontiguousarray(g("L0_attn_b_Wv").T.astype(f32))
    d["bWpT"] = np.ascontiguousarray(g("L0_attn_b_Wp").T.astype(f32))
    d["bW1T"] = np.ascontiguousarray(g("L0_attn_b_W1").T.astype(f32))
    d["b_b1"] = g("L0_attn_b_b1").astype(f32).reshape(64, 1)
    d["bW2aug"] = np.ascontiguousarray(np.concatenate(
        [g("L0_attn_b_W2").T.astype(f32), g("L0_attn_b_b2").astype(f32).reshape(1, D)], 0))
    HID, NHC = 2730, 22
    d["f_wqkv"] = np.ascontiguousarray(g("L0_ffn_wqkv").astype(f32))
    d["f_wo"] = np.ascontiguousarray(g("L0_ffn_wo").astype(f32))
    w1 = g("L0_ffn_w1").astype(f32)
    w1p = np.zeros((2, D, NHC, 256), f32)
    for c in range(NHC):
        w = min(128, HID - c * 128)
        w1p[:, :, c, 0:w] = w1[:, :, c*128:c*128+w]
        w1p[:, :, c, 128:128+w] = w1[:, :, HID + c*128:HID + c*128+w]
    d["f_w1p"] = np.ascontiguousarray(w1p.reshape(2, D, NHC * 256))
    b1 = g("L0_ffn_b1").astype(f32)
    b1c = np.zeros((2, 128, 2 * NHC), f32)
    for c in range(NHC):
        w = min(128, HID - c * 128)
        b1c[:, 0:w, 2*c] = b1[:, c*128:c*128+w]
        b1c[:, 0:w, 2*c+1] = b1[:, HID + c*128:HID + c*128+w]
    d["f_b1c"] = np.ascontiguousarray(b1c)
    w2 = g("L0_ffn_w2").astype(f32)
    w2p = np.zeros((2, NHC, 128, D), f32)
    for c in range(NHC):
        w = min(128, HID - c * 128)
        w2p[:, c, 0:w, :] = w2[:, c*128:c*128+w, :]
    d["f_w2p"] = np.ascontiguousarray(w2p)
    d["f_b2"] = g("L0_ffn_b2").astype(f32).reshape(2, 1, D)
    d["f_lng"] = g("L0_ffn_ln_g").astype(f32).reshape(2, 1, D)
    d["f_lnb"] = g("L0_ffn_ln_b").astype(f32).reshape(2, 1, D)
    d["f_qs8"] = np.ascontiguousarray(np.tile(g("L0_ffn_q_scale").astype(f32), (1, 8)))
    d["f_ks8"] = np.ascontiguousarray(np.tile(g("L0_ffn_k_scale").astype(f32), (1, 8)))
    d["ident"] = np.eye(128, dtype=bfloat16)
    r = np.arange(128)
    d["maskP12"] = np.concatenate(
        [(r[:, None] <= r[None, :]).astype(bfloat16),
         (r[:, None] >= r[None, :]).astype(bfloat16)], axis=1)
    d["onesbf"] = np.ones((1, 128), dtype=bfloat16)
    r2 = np.arange(128)
    maskadd = np.zeros((128, 256), np.float32)
    maskadd[:, 0:128][r2[:, None] > r2[None, :]] = -1e30
    maskadd[:, 128:256][r2[:, None] < r2[None, :]] = -1e30
    d["maskadd"] = maskadda = maskadd
    d["onesf32"] = np.ones((1, 128), dtype=np.float32)
    d["negrow"] = np.full((1, S), -1.0, dtype=bfloat16)
    ind8 = np.zeros((128, 8), dtype=bfloat16)
    for rr in range(128):
        ind8[rr, rr // 16] = 1
    d["ind8"] = ind8
    d["rot_qc"], d["rot_qs"], d["rot_kc"], d["rot_ks"] = _rotary_tables()
    return d

def ln_tile(ctx, x_sb, outs, slot, scratch, pre_waits=()):
    p, C, sb = ctx["p"], ctx["C"], ctx["sb"]
    st = sb["stats"]
    o = slot * 8
    ssum = st[:, o:o+1]; ssq = st[:, o+1:o+2]; mu = st[:, o+2:o+3]
    m2 = st[:, o+3:o+4]; var = st[:, o+4:o+5]; istd = st[:, o+5:o+6]
    nb = st[:, o+6:o+7]; lnv = st[:, o+7:o+8]
    vC, sC = C["vC"], C["sC"]
    p.op("V", lambda e: e.reduce_sum(ssum, x_sb, axis=AX.X), waits=list(pre_waits), inc=vC)
    p.op("S", lambda e: e.activation(scratch, x_sb, AF.Square, accum_out=ssq),
         waits=list(pre_waits), inc=sC)
    svv = sC.n
    p.op("V", lambda e: e.tensor_scalar_mul(mu, ssum, 1.0 / D))
    p.op("V", lambda e: e.tensor_scalar_mul(m2, ssq, 1.0 / D), waits=[(sC, svv)])
    p.op("V", lambda e: e.tensor_mul(var, mu, mu))
    p.op("V", lambda e: e.tensor_sub(var, m2, var))
    p.op("V", lambda e: e.tensor_scalar_add(var, var, 1e-5), inc=vC)
    vstat = vC.n
    p.op("S", lambda e: e.activation(lnv, var, AF.Ln), waits=[(vC, vstat)])
    p.op("S", lambda e: e.activation(istd, lnv, AF.Exp, scale=-0.5), inc=sC)
    sistd = sC.n
    p.op("V", lambda e: e.tensor_mul(nb, mu, istd), waits=[(sC, sistd)])
    p.op("V", lambda e: e.tensor_scalar_mul(nb, nb, -1.0), inc=vC)
    vnb = vC.n
    for ap in outs:
        p.op("V", lambda e, ap=ap: e.tensor_scalar(
            ap, x_sb, istd, nb, ALU.mult, ALU.add),
            waits=[(vC, vnb), (sC, sistd)], inc=vC)
    return sC.n, vC.n

def transpose_tile(ctx, src_bf, dst_fn, ncol, pre_wait=(), bank=7):
    p, C, sb, PS = ctx["p"], ctx["C"], ctx["sb"], ctx["PS"]
    tC, vC = C["tC"], C["vC"]
    done = 0
    j0 = 0
    while j0 < ncol:
        w = min(4, ncol - j0)
        for j in range(w):
            jj = j0 + j
            wl = list(pre_wait) if jj == 0 else []
            if j == 0:
                wl.append((vC, vC.n))
            p.op("T", lambda e, jj=jj, j=j: e.matmul(
                PS(bank)[:, 128*j:128*(j+1)], src_bf[:, 128*jj:128*(jj+1)],
                sb["ident"][:], start=True, stop=True), waits=wl)
        p.grp("T", tC)
        tv = tC.n
        p.op("V", lambda e, j0=j0, w=w, tv=tv: e.tensor_copy(
            dst_fn(j0, w), PS(bank)[:, 0:128*w].rearrange("p (c k) -> p c k", k=128)),
            waits=[(tC, tv)], inc=vC)
        done = tC.n
        j0 += w
    return done

def emit_attn_a(ctx, src, dst):
    nc, io, p, C, sb, mem, PS = (ctx[k] for k in ["nc", "io", "p", "C", "sb", "mem", "PS"])
    dY, dG, tC, sC, vC, gC = (C[k] for k in ["dY", "dG", "tC", "sC", "vC", "gC"])
    mem.reset(ctx["PHASE0"])
    yF = mem.seq("a_yF", [128, 8 * 2048], BF16)
    QF = mem.seq("a_QF", [128, 8 * 2048], BF16)
    KF = mem.seq("a_KF", [128, 8 * 2048], BF16)
    VT = mem.seq("a_VT", [128, 16 * 1024], BF16)
    wqk = mem.seq("a_wqk", [128, 2 * 1024], BF16)
    wvo = mem.seq("a_wvo", [128, 8 * 1024], BF16)
    xt = mem.seq("a_xt", [128, 4 * 1024], F32)
    ybf = mem.seq("a_ybf", [128, 2 * 1024], BF16)
    sinS = mem.seq("a_sinS", [128, 2 * 1024], BF16)
    scr = mem.seq("a_scr", [128, 1024], F32)
    res = mem.seq("a_res", [128, 2 * 1024], F32)
    rowbf0 = mem.seq("a_rowbf0", [1, D], BF16)
    rowbf1 = mem.seq("a_rowbf1", [1, D], BF16)
    sb["rowbf0"], sb["rowbf1"] = rowbf0, rowbf1
    yFr = lambda: yF[:].rearrange("p (c k) -> p c k", k=2048)

    r0 = p.dma("G", lambda e: e.dma_start(out=sb["rowbf0"][:], in_=io["a_bv"][:]))
    r1 = p.dma("G", lambda e: e.dma_start(out=sb["rowbf1"][:], in_=io["a_bo"][:]))
    ldrows = [r0, r1]

    for b in range(B_LOC):
        ln_s, tr_done = {}, {}
        for t in range(NT):
            x_sb = xt[:, 1024*(t % 4):1024*(t % 4 + 1)]
            w = [(sC, ln_s[t-4][0]), (vC, ln_s[t-4][1])] if t >= 4 else []
            ld = p.dma("Y", lambda e, b=b, t=t, x_sb=x_sb: e.dma_start(
                out=x_sb, in_=src[b, 128*t:128*(t+1), :]), waits=w)
            out_bf = ybf[:, 1024*(t % 2):1024*(t % 2 + 1)]
            pw = [ld]
            if t >= 2:
                pw.append((tC, tr_done[t-2]))
            sN, vN = ln_tile(ctx, x_sb, [out_bf], slot=t % 8, scratch=scr[:], pre_waits=pw)
            ln_s[t] = (sN, vN)
            if DBG_TAP == "attn_a" and b == 0:
                p.dma("Y", lambda e, t=t, out_bf=out_bf: e.dma_start(
                    out=io["dbg2"][:, 1024*t:1024*(t+1)], in_=out_bf), waits=[(vC, vN)])
            tr_done[t] = transpose_tile(
                ctx, out_bf, lambda j0, w_, t=t: yFr()[:, j0:j0+w_, 128*t:128*(t+1)],
                ncol=8, pre_wait=[(vC, vN)])
        p.join(["T", "S", "G"], [vC])
        if DBG_TAP == "attn_a" and b == 0:
            p.wait("Y", [(vC, vC.n)])
            p.dma("Y", lambda e: e.dma_start(out=io["dbg"][:], in_=yF[:]))


        wq_hist = []
        for wi, (Wn, OUT, bw) in enumerate((("aWqT", QF, 0), ("aWkT", KF, 1))):
            for oc in range(8):
                idx = wi * 8 + oc
                wb = wqk[:, 1024*(idx % 2):1024*(idx % 2 + 1)]
                w = [(tC, wq_hist[idx-2])] if idx >= 2 else []
                ldw = p.dma("G", lambda e, Wn=Wn, oc=oc, wb=wb: e.dma_start(
                    out=wb.rearrange("p (c k) -> p c k", k=128),
                    in_=io[Wn][:, 128*oc:128*(oc+1)].rearrange("(c p) k -> p c k", p=128)),
                    waits=w)
                for tg in range(4):
                    bank = tg % 2
                    for kc in range(8):
                        w2 = [ldw, (vC, vC.n)] if kc == 0 else []
                        p.op("T", lambda e, wb=wb, tg=tg, kc=kc, bank=bank: e.matmul(
                            PS(bank), wb[:, 128*kc:128*(kc+1)],
                            yFr()[:, kc, 512*tg:512*(tg+1)],
                            start=(kc == 0), stop=(kc == 7)), waits=w2)
                    p.grp("T", tC)
                    mmv = tC.n
                    p.op("V", lambda e, OUT=OUT, oc=oc, tg=tg, bank=bank, bw=bw:
                         e.tensor_scalar_add(
                            OUT[:].rearrange("p (c k) -> p c k", k=2048)[:, oc,
                                                                        512*tg:512*(tg+1)],
                            PS(bank), sb["a_bqk"][:, 2*oc+bw:2*oc+bw+1]),
                        waits=[(tC, mmv)], inc=vC)
                wq_hist.append(tC.n)
        p.join(["T"], [vC])


        ldv = p.dma("G", lambda e: e.dma_start(
            out=wvo[:].rearrange("p (c k) -> p c k", k=1024),
            in_=io["aWvT"][:].rearrange("(c p) k -> p c k", p=128)))
        for t in range(NT):
            for half in range(2):
                bank = (2 * t + half) % 2
                for kc in range(8):
                    w2 = [ldv, (vC, vC.n)] if kc == 0 else []
                    p.op("T", lambda e, t=t, half=half, kc=kc, bank=bank: e.matmul(
                        PS(bank), yFr()[:, kc, 128*t:128*(t+1)],
                        wvo[:, 1024*kc + 512*half: 1024*kc + 512*(half+1)],
                        start=(kc == 0), stop=False), waits=w2)
                p.op("T", lambda e, half=half, bank=bank: e.matmul(
                    PS(bank), sb["onesbf"][:, 0:128], sb["rowbf0"][:, 512*half:512*(half+1)],
                    start=False, stop=True),
                    waits=(ldrows if (t == 0 and half == 0) else []))
                p.grp("T", tC)
                mmv = tC.n
                p.op("V", lambda e, t=t, half=half, bank=bank: e.tensor_copy(
                    VT[:, 1024*t + 512*half: 1024*t + 512*(half+1)], PS(bank)),
                    waits=[(tC, mmv)], inc=vC)
        p.join(["T"], [vC])
        if DBG_TAP == "attn_a" and b == 0:
            p.wait("Y", [(vC, vC.n)])
            p.dma("Y", lambda e: e.dma_start(out=io["dbg3"][:], in_=VT[:]))

        ldwo = p.dma("G", lambda e: e.dma_start(
            out=wvo[:].rearrange("p (c k) -> p c k", k=1024),
            in_=io["aWoT"][:].rearrange("(c p) k -> p c k", p=128)),
            waits=[(vC, vC.n)])
        kf2 = mem.at("a_kf2", [128, 2 * 2048], BF16, mem.offs["a_xt"][0] + 8192)
        p.op("G", lambda e: e.memset(kf2[64:128, 0:2048], 0.0),
             waits=[(sC, sC.n), (vC, vC.n)])
        p.op("G", lambda e: e.memset(kf2[0:64, 2048:4096], 0.0), inc=gC)
        kzero = gC.n
        kf2_done, s_last = {}, {}
        sin_done, av_done = {}, {}
        for hp in range(8):
            wcp = [(vC, vC.n)] if hp == 0 else [(tC, s_last[hp-1])]
            if hp == 0:
                wcp.append((gC, kzero))
            p.op("V", lambda e, hp=hp: e.tensor_copy(
                kf2[0:64, 0:2048], KF[0:64, 2048*hp:2048*(hp+1)]), waits=wcp)
            p.op("V", lambda e, hp=hp: e.tensor_copy(
                kf2[64:128, 2048:4096], KF[64:128, 2048*hp:2048*(hp+1)]), inc=vC)
            kf2_done[hp] = vC.n
            for qg in range(4):
                for kc in range(16):
                    par = kc % 2
                    for which in range(2):
                        bank = 2 * par + which
                        w = ([(sC, sin_done[(hp, qg, kc - 2)])]
                             if (hp, qg, kc - 2) in sin_done else [])
                        if kc == 0 and which == 0 and (hp, qg) == (0, 0):
                            w = [(sC, sC.n)]
                        if kc == 0 and which == 0 and qg == 0:
                            w = list(w) + [(vC, kf2_done[hp])]
                        p.op("T", lambda e, hp=hp, qg=qg, kc=kc, bank=bank,
                             which=which: e.matmul(
                            PS(bank),
                            kf2[:, 2048*which + 128*kc: 2048*which + 128*(kc+1)],
                            QF[:, 2048*hp + 512*qg: 2048*hp + 512*(qg+1)],
                            start=True, stop=True), waits=w)
                    p.grp("T", tC)
                    smm = tC.n
                    if qg == 3 and kc == 15:
                        s_last[hp] = smm
                    slot = par
                    w = [(tC, smm)]
                    if (hp, qg, kc - 2) in av_done:
                        w.append((tC, av_done[(hp, qg, kc - 2)]))
                    elif kc < 2 and (hp, qg) != (0, 0):
                        w.append((tC, av_prev))
                    p.op("S", lambda e, par=par, slot=slot: e.activation(
                        sinS[:, 1024*slot:1024*(slot+1)], PS(2*par, 2), AF.Sin, scale=3.75),
                        waits=w, inc=sC)
                    sin_done[(hp, qg, kc)] = sC.n
                    for which in range(2):
                        w = [(sC, sin_done[(hp, qg, kc)])] if which == 0 else []
                        if kc == 0 and which == 0:
                            w.append((vC, vC.n))
                        p.op("T", lambda e, hp=hp, kc=kc, which=which, slot=slot: e.matmul(
                            PS(4)[64*which:64*(which+1), :],
                            VT[:, 1024*kc + 128*hp + 64*which: 1024*kc + 128*hp + 64*(which+1)],
                            sinS[:, 1024*slot + 512*which: 1024*slot + 512*(which+1)],
                            start=(kc == 0), stop=(kc == 15)), waits=w)
                    p.grp("T", tC)
                    av_done[(hp, qg, kc)] = tC.n
                av_prev = av_done[(hp, qg, 15)]
                p.op("V", lambda e, hp=hp, qg=qg, v=av_prev: e.tensor_copy(
                    yF[:, 2048*hp + 512*qg: 2048*hp + 512*(qg+1)], PS(4)),
                    waits=[(tC, av_prev)], inc=vC)
        p.join(["T"], [vC])

        add_done = 0
        st_tok = {}
        for t in range(NT):
            x_sb = xt[:, 1024*(t % 2):1024*(t % 2 + 1)]
            w2 = [(vC, add_done)] if t >= 2 else [(vC, vC.n), (sC, sC.n)]
            ldx = p.dma("Y", lambda e, b=b, t=t, x_sb=x_sb: e.dma_start(
                out=x_sb, in_=src[b, 128*t:128*(t+1), :]), waits=w2)
            rslot = t % 2
            for half in range(2):
                bank = 5 + half
                for kc in range(8):
                    w3 = [(vC, vC.n)] if kc == 0 else []
                    if kc == 0 and t == 0 and half == 0:
                        w3.append(ldwo)
                    p.op("T", lambda e, t=t, kc=kc, half=half, bank=bank: e.matmul(
                        PS(bank), yF[:, 2048*kc + 128*t: 2048*kc + 128*(t+1)],
                        wvo[:, 1024*kc + 512*half: 1024*kc + 512*(half+1)],
                        start=(kc == 0), stop=False), waits=w3)
                p.op("T", lambda e, half=half, bank=bank: e.matmul(
                    PS(bank), sb["onesbf"][:, 0:128], sb["rowbf1"][:, 512*half:512*(half+1)],
                    start=False, stop=True))
                p.grp("T", tC)
                mmv = tC.n
                wv = [(tC, mmv), ldx]
                if half == 0 and t >= 2:
                    wv.append(st_tok[t-2])
                p.op("V", lambda e, rslot=rslot, half=half, bank=bank, x_sb=x_sb: e.tensor_add(
                    res[:, 1024*rslot + 512*half: 1024*rslot + 512*(half+1)],
                    x_sb[:, 512*half:512*(half+1)], PS(bank)),
                    waits=wv, inc=vC)
            add_done = vC.n
            st_tok[t] = p.dma("Y", lambda e, b=b, t=t, rslot=rslot: e.dma_start(
                out=dst[b, 128*t:128*(t+1), :], in_=res[:, 1024*rslot:1024*(rslot+1)]),
                waits=[(vC, add_done)])
        p.join(["T", "S", "V", "G", "Y"], [vC, sC, tC])

def emit_ffn(ctx, stream, dst):
    nc, io, p, C, sb, mem, PS = (ctx[k] for k in ["nc", "io", "p", "C", "sb", "mem", "PS"])
    dY, dG, tC, sC, vC, gC = (C[k] for k in ["dY", "dG", "tC", "sC", "vC", "gC"])
    zs = io["zs"]
    NHC = 22
    CLIP = [(0, 128), (0, 256), (0, 384), (0, 512), (0, 512), (128, 512), (256, 512), (384, 512)]

    mem.reset(ctx["PHASE0"])
    hF = mem.seq("f_hF", [128, 8 * 1024], BF16)
    hF_off = mem.mark() - 8 * 1024 * 2
    qpk = mem.seq("f_qpk", [128, 4 * 2048], BF16)
    kpk = mem.seq("f_kpk", [128, 4 * 2048], BF16)
    kpk_off = mem.mark() - 4 * 2048 * 2
    vaug = mem.seq("f_vaug", [128, 16 * 520], BF16)
    wbuf = mem.seq("f_wbuf", [128, 8 * 1536], BF16)
    tabs = mem.seq("f_tabs", [128, 4 * 1024], F32)
    pexp = mem.seq("f_pexp", [128, 2 * 640], BF16)
    ptb = mem.seq("f_ptb", [128, 2 * 640], BF16)
    expSoff_ = mem.mark() - 4 * 640 * 2
    hbf = mem.seq("f_hbf", [128, 2 * 1024], BF16)
    hbf_off = mem.mark() - 2 * 1024 * 2
    hf32 = mem.seq("f_hf32", [128, 3 * 1024], F32)
    hf32_off = mem.mark() - 3 * 1024 * 4
    qkT = mem.seq("f_qkT", [128, 4 * 512], F32)
    qkT_off = mem.mark() - 4 * 512 * 4
    qkbf = mem.seq("f_qkbf", [128, 512], BF16)
    qkbf_off = mem.mark() - 512 * 2
    zupd = mem.seq("f_zupd", [128, 2 * 1024], F32)
    zupd_off = mem.mark() - 2 * 1024 * 4
    scr = mem.seq("f_scr", [128, 1024], F32)
    nrm = mem.seq("f_nrm", [128, 128], F32)
    qp2 = mem.seq("f_qp2", [128, 2 * 2048], BF16)
    oF = mem.at("f_oF", [128, 4 * 2048], BF16, hF_off)
    hlnF = mem.at("fB_hlnF", [128, 8 * 2048], BF16, hF_off)
    ohalf = mem.at("fB_ohalf", [128, 11 * 2048], BF16, kpk_off)
    w2buf = mem.at("fB_w2buf", [128, 11 * 1024], BF16, kpk_off + 45056)
    w1bufB = mem.at("fB_w1buf", [128, 2 * 2048], BF16, kpk_off + 45056 + 22528)
    gelt = mem.at("fB_gelt", [128, 512], F32, kpk_off + 45056 + 22528 + 8192)
    hlnbf = mem.at("fB_hlnbf", [128, 1024], BF16, kpk_off + 45056 + 22528 + 10240)

    lngB = mem.at("fB_lng", [128, D], F32, qkT_off)
    lnbB = mem.at("fB_lnb", [128, D], F32, qkT_off + 4096)
    b1cB = mem.at("fB_b1c", [128, 44], F32, qkbf_off)
    row2bfB = sb["row2bf"]
    assert kpk_off + 45056 + 22528 + 10240 + 2048 <= mem.offs["f_hf32"][0], "B alias overlap"
    hFr = lambda: hF[:].rearrange("p (c k) -> p c k", k=1024)
    hlnFr = lambda: hlnF[:].rearrange("p (c k) -> p c k", k=2048)
    tbr = lambda ti: tabs[:, 1024*ti:1024*(ti+1)].rearrange("p (t k) -> p t k", k=64)

    final_phase = dst is not zs and dst is not None

    p.op("G", lambda e: e.memset(qp2[64:128, 0:2048], 0.0),
         waits=[(sC, sC.n), (vC, vC.n)])
    p.op("G", lambda e: e.memset(qp2[0:64, 2048:4096], 0.0), inc=gC)
    qp2_zero = gC.n
    qp2_war = {0: 0, 1: 0}

    for b in range(B_LOC):
        zwr, ln_s = {}, {}
        for t in range(NT):
            x_sb = hf32[:, 1024*(t % 3):1024*(t % 3 + 1)]
            w = [(sC, ln_s[t-3][0]), (vC, ln_s[t-3][1])] if t >= 3 else []
            ld = p.dma("Y", lambda e, b=b, t=t, x_sb=x_sb: e.dma_start(
                out=x_sb, in_=stream[b, 128*t:128*(t+1), :]), waits=w)
            z_sb = zupd[:, 1024*(t % 2):1024*(t % 2 + 1)]
            pw = [ld]
            if t >= 2:
                pw.append(zwr[t-2])
            sN, vN = ln_tile(ctx, x_sb, [z_sb], slot=t % 8, scratch=scr[:], pre_waits=pw)
            ln_s[t] = (sN, vN)
            zwr[t] = p.dma("Y", lambda e, b=b, t=t, z_sb=z_sb: e.dma_start(
                out=zs[b, 128*t:128*(t+1), :], in_=z_sb), waits=[(vC, vN)])

        for layer in range(2):
            final = final_phase and layer == 1
            tabtoks = []
            tabtoks.append(p.dma("Y", lambda e, layer=layer: e.dma_start(
                out=sb["qs8"][:], in_=io["f_qs8"][layer:layer+1, :].broadcast_to([128, 512])),
                waits=[(vC, vC.n)]))
            tabtoks.append(p.dma("Y", lambda e, layer=layer: e.dma_start(
                out=sb["ks8"][:], in_=io["f_ks8"][layer:layer+1, :].broadcast_to([128, 512]))))
            for (ti, rawname) in ((0, "rot_qc"), (1, "rot_qs"), (2, "rot_kc"), (3, "rot_ks")):
                tabtoks.append(p.dma("Y", lambda e, rawname=rawname, ti=ti: e.dma_start(
                    out=tbr(ti), in_=io[rawname][:].rearrange("(t p) k -> p t k", p=128)),
                    waits=[(vC, vC.n)] if ti == 0 else []))
            for ti, sc in ((0, "qs8"), (2, "ks8")):
                p.op("V", lambda e, ti=ti, sc=sc: e.tensor_tensor(
                    tbr(ti), tbr(ti),
                    sb[sc][:, 0:64].unsqueeze(1).broadcast_to([128, 16, 64]), ALU.mult),
                    waits=tabtoks, inc=vC)
            for ti, sc in ((1, "qs8"), (3, "ks8")):
                p.op("V", lambda e, ti=ti, sc=sc: e.tensor_tensor(
                    tbr(ti)[:, :, 0:32], tbr(ti)[:, :, 0:32],
                    sb[sc][:, 32:64].unsqueeze(1).broadcast_to([128, 16, 32]), ALU.mult),
                    waits=tabtoks)
                p.op("V", lambda e, ti=ti, sc=sc: e.tensor_tensor(
                    tbr(ti)[:, :, 32:64], tbr(ti)[:, :, 32:64],
                    sb[sc][:, 0:32].unsqueeze(1).broadcast_to([128, 16, 32]), ALU.mult),
                    inc=vC)
            tab_done = vC.n
            ld_wqkv = p.dma("G", lambda e, layer=layer: e.dma_start(
                out=wbuf[:].rearrange("p (c k) -> p c k", k=1536),
                in_=io["f_wqkv"][layer].rearrange("(c p) k -> p c k", p=128)),
                waits=[(tC, tC.n)])
            for t in range(NT):
                p.op("G", lambda e, t=t: e.memset(
                    vaug[:, 520*t:520*(t+1)].rearrange("p (g k) -> p g k", k=65)[:, :, 64:65],
                    1.0), inc=gC)
            neg_done = gC.n

            tr_done = {}
            for half in range(2):
                for tl in range(8):
                    t = 8 * half + tl
                    h_sb = hbf[:, 1024*(t % 2):1024*(t % 2 + 1)]
                    w = [(tC, tr_done[t-2])] if t >= 2 else [(vC, tab_done)]
                    w += [zwr[t], zwr[max(t-1, 0)]]
                    ldh1 = p.dma("G", lambda e, b=b, t=t, h_sb=h_sb: e.dma_start(
                        out=h_sb[:, 0:512], in_=zs[b, 128*t:128*(t+1), 0:512]), waits=w)
                    if t == 0:
                        p.op("G", lambda e, h_sb=h_sb: e.memset(h_sb[0:1, 512:1024], 0.0),
                             inc=gC)
                        ldh2 = p.dma("G", lambda e, b=b, h_sb=h_sb: e.dma_start(
                            out=h_sb[1:128, 512:1024], in_=zs[b, 0:127, 512:1024]))
                    else:
                        ldh2 = p.dma("G", lambda e, b=b, t=t, h_sb=h_sb: e.dma_start(
                            out=h_sb[:, 512:1024], in_=zs[b, 128*t-1:128*(t+1)-1, 512:1024]))
                    tr_done[t] = transpose_tile(
                        ctx, h_sb, lambda j0, w_, tl=tl: hFr()[:, j0:j0+w_, 128*tl:128*(tl+1)],
                        ncol=8, pre_wait=[ldh1, ldh2, (gC, gC.n)])
                if FFN_DBG_STOP == f"A1h{half}":
                    p.join(["T", "S", "V", "G", "Y"], [tC, sC, vC, gC])
                    return
                for tl in range(8):
                    t = 8 * half + tl
                    for pi in range(3):
                        bank = pi
                        for kc in range(8):
                            w2 = []
                            if kc == 0:
                                w2 = [(vC, vC.n), (sC, sC.n)]
                                if tl == 0 and pi == 0:
                                    w2 += [ld_wqkv, (tC, tr_done[8*half+7])]
                            p.op("T", lambda e, tl=tl, pi=pi, kc=kc, bank=bank: e.matmul(
                                PS(bank), hFr()[:, kc, 128*tl:128*(tl+1)],
                                wbuf[:, 1536*kc + 512*pi: 1536*kc + 512*(pi+1)],
                                start=(kc == 0), stop=(kc == 7)), waits=w2)
                        p.grp("T", tC)
                        mmv = tC.n
                        if pi == 2:
                            p.op("V", lambda e, t=t, bank=bank: e.tensor_copy(
                                vaug[:, 520*t:520*(t+1)].rearrange(
                                    "p (g k) -> p g k", k=65)[:, :, 0:64],
                                PS(bank).rearrange("p (g k) -> p g k", k=64)),
                                waits=[(tC, mmv)], inc=vC)
                        else:
                            p.op("V", lambda e, pi=pi, bank=bank: e.tensor_copy(
                                qkT[:, 512*pi:512*(pi+1)], PS(bank)),
                                waits=[(tC, mmv)], inc=vC)
                    qk_ready = vC.n
                    if FFN_DBG_STOP == f"A2h{half}t{tl}":
                        p.join(["T", "S", "V", "G", "Y"], [tC, sC, vC, gC])
                        return
                    for (pi, TA, TB, dstc) in ((0, 0, 1, qpk), (1, 2, 3, kpk)):
                        src3 = qkT[:, 512*pi:512*(pi+1)].rearrange("p (h k) -> p h k", k=64)
                        u3 = qkT[:, 1024:1536].rearrange("p (h k) -> p h k", k=64)
                        w3 = qkT[:, 1536:2048].rearrange("p (h k) -> p h k", k=64)
                        ss = nrm[:, (t % 4) * 16 + 8 * pi:(t % 4) * 16 + 8 * pi + 8]
                        p.op("S", lambda e, pi=pi: e.activation(
                            scr[:, 0:512], qkT[:, 512*pi:512*(pi+1)], AF.Square),
                            waits=[(vC, qk_ready)], inc=sC)
                        sqv = sC.n
                        p.op("V", lambda e, ss=ss: e.reduce_sum(
                            ss, scr[:, 0:512].rearrange("p (h k) -> p h k", k=64), axis=AX.X),
                            waits=[(sC, sqv)])
                        p.op("V", lambda e, ss=ss: e.tensor_scalar_max(ss, ss, 1e-24), inc=vC)
                        ssv = vC.n
                        p.op("S", lambda e, ss=ss: e.activation(ss, ss, AF.Ln),
                             waits=[(vC, ssv)])
                        p.op("S", lambda e, ss=ss: e.activation(
                            ss, ss, AF.Exp, scale=-0.5), inc=sC)
                        sexp = sC.n
                        if pi == 0:
                            p.op("V", lambda e, ss=ss: e.tensor_scalar_mul(ss, ss, 8.0),
                                 waits=[(sC, sexp)], inc=vC)
                            inv_ctr, inv_done = vC, vC.n
                        else:
                            inv_ctr, inv_done = sC, sexp
                        TAv = tbr(TA)[:, t % 16, :]
                        TBv = tbr(TB)[:, t % 16, :]
                        p.op("V", lambda e, u3=u3, src3=src3, TAv=TAv: e.tensor_tensor(
                            u3, src3, TAv.unsqueeze(1).broadcast_to([128, 8, 64]), ALU.mult))
                        p.op("V", lambda e, w3=w3, src3=src3, TBv=TBv: e.tensor_tensor(
                            w3[:, :, 0:32], src3[:, :, 32:64],
                            TBv[:, 0:32].unsqueeze(1).broadcast_to([128, 8, 32]), ALU.mult))
                        p.op("V", lambda e, w3=w3, src3=src3, TBv=TBv: e.tensor_tensor(
                            w3[:, :, 32:64], src3[:, :, 0:32],
                            TBv[:, 32:64].unsqueeze(1).broadcast_to([128, 8, 32]), ALU.mult))
                        p.op("V", lambda e, u3=u3, w3=w3: e.tensor_sub(
                            u3[:, :, 0:32], u3[:, :, 0:32], w3[:, :, 0:32]))
                        p.op("V", lambda e, u3=u3, w3=w3: e.tensor_add(
                            u3[:, :, 32:64], u3[:, :, 32:64], w3[:, :, 32:64]))
                        p.op("V", lambda e, u3=u3, ss=ss: e.tensor_tensor(
                            u3, u3, ss.unsqueeze(2).broadcast_to([128, 8, 64]), ALU.mult),
                            waits=[(inv_ctr, inv_done)], inc=vC)
                        rot_v = vC.n
                        p.op("V", lambda e: e.tensor_copy(qkbf[:], qkT[:, 1024:1536]),
                             waits=[(vC, rot_v), (tC, tC.n)], inc=vC)
                        cpv = vC.n
                        for j in range(4):
                            w4 = [(vC, cpv), (sC, sC.n)] if j == 0 else []
                            p.op("T", lambda e, j=j: e.matmul(
                                PS(6)[:, 128*j:128*(j+1)], qkbf[:, 128*j:128*(j+1)],
                                sb["ident"][:], start=True, stop=True), waits=w4)
                        p.grp("T", tC)
                        trv = tC.n
                        p.op("S", lambda e, dstc=dstc, t=t: e.activation(
                            dstc[:].rearrange("p (c k) -> p c k", k=2048)[:, :, 128*t:128*(t+1)],
                            PS(6)[:].rearrange("p (c k) -> p c k", k=128), AF.Copy),
                            waits=[(tC, trv)], inc=sC)
            p.join(["T"], [sC, vC, gC])
            if FFN_DBG_STOP == "A3":
                return

            for h in range(8):
                hb, hc2 = 64 * (h % 2), h // 2
                hs = h % 2
                wcp = [(sC, sC.n), (tC, qp2_war[hs])]
                if h == 0 and b == 0:
                    wcp.append((gC, qp2_zero))
                p.op("V", lambda e, hb=hb, hc2=hc2, hs=hs: e.tensor_copy(
                    qp2[hb:hb+64, 2048*hs:2048*(hs+1)],
                    qpk[hb:hb+64, 2048*hc2:2048*(hc2+1)]), waits=wcp, inc=vC)
                qp2_done = vC.n
                prev_exp = 0
                state = {}
                ptev_hist = {-1: (vC, 0)}
                oev_hist = {-1: 0}

                def emit_S(qc, h=h, hb=hb, hc2=hc2):
                    kcs = list(range(max(0, qc - 4), qc + 1))
                    nk = len(kcs)
                    W_ = 128 * nk
                    sb0 = 2 * (qc % 2)
                    pslot = qc % 2
                    stat = nrm[:, 16*pslot:16*pslot+4]
                    st_ = state[qc] = dict(kcs=kcs, nk=nk, W_=W_, sb0=sb0, pslot=pslot,
                                           stat=stat)
                    w = [(sC, state[qc-2]["exp"])] if qc >= 2 else [(sC, sC.n)]
                    if qc == 0:
                        w = list(w) + [(vC, qp2_done)]
                    for ji, kc in enumerate(kcs):
                        p.op("T", lambda e, hs=hs, hc2=hc2, qc=qc, kc=kc, ji=ji,
                             sb0=sb0: e.matmul(
                            PS(sb0, 2)[:, 128*ji:128*(ji+1)],
                            qp2[:, 2048*hs + 128*qc: 2048*hs + 128*(qc+1)],
                            kpk[:, 2048*hc2 + 128*kc: 2048*hc2 + 128*(kc+1)],
                            start=True, stop=True),
                            waits=(w if ji == 0 else []))
                        p.grp("T", tC)
                    st_["smm"] = tC.n

                def emit_softmax(qc):
                    st_ = state[qc]
                    sb0, W_, nk, pslot, stat = (st_[k] for k in
                                                ("sb0", "W_", "nk", "pslot", "stat"))
                    smm = st_["smm"]
                    if qc >= 4:
                        p.op("V", lambda e, sb0=sb0: e.tensor_add(
                            PS(sb0, 2)[:, 0:128], PS(sb0, 2)[:, 0:128],
                            sb["maskadd"][:, 0:128]), waits=[(tC, smm)])
                    p.op("V", lambda e, sb0=sb0, nk=nk: e.tensor_add(
                        PS(sb0, 2)[:, 128*(nk-1):128*nk], PS(sb0, 2)[:, 128*(nk-1):128*nk],
                        sb["maskadd"][:, 128:256]), waits=[(tC, smm)])
                    p.op("V", lambda e, sb0=sb0, W_=W_, stat=stat: e.reduce_max(
                        stat[:, 0:1], PS(sb0, 2)[:, 0:W_], axis=AX.X),
                         waits=[(sC, sC.n)])
                    p.op("V", lambda e, stat=stat: e.tensor_scalar_mul(
                        stat[:, 1:2], stat[:, 0:1], -1.0), inc=vC)
                    nmv = vC.n
                    ptw = state[qc-2]["ptv"] if qc >= 2 else 0
                    p.op("S", lambda e, sb0=sb0, W_=W_, stat=stat, pslot=pslot: e.activation(
                        pexp[:, 640*pslot:640*pslot + W_], PS(sb0, 2)[:, 0:W_], AF.Exp,
                        bias=stat[:, 1:2], accum_out=stat[:, 2:3]),
                        waits=[(vC, nmv), (tC, ptw)], inc=sC)
                    st_["exp"] = sC.n
                    p.op("V", lambda e, stat=stat: e.reciprocal(stat[:, 3:4], stat[:, 2:3]),
                         waits=[(sC, st_["exp"])], inc=vC)
                    rcv = vC.n
                    p.op("S", lambda e, W_=W_, stat=stat, pslot=pslot: e.activation(
                        pexp[:, 640*pslot:640*pslot + W_], pexp[:, 640*pslot:640*pslot + W_],
                        AF.Copy, scale=stat[:, 3:4]), waits=[(vC, rcv)], inc=sC)
                    st_["pnorm"] = (sC, sC.n)

                def emit_tail(qc, h=h):
                    st_ = state[qc]
                    nk, W_, pslot, kcs = (st_[k] for k in ("nk", "W_", "pslot", "kcs"))
                    for ji in range(nk):
                        w5 = ([st_["pnorm"], ptev_hist[qc-1]]
                              if ji == 0 else [])
                        p.op("T", lambda e, ji=ji, pslot=pslot: e.matmul(
                            PS(6, 2)[:, 128*ji:128*(ji+1)],
                            pexp[:, 640*pslot + 128*ji: 640*pslot + 128*(ji+1)],
                            sb["ident"][:], start=True, stop=True), waits=w5)
                        p.grp("T", tC)
                    st_["ptv"] = tC.n
                    if qc % 2 == 0:
                        p.op("V", lambda e, W_=W_, pslot=pslot: e.tensor_copy(
                            ptb[:, 640*pslot:640*pslot + W_], PS(6, 2)[:, 0:W_]),
                            waits=[(tC, st_["ptv"])], inc=vC)
                        ptev_hist[qc] = (vC, vC.n)
                    else:
                        p.op("S", lambda e, W_=W_, pslot=pslot: e.activation(
                            ptb[:, 640*pslot:640*pslot + W_], PS(6, 2)[:, 0:W_], AF.Copy),
                            waits=[(tC, st_["ptv"])], inc=sC)
                        ptev_hist[qc] = (sC, sC.n)
                    for ji, kc in enumerate(kcs):
                        w6 = ([ptev_hist[qc], (vC, oev_hist[qc-1])]
                              if ji == 0 else [])
                        p.op("T", lambda e, kc=kc, h=h, ji=ji, pslot=pslot,
                             first=(ji == 0), last=(ji == nk - 1): e.matmul(
                            PS(5)[0:64, 0:128],
                            vaug[:, 520*kc + 65*h: 520*kc + 65*h + 64],
                            ptb[:, 640*pslot + 128*ji: 640*pslot + 128*(ji+1)],
                            start=first, stop=last), waits=w6)
                        p.grp("T", tC)
                    avv = tC.n
                    p.op("V", lambda e, h=h, qc=qc: e.tensor_copy(
                        oF[64*(h % 2):64*(h % 2)+64,
                           2048*(h // 2) + 128*qc: 2048*(h // 2) + 128*(qc+1)],
                        PS(5)[0:64, 0:128]), waits=[(tC, avv)], inc=vC)
                    oev_hist[qc] = vC.n

                for qc in range(NT):
                    emit_S(qc)
                    emit_softmax(qc)
                    if qc >= 1:
                        emit_tail(qc - 1)
                emit_tail(NT - 1)
                qp2_war[hs] = state[NT - 1]["smm"]
            p.join(["T"], [vC, sC])

            ldwo = p.dma("G", lambda e, layer=layer: e.dma_start(
                out=wbuf[:, 0:4*1024].rearrange("p (c k) -> p c k", k=1024),
                in_=io["f_wo"][layer].rearrange("(c p) k -> p c k", p=128)),
                waits=[(tC, tC.n)])
            zw2 = {}
            upd_v = 0
            for t in range(NT):
                z_sb = zupd[:, 1024*(t % 2):1024*(t % 2 + 1)]
                w = ([(vC, upd_v), zw2[t-2]] if t >= 2
                     else [(sC, sC.n)] + p.dma_all()) + [zwr[t]]
                ldz = p.dma("Y", lambda e, b=b, t=t, z_sb=z_sb: e.dma_start(
                    out=z_sb, in_=zs[b, 128*t:128*(t+1), :]), waits=w)
                for half in range(2):
                    bank = half
                    for kc in range(4):
                        w2 = [(vC, vC.n)] if kc == 0 else []
                        if kc == 0 and t == 0 and half == 0:
                            w2.append(ldwo)
                        p.op("T", lambda e, t=t, kc=kc, half=half, bank=bank: e.matmul(
                            PS(bank), oF[:, 2048*kc + 128*t: 2048*kc + 128*(t+1)],
                            wbuf[:, 1024*kc + 512*half: 1024*kc + 512*(half+1)],
                            start=(kc == 0), stop=(kc == 3)), waits=w2)
                    p.grp("T", tC)
                    mmv = tC.n
                    p.op("V", lambda e, z_sb=z_sb, half=half, bank=bank: e.tensor_add(
                        z_sb[:, 512*half:512*(half+1)], z_sb[:, 512*half:512*(half+1)],
                        PS(bank)), waits=[(tC, mmv), ldz], inc=vC)
                upd_v = vC.n
                zw2[t] = p.dma("Y", lambda e, b=b, t=t, z_sb=z_sb: e.dma_start(
                    out=zs[b, 128*t:128*(t+1), :], in_=z_sb), waits=[(vC, upd_v)])
            p.join(["T", "S", "V", "G"], [])

            lnctoks = []
            lnctoks.append(p.dma("Y", lambda e, layer=layer: e.dma_start(
                out=lngB[:], in_=io["f_lng"][layer].broadcast_to([128, D])),
                waits=[(sC, sC.n), (vC, vC.n)]))
            lnctoks.append(p.dma("Y", lambda e, layer=layer: e.dma_start(
                out=lnbB[:], in_=io["f_lnb"][layer].broadcast_to([128, D]))))
            lnctoks.append(p.dma("Y", lambda e, layer=layer: e.dma_start(
                out=b1cB[:], in_=io["f_b1c"][layer])))
            ldb2 = p.dma("G", lambda e, layer=layer: e.dma_start(
                out=row2bfB[:], in_=io["f_b2"][layer]), waits=[(tC, tC.n)])
            trB = {}
            hlv = 0
            for t in range(NT):
                h_sb = hf32[:, 1024*(t % 3):1024*(t % 3 + 1)]
                w = [(tC, trB[t-3])] if t >= 3 else []
                w += [zw2[t], zw2[max(t-1, 0)]]
                ldh1 = p.dma("G", lambda e, b=b, t=t, h_sb=h_sb: e.dma_start(
                    out=h_sb[:, 0:512], in_=zs[b, 128*t:128*(t+1), 0:512]), waits=w)
                if t == 0:
                    p.op("G", lambda e, h_sb=h_sb: e.memset(h_sb[0:1, 512:1024], 0.0), inc=gC)
                    ldh2 = p.dma("G", lambda e, b=b, h_sb=h_sb: e.dma_start(
                        out=h_sb[1:128, 512:1024], in_=zs[b, 0:127, 512:1024]))
                else:
                    ldh2 = p.dma("G", lambda e, b=b, t=t, h_sb=h_sb: e.dma_start(
                        out=h_sb[:, 512:1024], in_=zs[b, 128*t-1:128*(t+1)-1, 512:1024]))
                pw = [ldh1, ldh2, (gC, gC.n), (vC, hlv)] + lnctoks
                sN, vN = ln_tile(ctx, h_sb, [scr[:]], slot=t % 8,
                                 scratch=zupd[:, 0:1024], pre_waits=pw)
                p.op("V", lambda e: e.tensor_mul(scr[:], scr[:], lngB[:]))
                p.op("V", lambda e: e.tensor_add(scr[:], scr[:], lnbB[:]))
                p.op("V", lambda e: e.tensor_copy(hlnbf[:], scr[:]), inc=vC)
                hlv = vC.n
                trB[t] = transpose_tile(
                    ctx, hlnbf, lambda j0, w_, t=t: hlnFr()[:, j0:j0+w_, 128*t:128*(t+1)],
                    ncol=8, pre_wait=[(vC, hlv)])
            p.join(["T"], [vC])

            zw3 = dict(zw2)
            for hhalf in range(2):
                chunks = list(range(11 * hhalf, min(11 * (hhalf + 1), NHC)))
                g_done = 0
                for ci, c in enumerate(chunks):
                    w1b = w1bufB[:, 2048*(ci % 2):2048*(ci % 2 + 1)]
                    w = [(tC, g_done)] if ci >= 2 else [(tC, tC.n)]
                    ldw1 = p.dma("G", lambda e, c=c, layer=layer, w1b=w1b: e.dma_start(
                        out=w1b.rearrange("p (ic k) -> p ic k", k=256),
                        in_=io["f_w1p"][layer][:, 256*c:256*(c+1)].rearrange(
                            "(ic p) k -> p ic k", p=128)), waits=w)
                    for tg in range(4):
                        abank = tg % 2
                        gbank = 2 + tg % 2
                        for kc in range(8):
                            w2 = [(vC, vC.n), ldw1] if kc == 0 else []
                            p.op("T", lambda e, kc=kc, tg=tg, abank=abank, w1b=w1b: e.matmul(
                                PS(abank), w1b[:, 256*kc:256*kc+128],
                                hlnFr()[:, kc, 512*tg:512*(tg+1)],
                                start=(kc == 0), stop=(kc == 7)), waits=w2)
                        for kc in range(8):
                            w2 = [(sC, sC.n)] if kc == 0 else []
                            p.op("T", lambda e, kc=kc, tg=tg, gbank=gbank, w1b=w1b: e.matmul(
                                PS(gbank), w1b[:, 256*kc+128:256*kc+256],
                                hlnFr()[:, kc, 512*tg:512*(tg+1)],
                                start=(kc == 0), stop=(kc == 7)), waits=w2)
                        p.grp("T", tC)
                        mmv = tC.n
                        p.op("S", lambda e, gbank=gbank, c=c: e.activation(
                            gelt[:], PS(gbank), AF.Gelu, bias=b1cB[:, 2*c+1:2*c+2]),
                            waits=[(tC, mmv)], inc=sC)
                        gel = sC.n
                        p.op("V", lambda e, abank=abank, c=c: e.tensor_scalar_add(
                            scr[:, 0:512], PS(abank), b1cB[:, 2*c:2*c+1]),
                            waits=[(tC, mmv)])
                        p.op("V", lambda e, ci=ci, tg=tg: e.tensor_mul(
                            ohalf[:, 2048*ci + 512*tg: 2048*ci + 512*(tg+1)],
                            scr[:, 0:512], gelt[:]), waits=[(sC, gel)], inc=vC)
                    g_done = tC.n
                ldw2 = p.dma("G", lambda e, layer=layer, hhalf=hhalf, nch=len(chunks):
                             e.dma_start(
                    out=w2buf[:, 0:1024*nch].rearrange("p (c k) -> p c k", k=1024),
                    in_=io["f_w2p"][layer, 11*hhalf:11*hhalf+nch].rearrange("c p k -> p c k")),
                    waits=[(tC, tC.n)])
                p.join(["T"], [vC])
                updB = 0
                st3 = {}
                for t in range(NT):
                    z_sb = zupd[:, 1024*(t % 2):1024*(t % 2 + 1)]
                    w = ([(vC, updB), st3[t-2]] if t >= 2
                         else [(sC, sC.n)] + p.dma_all())
                    ldz = p.dma("Y", lambda e, b=b, t=t, z_sb=z_sb: e.dma_start(
                        out=z_sb, in_=zs[b, 128*t:128*(t+1), :]),
                        waits=w + [zw3[t]])
                    ldx = None
                    if final and hhalf == 1:
                        x_sb = hf32[:, 1024*(t % 2):1024*(t % 2 + 1)]
                        ldx = p.dma("Y", lambda e, b=b, t=t, x_sb=x_sb: e.dma_start(
                            out=x_sb, in_=stream[b, 128*t:128*(t+1), :]),
                            waits=([(sC, sC.n)] if t < 3 else []))
                    for half in range(2):
                        bank = 4 + half
                        nch = len(chunks)
                        for ci in range(nch):
                            w2 = [(vC, vC.n), ldw2] if ci == 0 else []
                            p.op("T", lambda e, ci=ci, t=t, half=half, bank=bank,
                                 last=(ci == nch - 1 and hhalf == 0): e.matmul(
                                PS(bank), ohalf[:, 2048*ci + 128*t: 2048*ci + 128*(t+1)],
                                w2buf[:, 1024*ci + 512*half: 1024*ci + 512*(half+1)],
                                start=(ci == 0), stop=last), waits=w2)
                        if hhalf == 1:
                            p.op("T", lambda e, half=half, bank=bank: e.matmul(
                                PS(bank), sb["onesbf"][:, 0:128],
                                row2bfB[:, 512*half:512*(half+1)],
                                start=False, stop=True),
                                waits=([ldb2] if (t == 0 and half == 0) else []))
                        p.grp("T", tC)
                        mmv = tC.n
                        if final and hhalf == 1:
                            p.op("V", lambda e, z_sb=z_sb, half=half, bank=bank: e.tensor_add(
                                z_sb[:, 512*half:512*(half+1)],
                                z_sb[:, 512*half:512*(half+1)], PS(bank)),
                                waits=[(tC, mmv), ldz])
                            p.op("V", lambda e, z_sb=z_sb, half=half, x_sb=x_sb: e.tensor_add(
                                z_sb[:, 512*half:512*(half+1)],
                                z_sb[:, 512*half:512*(half+1)],
                                x_sb[:, 512*half:512*(half+1)]),
                                waits=[ldx], inc=vC)
                        else:
                            p.op("V", lambda e, z_sb=z_sb, half=half, bank=bank: e.tensor_add(
                                z_sb[:, 512*half:512*(half+1)],
                                z_sb[:, 512*half:512*(half+1)], PS(bank)),
                                waits=[(tC, mmv), ldz], inc=vC)
                    updB = vC.n
                    outdst = dst if (final and hhalf == 1) else zs
                    zw3[t] = st3[t] = p.dma("Y", lambda e, b=b, t=t, z_sb=z_sb,
                                            outdst=outdst: e.dma_start(
                        out=outdst[b, 128*t:128*(t+1), :], in_=z_sb),
                        waits=[(vC, updB)])
                p.join(["T", "S", "V", "G"], [])
            zwr = dict(zw3)
        p.join(["T", "S", "V", "G", "Y"], [vC, sC, tC, gC])

def emit_attn_b(ctx, src, dst):
    nc, io, p, C, sb, mem, PS = (ctx[k] for k in ["nc", "io", "p", "C", "sb", "mem", "PS"])
    dY, dG, tC, sC, vC, gC = (C[k] for k in ["dY", "dG", "tC", "sC", "vC", "gC"])
    es = io["zs"]
    SCALE = 0.25

    mem.reset(ctx["PHASE0"])
    eF = mem.seq("b_eF", [128, 8 * 2048], BF16)
    eF_off = mem.mark() - 32 * 1024
    qpad = mem.seq("b_qpad", [128, 4 * 2048], BF16)
    kpad = mem.seq("b_kpad", [128, 4 * 2048], BF16)
    vaug = mem.seq("b_vaug", [128, 16 * 1040], BF16)
    wvp = mem.seq("b_wvp", [128, 8 * 1024], BF16)
    qkraw = mem.seq("b_qkraw", [128, 4 * 2048], BF16)
    et = mem.seq("b_et", [128, 4 * 1024], F32)
    qn = mem.seq("b_qn", [8, 4096], BF16)
    kn = mem.seq("b_kn", [8, 4096], BF16)
    sqb = mem.seq("b_sqb", [128, 2048], BF16)
    scr = mem.seq("b_scr", [128, 1024], F32)
    maxk1 = mem.seq("b_maxk1", [1, 16], F32)
    ebf = mem.seq("b_ebf", [128, 1024], BF16)
    wq1 = mem.seq("b_wq1", [128, 2 * 1024], BF16)
    w1buf = mem.seq("b_w1buf", [128, 8 * 64], BF16)
    expS = mem.at("b_expS", [128, 2 * 2048], BF16, eF_off)
    outF = mem.at("b_outF", [128, 8 * 512], BF16, eF_off + 8 * 1024)
    resF = mem.at("b_resF", [128, 8 * 512], BF16, eF_off + 16 * 1024)
    obf = mem.at("b_obf", [128, 2048], BF16, eF_off + 24 * 1024)
    hW = mem.at("b_hW", [65, 512], BF16, eF_off + 28 * 1024)
    rowstb = mem.at("b_rowst", [1, 2048], BF16, eF_off)
    rowstb2 = mem.at("b_rowst2", [1, 2048], BF16, eF_off + 4096)
    eFr = lambda: eF[:].rearrange("p (c k) -> p c k", k=2048)

    for b in range(B_LOC):
        ewr, ln_s = {}, {}
        for t in range(NT):
            x_sb = et[:, 1024*(t % 4):1024*(t % 4 + 1)]
            w = [(sC, ln_s[t-4][0]), (vC, ln_s[t-4][1])] if t >= 4 else []
            ld = p.dma("Y", lambda e, b=b, t=t, x_sb=x_sb: e.dma_start(
                out=x_sb, in_=src[b, 128*t:128*(t+1), :]), waits=w)
            z_sb = scr[:]
            pw = [ld]
            if t >= 1:
                pw.append(ewr[t-1])
            sN, vN = ln_tile(ctx, x_sb, [z_sb], slot=t % 8,
                             scratch=sqb[:].bitcast(F32), pre_waits=pw)
            ln_s[t] = (sN, vN)
            ewr[t] = p.dma("Y", lambda e, b=b, t=t, z_sb=z_sb: e.dma_start(
                out=es[b, 128*t:128*(t+1), :], in_=z_sb), waits=[(vC, vN)])

        negd = []
        for cch in range(16):
            g_, si = cch // 4, cch % 4
            negd.append(p.dma("Y", lambda e, g_=g_, si=si: e.dma_start(
                out=kpad[32*si+16:32*si+17, 2048*g_:2048*(g_+1)], in_=io["negrow"][:])))

        for it in range(2):
            tr_done = {}
            for t in range(NT):
                w = [(tC, tr_done[t-1])] if t >= 1 else list(negd)
                w += [ewr[t]]
                ldt = p.dma("G", lambda e, b=b, t=t: e.dma_start(
                    out=ebf[:], in_=es[b, 128*t:128*(t+1), :]), waits=w)
                tr_done[t] = transpose_tile(
                    ctx, ebf, lambda j0, w_, t=t: eFr()[:, j0:j0+w_, 128*t:128*(t+1)],
                    ncol=8, pre_wait=[ldt])
            p.join(["T"], [vC])

            for wi, Wn in enumerate(("bWqT", "bWkT")):
                for oc in range(2):
                    idx = 2 * wi + oc
                    wb = wq1[:, 1024*(idx % 2):1024*(idx % 2 + 1)]
                    w = [(tC, qk_hist)] if idx >= 2 else [(tC, tC.n)]
                    ldw = p.dma("G", lambda e, Wn=Wn, oc=oc, wb=wb: e.dma_start(
                        out=wb.rearrange("p (c k) -> p c k", k=128),
                        in_=io[Wn][:, 128*oc:128*(oc+1)].rearrange("(c p) k -> p c k", p=128)),
                        waits=w)
                    for tg in range(4):
                        bank = tg % 2
                        for kc in range(8):
                            w2 = [ldw, (sC, sC.n)] if kc == 0 else []
                            p.op("T", lambda e, wb=wb, tg=tg, kc=kc, bank=bank: e.matmul(
                                PS(bank), wb[:, 128*kc:128*(kc+1)],
                                eFr()[:, kc, 512*tg:512*(tg+1)],
                                start=(kc == 0), stop=(kc == 7)), waits=w2)
                        p.grp("T", tC)
                        mmv = tC.n
                        sc_ = SCALE if wi == 0 else 1.0
                        p.op("S", lambda e, wi=wi, oc=oc, tg=tg, bank=bank, sc_=sc_: e.activation(
                            qkraw[:, 2048*(2*wi+oc) + 512*tg: 2048*(2*wi+oc) + 512*(tg+1)],
                            PS(bank), AF.Copy, scale=sc_), waits=[(tC, mmv)], inc=sC)
                    qk_hist = tC.n
            ldv = p.dma("G", lambda e: e.dma_start(
                out=wvp[:].rearrange("p (c k) -> p c k", k=1024),
                in_=io["bWvT"][:].rearrange("(c p) k -> p c k", p=128)),
                waits=[(tC, tC.n)])
            for t in range(NT):
                for half in range(2):
                    bank = (2 * t + half) % 2
                    for kc in range(8):
                        w2 = [ldv, (vC, vC.n), (sC, sC.n)] if kc == 0 else []
                        p.op("T", lambda e, t=t, half=half, kc=kc, bank=bank: e.matmul(
                            PS(bank), eFr()[:, kc, 128*t:128*(t+1)],
                            wvp[:, 1024*kc + 512*half: 1024*kc + 512*(half+1)],
                            start=(kc == 0), stop=(kc == 7)), waits=w2)
                    p.grp("T", tC)
                    mmv = tC.n
                    p.op("V", lambda e, t=t, half=half, bank=bank: e.tensor_copy(
                        vaug[:, 1040*t:1040*(t+1)].rearrange(
                            "p (g k) -> p g k", k=65)[:, 8*half:8*half+8, 0:64],
                        PS(bank).rearrange("p (g k) -> p g k", k=64)),
                        waits=[(tC, mmv)], inc=vC)
            for t in range(NT):
                p.op("G", lambda e, t=t: e.memset(
                    vaug[:, 1040*t:1040*(t+1)].rearrange("p (g k) -> p g k", k=65)[:, :, 64:65],
                    1.0), inc=gC)
            vones = gC.n
            p.join(["T"], [vC, sC])

            for wi, (rawbase, dstn) in enumerate(((0, qn), (2, kn))):
                for ch in range(2):
                    p.op("S", lambda e, rawbase=rawbase, ch=ch: e.activation(
                        sqb[:], qkraw[:, 2048*(rawbase+ch):2048*(rawbase+ch+1)], AF.Square),
                        waits=[(vC, vC.n)], inc=sC)
                    sqv = sC.n
                    for ng in range(4):
                        w4 = [(sC, sqv), (sC, sC.n)] if ng == 0 else []
                        p.op("T", lambda e, ng=ng: e.matmul(
                            PS(2, 4)[0:8, 512*ng:512*(ng+1)], sb["ind8"][:],
                            sqb[:, 512*ng:512*(ng+1)], start=True, stop=True), waits=w4)
                    p.grp("T", tC)
                    mmn = tC.n
                    p.op("S", lambda e, dstn=dstn, ch=ch: e.activation(
                        dstn[0:8, 2048*ch:2048*(ch+1)], PS(2, 4)[0:8, 0:2048], AF.Copy),
                        waits=[(tC, mmn)], inc=sC)
            nrm_s = sC.n
            p.op("V", lambda e: e.reduce_max(
                scr[0:8, 0:2], kn[0:8, :].rearrange("p (c k) -> p c k", k=2048), axis=AX.X),
                waits=[(sC, nrm_s)], inc=vC)
            mkv = vC.n
            mkd = p.dma("Y", lambda e: e.dma_start(out=maxk1[0:1, 0:16], in_=scr[0:8, 0:2],
                                                   allow_slow_non_contiguous=True),
                        waits=[(vC, mkv)])
            for cch in range(16):
                g_, si = cch // 4, cch % 4
                src_ch, src_r = cch // 8, 16 * (cch % 8)
                for (rawbase, dpad) in ((0, qpad), (2, kpad)):
                    p.dma("Y", lambda e, rawbase=rawbase, src_ch=src_ch, src_r=src_r,
                          g_=g_, si=si, dpad=dpad: e.dma_start(
                        out=dpad[32*si:32*si+16, 2048*g_:2048*(g_+1)],
                        in_=qkraw[src_r:src_r+16,
                                  2048*(rawbase+src_ch):2048*(rawbase+src_ch+1)]),
                        waits=[(sC, nrm_s)] if cch == 0 else [])
                rowd = p.dma("Y", lambda e, cch=cch, src_ch=src_ch: e.dma_start(
                    out=rowstb[:], in_=qn[cch % 8:cch % 8 + 1,
                                          2048*src_ch:2048*(src_ch+1)]),
                    waits=[(sC, bsts)] if cch > 0 else [(sC, nrm_s)])
                wS = [rowd, mkd]
                if cch > 0:
                    wS.append(brow_tok)
                p.op("S", lambda e, cch=cch: e.activation(
                    rowstb2[:], rowstb[:], AF.Sqrt,
                    scale=maxk1[0:1, cch:cch+1]), waits=wS, inc=sC)
                bsts = sC.n
                brow_tok = p.dma("Y", lambda e, g_=g_, si=si: e.dma_start(
                    out=qpad[32*si+16:32*si+17, 2048*g_:2048*(g_+1)], in_=rowstb2[:]),
                    waits=[(sC, bsts)])
            p.join(["T"], [vC, gC])

            ldwp = p.dma("G", lambda e: e.dma_start(
                out=wvp[:].rearrange("p (c k) -> p c k", k=1024),
                in_=io["bWpT"][:].rearrange("(c p) k -> p c k", p=128)),
                waits=[(tC, tC.n)])
            ldw1 = p.dma("G", lambda e: e.dma_start(
                out=w1buf[:].rearrange("p (c k) -> p c k", k=64),
                in_=io["bW1T"][:].rearrange("(c p) k -> p c k", p=128)))
            p.op("G", lambda e: e.memset(hW[64:65, :], 1.0), inc=gC)
            hwones = gC.n

            PSF = PS(0, 8)

            def AVAP(i):
                off = 512 * (4 + i // 6) + 65 * (i % 6)
                return PSF[:, off:off + 65]
            av_war = vC.n
            prev_av_last = tC.n
            ptv_hist = {}
            bank_tok = {3: (vC, 0), 7: (vC, 0)}
            for ng in range(4):
                for cg in range(4):
                    gcg = 4 * ng + cg
                    obslot = 1024 * (gcg % 2)
                    avd = {}
                    exp_done = {}
                    for mg in range(16):
                        w = ([(sC, exp_done[mg - 1])] if mg >= 1
                             else [(sC, sC.n), (vC, vC.n)])
                        for j in range(4):
                            for s in range(4):
                                cc = 32 * ((s + j) % 4)
                                p.op("T", lambda e, cg=cg, s=s, cc=cc, mg=mg, ng=ng:
                                     e.matmul(
                                    PSF[cc:cc+32, 512*s:512*s+512],
                                    kpad[32*s:32*s+17,
                                         2048*cg + 128*mg + cc: 2048*cg + 128*mg + cc + 32],
                                    qpad[32*s:32*s+17,
                                         2048*cg + 512*ng: 2048*cg + 512*(ng+1)],
                                    start=True, stop=True, tile_position=(32*s, cc)),
                                    waits=(w if (j == 0 and s == 0) else []))
                        p.grp("T", tC)
                        smm = tC.n
                        eslot = mg % 2
                        w = [(tC, smm)]
                        if mg >= 2:
                            w.append((tC, avd[mg - 2]))
                        elif gcg > 0 or it > 0 or b > 0:
                            w.append((tC, prev_av_last))
                        p.op("S", lambda e, eslot=eslot: e.activation(
                            expS[:, 2048*eslot:2048*(eslot+1)], PS(0, 4), AF.Exp),
                            waits=w, inc=sC)
                        exp_done[mg] = sC.n
                        for s in range(4):
                            for qq in range(4):
                                i = 4 * s + qq
                                w = []
                                if i == 0:
                                    w = [(sC, exp_done[mg])]
                                    if mg == 0:
                                        w += [(gC, vones), (vC, av_war)]
                                p.op("T", lambda e, i=i, s=s, qq=qq, mg=mg, cg=cg,
                                     eslot=eslot,
                                     first=(mg == 0 and i in (0, 6, 12)),
                                     last=(mg == 15 and i in (5, 11, 15)):
                                     e.matmul(
                                    AVAP(i),
                                    expS[:, 2048*eslot + 512*s + 128*qq:
                                         2048*eslot + 512*s + 128*(qq+1)],
                                    vaug[:, 1040*mg + 65*(4*cg+s):
                                         1040*mg + 65*(4*cg+s) + 65],
                                    start=first, stop=last), waits=w)
                        p.grp("T", tC)
                        avd[mg] = tC.n
                    av_last = prev_av_last = avd[15]
                    for jb in range(3):
                        nsl = (6, 6, 4)[jb]
                        p.op("V", lambda e, jb=jb, nsl=nsl: e.reciprocal(
                            scr[:, 8*jb:8*jb+nsl].rearrange("p (g k) -> p g k", k=1),
                            PSF[:, 512*(4+jb):512*(4+jb) + 65*nsl].rearrange(
                                "p (g k) -> p g k", k=65)[:, :, 64:65]),
                            waits=([(tC, av_last)] if jb == 0 else []))
                    rcv = vC.n
                    for i in range(16):
                        w = []
                        if i == 0:
                            w = [(vC, rcv), (tC, ptv_hist.get(gcg - 2, 0))]
                        p.op("V", lambda e, i=i, obslot=obslot: e.tensor_scalar_mul(
                            obf[:, obslot + 64*i: obslot + 64*(i+1)],
                            AVAP(i)[:, 0:64], scr[:, 8*(i//6) + (i % 6):
                                                  8*(i//6) + (i % 6) + 1]), waits=w)
                    nrmv = vC.n
                    av_war = nrmv
                    for s in range(4):
                        tb = 3 if s % 2 == 0 else 7
                        for qq in range(4):
                            w = []
                            if qq == 0:
                                w = [(vC, nrmv), bank_tok[tb]]
                            p.op("T", lambda e, s=s, qq=qq, tb=tb, obslot=obslot:
                                 e.matmul(
                                PS(tb)[0:64, 128*qq:128*(qq+1)],
                                obf[:, obslot + 256*s + 64*qq:
                                    obslot + 256*s + 64*(qq+1)],
                                sb["ident"][:], start=True, stop=True), waits=w)
                        p.grp("T", tC)
                        ptv_s = tC.n
                        cch = 4 * cg + s
                        if s % 2 == 0:
                            p.op("V", lambda e, cch=cch, tb=tb: e.tensor_copy(
                                outF[64*(cch % 2):64*(cch % 2)+64,
                                     512*(cch // 2):512*(cch // 2 + 1)],
                                PS(tb)[0:64, 0:512]), waits=[(tC, ptv_s)], inc=vC)
                            bank_tok[tb] = (vC, vC.n)
                        else:
                            p.op("S", lambda e, cch=cch, tb=tb: e.activation(
                                outF[64*(cch % 2):64*(cch % 2)+64,
                                     512*(cch // 2):512*(cch // 2 + 1)],
                                PS(tb)[0:64, 0:512], AF.Copy), waits=[(tC, ptv_s)], inc=sC)
                            bank_tok[tb] = (sC, sC.n)
                    ptv_hist[gcg] = tC.n
                out_done = vC.n
                wp_v = 0
                for tc in range(4):
                    t = 4 * ng + tc
                    e_sb = et[:, 1024*tc:1024*(tc+1)]
                    prev_t = (4 * (ng - 1)) % 16 + tc
                    lde = p.dma("Y", lambda e, b=b, t=t, e_sb=e_sb: e.dma_start(
                        out=e_sb, in_=es[b, 128*t:128*(t+1), :]),
                        waits=[(vC, out_done), ewr[t], ewr[prev_t]])
                    for half in range(2):
                        bank = 6 + half
                        for kc in range(8):
                            w2 = [(vC, vC.n)] if kc == 0 else []
                            if tc == 0 and half == 0 and kc == 0:
                                w2 += [ldwp, (vC, out_done), (sC, sC.n)]
                            p.op("T", lambda e, kc=kc, tc=tc, half=half, bank=bank: e.matmul(
                                PS(bank), outF[:, 512*kc + 128*tc: 512*kc + 128*(tc+1)],
                                wvp[:, 1024*kc + 512*half: 1024*kc + 512*(half+1)],
                                start=(kc == 0), stop=(kc == 7)), waits=w2)
                        p.grp("T", tC)
                        mmv = tC.n
                        p.op("V", lambda e, e_sb=e_sb, half=half, bank=bank: e.tensor_add(
                            e_sb[:, 512*half:512*(half+1)], e_sb[:, 512*half:512*(half+1)],
                            PS(bank)), waits=[(tC, mmv), lde], inc=vC)
                    wp_v = vC.n
                rtr = 0
                for tc in range(4):
                    e_sb = et[:, 1024*tc:1024*(tc+1)]
                    p.op("V", lambda e, e_sb=e_sb: e.tensor_copy(ebf[:], e_sb),
                         waits=[(vC, wp_v)] if tc == 0 else [(tC, rtr)], inc=vC)
                    cpv = vC.n
                    rtr = transpose_tile(
                        ctx, ebf, lambda j0, w_, tc=tc: resF[:].rearrange(
                            "p (c k) -> p c k", k=512)[:, j0:j0+w_, 128*tc:128*(tc+1)],
                        ncol=8, pre_wait=[(vC, cpv)], bank=0)
                p.join(["T"], [vC])
                for kc in range(8):
                    w2 = [ldw1, (sC, sC.n)] if kc == 0 else []
                    p.op("T", lambda e, kc=kc: e.matmul(
                        PS(1)[0:64, 0:512], w1buf[:, 64*kc:64*(kc+1)],
                        resF[:].rearrange("p (c k) -> p c k", k=512)[:, kc, :],
                        start=(kc == 0), stop=(kc == 7)), waits=w2)
                p.grp("T", tC)
                w1v = tC.n
                p.op("S", lambda e: e.activation(
                    hW[0:64, :], PS(1)[0:64, 0:512], AF.Relu, bias=sb["b_b1"][:]),
                    waits=[(tC, w1v)], inc=sC)
                hWs = sC.n
                lastit = it == 1
                for tc in range(4):
                    t = 4 * ng + tc
                    e_sb = et[:, 1024*tc:1024*(tc+1)]
                    ldx = None
                    if lastit:
                        ldx = p.dma("Y", lambda e, b=b, t=t: e.dma_start(
                            out=scr[:, 0:1024], in_=src[b, 128*t:128*(t+1), :]),
                            waits=[(vC, vC.n)])
                    for half in range(2):
                        bank = 2 + half
                        w2 = [(sC, hWs), (gC, hwones), (vC, vC.n)]
                        p.op("T", lambda e, tc=tc, half=half, bank=bank: e.matmul(
                            PS(bank), hW[0:65, 128*tc:128*(tc+1)],
                            sb["bW2aug"][0:65, 512*half:512*(half+1)],
                            start=True, stop=True), waits=w2, inc=tC)
                        mmv = tC.n
                        if lastit:
                            p.op("V", lambda e, e_sb=e_sb, half=half, bank=bank: e.tensor_add(
                                e_sb[:, 512*half:512*(half+1)],
                                e_sb[:, 512*half:512*(half+1)], PS(bank)),
                                waits=[(tC, mmv)])
                            p.op("V", lambda e, e_sb=e_sb, half=half: e.tensor_add(
                                e_sb[:, 512*half:512*(half+1)],
                                e_sb[:, 512*half:512*(half+1)], scr[:, 512*half:512*(half+1)]),
                                waits=[ldx], inc=vC)
                        else:
                            p.op("V", lambda e, e_sb=e_sb, half=half, bank=bank: e.tensor_add(
                                e_sb[:, 512*half:512*(half+1)],
                                e_sb[:, 512*half:512*(half+1)], PS(bank)),
                                waits=[(tC, mmv)], inc=vC)
                    uv = vC.n
                    outdst = dst if lastit else es
                    ewr[t] = p.dma("Y", lambda e, b=b, t=t, e_sb=e_sb, outdst=outdst:
                                   e.dma_start(
                        out=outdst[b, 128*t:128*(t+1), :], in_=e_sb),
                        waits=[(vC, uv)])
            p.join(["T", "S", "V", "G"], [])
        p.join(["T", "S", "V", "G", "Y"], [vC, sC, tC, gC])

def build_program():
    nc = bass.Bass("TRN2", target_bir_lowering=False)
    io = {}
    def din(name, shape, dtype=F32):
        io[name] = nc.dram_tensor(name, list(shape), dtype, kind="ExternalInput")
    din("x", [B_LOC, S, D])
    for n, s_ in (("aWqT", [D, D]), ("aWkT", [D, D]), ("aWvT", [D, D]), ("aWoT", [D, D]),
                  ("a_bqk", [128, 16]), ("a_bv", [1, D]), ("a_bo", [1, D]),
                  ("bWqT", [D, 256]), ("bWkT", [D, 256]), ("bWvT", [D, D]), ("bWpT", [D, D]),
                  ("bW1T", [D, 64]), ("b_b1", [64, 1]), ("bW2aug", [65, D]),
                  ("f_wqkv", [2, D, 1536]), ("f_wo", [2, 512, D]),
                  ("f_w1p", [2, D, 22 * 256]), ("f_b1c", [2, 128, 44]),
                  ("f_w2p", [2, 22, 128, D]), ("f_b2", [2, 1, D]),
                  ("f_lng", [2, 1, D]), ("f_lnb", [2, 1, D]),
                  ("f_qs8", [2, 512]), ("f_ks8", [2, 512]), ("onesf32", [1, 128]),
                  ("maskadd", [128, 256]),
                  ("rot_qc", [S, 64]), ("rot_qs", [S, 64]), ("rot_kc", [S, 64]),
                  ("rot_ks", [S, 64])):
        din(n, s_)
    for n, s_ in (("ident", [128, 128]), ("maskP12", [128, 256]), ("onesbf", [1, 128]),
                  ("ind8", [128, 8]), ("negrow", [1, S])):
        din(n, s_, BF16)
    io["dbg"] = nc.dram_tensor("dbg", [128, 16384], BF16, kind="ExternalOutput")
    io["dbg2"] = nc.dram_tensor("dbg2", [128, 16384], BF16, kind="ExternalOutput")
    io["dbg3"] = nc.dram_tensor("dbg3", [128, 16384], BF16, kind="ExternalOutput")
    io["xs"] = nc.dram_tensor("xs", [B_LOC, S, D], F32, kind="Internal")
    io["zs"] = nc.dram_tensor("zs", [B_LOC, S, D], F32, kind="Internal")
    io["y"] = nc.dram_tensor("y", [B_LOC, S, D], F32, kind="ExternalOutput")

    p = Plan()
    C = {n: p.ctr(n) for n in ["tC", "sC", "vC", "gC", "dY", "dG"]}
    p.auto = {"V": C["vC"], "S": C["sC"], "G": C["gC"]}
    p.mkqueue("Y", 10)
    p.mkqueue("G", 8)
    mem = Mem(nc)
    sb = {}
    sb["ident"] = mem.seq("c_ident", [128, 128], BF16)
    sb["maskP12"] = mem.seq("c_maskP12", [128, 256], BF16)
    sb["maskadd"] = mem.seq("c_maskadd", [128, 256], F32)
    sb["row2bf"] = mem.seq("c_row2bf", [1, D], BF16)
    sb["onesbf"] = mem.seq("c_onesbf", [1, 128], BF16)
    sb["onesf32"] = mem.seq("c_onesf32", [1, 128], F32)
    sb["ind8"] = mem.seq("c_ind8", [128, 8], BF16)
    sb["a_bqk"] = mem.seq("c_a_bqk", [128, 16], F32)
    sb["stats"] = mem.seq("c_stats", [128, 64], F32)
    sb["b_b1"] = mem.seq("c_b_b1", [64, 1], F32)
    sb["bW2aug"] = mem.seq("c_bW2aug", [65, D], BF16)
    sb["qs8"] = mem.seq("c_qs8", [128, 512], F32)
    sb["ks8"] = mem.seq("c_ks8", [128, 512], F32)
    PHASE0 = mem.mark()

    psum = nc.alloc_psum_tensor("ps", [128, 4096], F32)
    def PS(b_, w=1):
        return psum[:, 512*b_:512*(b_+w)]

    for n in ["ident", "maskP12", "maskadd", "onesbf", "onesf32", "ind8", "a_bqk",
              "b_b1"]:
        p.dma("Y", lambda e, n=n: e.dma_start(out=sb[n][:], in_=io[n][:]))
    p.dma("G", lambda e: e.dma_start(out=sb["bW2aug"][:], in_=io["bW2aug"][:]))
    p.wait("T", p.dma_all())
    p.wait("S", p.dma_all())
    p.wait("V", p.dma_all())
    ctx = dict(nc=nc, io=io, p=p, C=C, sb=sb, mem=mem, PS=PS, PHASE0=PHASE0)

    emit_attn_a(ctx, src=io["x"], dst=(io["xs"] if PHASES > 1 else io["y"]))
    if PHASES > 1:
        emit_ffn(ctx, stream=io["xs"], dst=(io["xs"] if PHASES > 2 else io["y"]))
    if PHASES > 2:
        emit_attn_b(ctx, src=io["xs"], dst=(io["xs"] if PHASES > 3 else io["y"]))
    if PHASES > 3:
        emit_ffn(ctx, stream=io["xs"], dst=io["y"])

    p.wait("Y", p.dma_all())

    import contextlib
    stack = contextlib.ExitStack()
    for c in p.ctrs:
        c.h = stack.enter_context(nc.semaphore(c.name))
    with nc.Block() as block:
        @block.tensor
        def _(e):
            for f in p.items["T"]:
                f(e)
        @block.scalar
        def _(e):
            for f in p.items["S"]:
                f(e)
        @block.vector
        def _(e):
            for f in p.items["V"]:
                f(e)
        @block.gpsimd
        def _(e):
            for f in p.items["G"]:
                f(e)
        @block.sync
        def _(e):
            for f in p.items["Y"]:
                f(e)
    stack.close()
    nc.dbg_mem = mem
    return nc

def kernel(**inputs):
    d = prepare_host(inputs)
    x = np.asarray(inputs["L0_attn_a_x"], dtype=np.float32)
    nc = build_program()
    in_maps = []
    for c in range(N_CORES):
        m = dict(d)
        m["x"] = np.ascontiguousarray(x[2*c:2*c+2])
        in_maps.append(m)
    res = run_bass_kernel_spmd(nc, in_maps, core_ids=list(range(N_CORES)))
    return np.concatenate([r["y"] for r in res.results], axis=0)


# version 09 (final)
# speedup vs baseline: 1.1775x; 1.0173x over previous
import numpy as np
import ml_dtypes
import concourse.bass as bass
import concourse.mybir as mybir
from concourse.bass_utils import run_bass_kernel_spmd

F32 = mybir.dt.float32
BF16 = mybir.dt.bfloat16
AF = mybir.ActivationFunctionType
ALU = mybir.AluOpType
AX = mybir.AxisListType
bfloat16 = ml_dtypes.bfloat16

N_CORES = 8
B_LOC = 2
S = 2048
D = 1024
NT = S // 128
PHASES = 4
DBG_TAP = None
FFN_DBG_STOP = None

class Ctr:
    def __init__(self, name):
        self.name, self.n, self.h = name, 0, None

class Plan:
    def __init__(self):
        self.items = {k: [] for k in "TSVGY"}
        self.ctrs = []
        self.queues = {}
    def ctr(self, name):
        c = Ctr(name); self.ctrs.append(c); return c
    def mkqueue(self, eng, n=8):
        self.queues[eng] = [self.ctr(f"d{eng}{i}") for i in range(n)], [0]
    def dma(self, eng, fn, waits=()):
        cs, kbox = self.queues[eng]
        c = cs[kbox[0] % len(cs)]
        kbox[0] += 1
        w = list(waits)
        if c.n > 0:
            w.append((c, c.n))
        self.op(eng, fn, waits=w, inc=c, by=16, _dma=True)
        return (c, c.n)
    def dma_all(self):
        toks = []
        for cs, _ in self.queues.values():
            toks += [(c, c.n) for c in cs if c.n > 0]
        return toks
    def op(self, eng, fn, waits=(), inc=None, by=1, _dma=False):
        if eng in ("V", "S", "G") and not _dma:
            c = self.auto[eng]
            waits = list(waits) + [(c, c.n)]
            if inc is None:
                inc, by = c, 1
            else:
                assert inc is c, f"{eng} op with foreign inc"
        for (c, v) in waits:
            assert v <= c.n, f"wait on future: {c.name} {v} > {c.n}"
        waits = tuple(waits)
        if inc is not None:
            inc.n += by
        def run(e, fn=fn, waits=waits, inc=inc, by=by):
            for (c, v) in waits:
                if v > 0:
                    e.wait_ge(c.h, v)
            ins = fn(e)
            if inc is not None:
                ins.then_inc(inc.h, by)
            return ins
        self.items[eng].append(run)
        return inc.n if inc is not None else None
    def grp(self, eng, ctr, by=1):
        ctr.n += by
        prev_run = self.items[eng].pop()
        def run2(e, prev_run=prev_run, ctr=ctr, by=by):
            ins = prev_run(e)
            ins.then_inc(ctr.h, by)
            return ins
        self.items[eng].append(run2)
        return ctr.n
    def wait(self, eng, waits):
        for (c, v) in waits:
            assert v <= c.n, f"wait on future: {c.name} {v} > {c.n}"
        waits = tuple(waits)
        def run(e, waits=waits):
            for (c, v) in waits:
                if v > 0:
                    e.wait_ge(c.h, v)
        self.items[eng].append(run)
    def join(self, engs, ctrs, dmas=True):
        for eng in engs:
            w = [(c, c.n) for c in ctrs]
            if dmas:
                w += self.dma_all()
            self.wait(eng, w)

class Mem:
    def __init__(self, nc):
        self.size = (nc.sbuf_bytes_remaining - 4096) & ~63
        self.nc = nc
        self.handle = nc.alloc_sbuf_tensor("pool", [128, self.size], mybir.dt.uint8)
        self.pap = self.handle.ap() if hasattr(self.handle, "ap") else self.handle[:]
        self.cur = 0
        self.reg = {}
        self.offs = {}
    def at(self, name, shape, dtype, off):
        nbytes = int(np.prod(shape[1:])) * mybir.dt.size(dtype)
        assert off + nbytes <= self.size, f"{name}: {off}+{nbytes} > {self.size}"
        ap = self.pap[:, off:off + nbytes].bitcast(dtype)
        if shape[0] != 128:
            ap = ap[0:shape[0], :]
        self.reg[name] = ap
        self.offs[name] = (off, tuple(shape), dtype)
        return ap
    def seq(self, name, shape, dtype):
        off = (self.cur + 63) & ~63
        t = self.at(name, shape, dtype, off)
        self.cur = off + int(np.prod(shape[1:])) * mybir.dt.size(dtype)
        return t
    def mark(self):
        return self.cur
    def reset(self, mark):
        self.cur = mark

def _rotary_tables():
    DIM_HEAD, WINDOW = 64, 512
    half_idx = np.arange(0, DIM_HEAD, 2, dtype=np.float64)
    inv_freq = 1.0 / 10000.0 ** (half_idx / DIM_HEAD)
    t = np.arange(S, dtype=np.float64)
    freqs = t[:, None] * inv_freq[None, :]
    freqs = np.concatenate([freqs, freqs], axis=-1)
    base = (half_idx + 0.4 * DIM_HEAD) / (1.4 * DIM_HEAD)
    power = (t - S // 2) / float(WINDOW // 2)
    scale = base[None, :] ** power[:, None]
    scale = np.concatenate([scale, scale], axis=-1)
    cos, sin = np.cos(freqs), np.sin(freqs)
    return ((cos * scale).astype(np.float32), (sin * scale).astype(np.float32),
            (cos / scale).astype(np.float32), (sin / scale).astype(np.float32))

def prepare_host(inp):
    f32 = np.float32
    g = lambda k: np.asarray(inp[k])
    d = {}
    d["aWqT"] = np.ascontiguousarray(g("L0_attn_a_Wq").T.astype(f32))
    d["aWkT"] = np.ascontiguousarray(g("L0_attn_a_Wk").T.astype(f32))
    d["aWvT"] = np.ascontiguousarray(g("L0_attn_a_Wv").T.astype(f32))
    d["aWoT"] = np.ascontiguousarray(g("L0_attn_a_Wo").T.astype(f32))
    a_bqk = np.zeros((128, 16), f32)
    a_bqk[:, 0::2] = g("L0_attn_a_bq").astype(f32).reshape(8, 128).T
    a_bqk[:, 1::2] = g("L0_attn_a_bk").astype(f32).reshape(8, 128).T
    d["a_bqk"] = a_bqk
    d["a_bv"] = g("L0_attn_a_bv").astype(f32).reshape(1, D)
    d["a_bo"] = g("L0_attn_a_bo").astype(f32).reshape(1, D)
    d["bWqT"] = np.ascontiguousarray(g("L0_attn_b_Wq").T.astype(f32))
    d["bWkT"] = np.ascontiguousarray(g("L0_attn_b_Wk").T.astype(f32))
    d["bWvT"] = np.ascontiguousarray(g("L0_attn_b_Wv").T.astype(f32))
    d["bWpT"] = np.ascontiguousarray(g("L0_attn_b_Wp").T.astype(f32))
    d["bW1T"] = np.ascontiguousarray(g("L0_attn_b_W1").T.astype(f32))
    d["b_b1"] = g("L0_attn_b_b1").astype(f32).reshape(64, 1)
    d["bW2aug"] = np.ascontiguousarray(np.concatenate(
        [g("L0_attn_b_W2").T.astype(f32), g("L0_attn_b_b2").astype(f32).reshape(1, D)], 0))
    HID, NHC = 2730, 22
    d["f_wqkv"] = np.ascontiguousarray(g("L0_ffn_wqkv").astype(f32))
    d["f_wo"] = np.ascontiguousarray(g("L0_ffn_wo").astype(f32))
    w1 = g("L0_ffn_w1").astype(f32)
    w1p = np.zeros((2, D, NHC, 256), f32)
    for c in range(NHC):
        w = min(128, HID - c * 128)
        w1p[:, :, c, 0:w] = w1[:, :, c*128:c*128+w]
        w1p[:, :, c, 128:128+w] = w1[:, :, HID + c*128:HID + c*128+w]
    d["f_w1p"] = np.ascontiguousarray(w1p.reshape(2, D, NHC * 256))
    b1 = g("L0_ffn_b1").astype(f32)
    b1c = np.zeros((2, 128, 2 * NHC), f32)
    for c in range(NHC):
        w = min(128, HID - c * 128)
        b1c[:, 0:w, 2*c] = b1[:, c*128:c*128+w]
        b1c[:, 0:w, 2*c+1] = b1[:, HID + c*128:HID + c*128+w]
    d["f_b1c"] = np.ascontiguousarray(b1c)
    w2 = g("L0_ffn_w2").astype(f32)
    w2p = np.zeros((2, NHC, 128, D), f32)
    for c in range(NHC):
        w = min(128, HID - c * 128)
        w2p[:, c, 0:w, :] = w2[:, c*128:c*128+w, :]
    d["f_w2p"] = np.ascontiguousarray(w2p)
    d["f_b2"] = g("L0_ffn_b2").astype(f32).reshape(2, 1, D)
    d["f_lng"] = g("L0_ffn_ln_g").astype(f32).reshape(2, 1, D)
    d["f_lnb"] = g("L0_ffn_ln_b").astype(f32).reshape(2, 1, D)
    d["f_qs8"] = np.ascontiguousarray(np.tile(g("L0_ffn_q_scale").astype(f32), (1, 8)))
    d["f_ks8"] = np.ascontiguousarray(np.tile(g("L0_ffn_k_scale").astype(f32), (1, 8)))
    d["ident"] = np.eye(128, dtype=bfloat16)
    r = np.arange(128)
    d["maskP12"] = np.concatenate(
        [(r[:, None] <= r[None, :]).astype(bfloat16),
         (r[:, None] >= r[None, :]).astype(bfloat16)], axis=1)
    d["onesbf"] = np.ones((1, 128), dtype=bfloat16)
    r2 = np.arange(128)
    maskadd = np.zeros((128, 256), np.float32)
    maskadd[:, 0:128][r2[:, None] > r2[None, :]] = -1e30
    maskadd[:, 128:256][r2[:, None] < r2[None, :]] = -1e30
    d["maskadd"] = maskadda = maskadd
    d["onesf32"] = np.ones((1, 128), dtype=np.float32)
    d["negrow"] = np.full((1, S), -1.0, dtype=bfloat16)
    ind8 = np.zeros((128, 8), dtype=bfloat16)
    for rr in range(128):
        ind8[rr, rr // 16] = 1
    d["ind8"] = ind8
    d["rot_qc"], d["rot_qs"], d["rot_kc"], d["rot_ks"] = _rotary_tables()
    return d

def ln_tile(ctx, x_sb, outs, slot, scratch, pre_waits=()):
    p, C, sb = ctx["p"], ctx["C"], ctx["sb"]
    st = sb["stats"]
    o = slot * 8
    ssum = st[:, o:o+1]; ssq = st[:, o+1:o+2]; mu = st[:, o+2:o+3]
    m2 = st[:, o+3:o+4]; var = st[:, o+4:o+5]; istd = st[:, o+5:o+6]
    nb = st[:, o+6:o+7]; lnv = st[:, o+7:o+8]
    vC, sC = C["vC"], C["sC"]
    p.op("V", lambda e: e.reduce_sum(ssum, x_sb, axis=AX.X), waits=list(pre_waits), inc=vC)
    p.op("S", lambda e: e.activation(scratch, x_sb, AF.Square, accum_out=ssq),
         waits=list(pre_waits), inc=sC)
    svv = sC.n
    p.op("V", lambda e: e.tensor_scalar_mul(mu, ssum, 1.0 / D))
    p.op("V", lambda e: e.tensor_scalar_mul(m2, ssq, 1.0 / D), waits=[(sC, svv)])
    p.op("V", lambda e: e.tensor_mul(var, mu, mu))
    p.op("V", lambda e: e.tensor_sub(var, m2, var))
    p.op("V", lambda e: e.tensor_scalar_add(var, var, 1e-5), inc=vC)
    vstat = vC.n
    p.op("S", lambda e: e.activation(lnv, var, AF.Ln), waits=[(vC, vstat)])
    p.op("S", lambda e: e.activation(istd, lnv, AF.Exp, scale=-0.5), inc=sC)
    sistd = sC.n
    p.op("V", lambda e: e.tensor_mul(nb, mu, istd), waits=[(sC, sistd)])
    p.op("V", lambda e: e.tensor_scalar_mul(nb, nb, -1.0), inc=vC)
    vnb = vC.n
    for ap in outs:
        p.op("V", lambda e, ap=ap: e.tensor_scalar(
            ap, x_sb, istd, nb, ALU.mult, ALU.add),
            waits=[(vC, vnb), (sC, sistd)], inc=vC)
    return sC.n, vC.n

def transpose_tile(ctx, src_bf, dst_fn, ncol, pre_wait=(), bank=7):
    p, C, sb, PS = ctx["p"], ctx["C"], ctx["sb"], ctx["PS"]
    tC, vC = C["tC"], C["vC"]
    done = 0
    j0 = 0
    while j0 < ncol:
        w = min(4, ncol - j0)
        for j in range(w):
            jj = j0 + j
            wl = list(pre_wait) if jj == 0 else []
            if j == 0:
                wl.append((vC, vC.n))
            p.op("T", lambda e, jj=jj, j=j: e.matmul(
                PS(bank)[:, 128*j:128*(j+1)], src_bf[:, 128*jj:128*(jj+1)],
                sb["ident"][:], start=True, stop=True), waits=wl)
        p.grp("T", tC)
        tv = tC.n
        p.op("V", lambda e, j0=j0, w=w, tv=tv: e.tensor_copy(
            dst_fn(j0, w), PS(bank)[:, 0:128*w].rearrange("p (c k) -> p c k", k=128)),
            waits=[(tC, tv)], inc=vC)
        done = tC.n
        j0 += w
    return done

def emit_attn_a(ctx, src, dst):
    nc, io, p, C, sb, mem, PS = (ctx[k] for k in ["nc", "io", "p", "C", "sb", "mem", "PS"])
    dY, dG, tC, sC, vC, gC = (C[k] for k in ["dY", "dG", "tC", "sC", "vC", "gC"])
    mem.reset(ctx["PHASE0"])
    yF = mem.seq("a_yF", [128, 8 * 2048], BF16)
    QF = mem.seq("a_QF", [128, 8 * 2048], BF16)
    KF = mem.seq("a_KF", [128, 8 * 2048], BF16)
    VT = mem.seq("a_VT", [128, 16 * 1024], BF16)
    wqk = mem.seq("a_wqk", [128, 2 * 1024], BF16)
    wvo = mem.seq("a_wvo", [128, 8 * 1024], BF16)
    xt = mem.seq("a_xt", [128, 4 * 1024], F32)
    ybf = mem.seq("a_ybf", [128, 2 * 1024], BF16)
    sinS = mem.seq("a_sinS", [128, 2 * 1024], BF16)
    scr = mem.seq("a_scr", [128, 1024], F32)
    res = mem.seq("a_res", [128, 2 * 1024], F32)
    rowbf0 = mem.seq("a_rowbf0", [1, D], BF16)
    rowbf1 = mem.seq("a_rowbf1", [1, D], BF16)
    sb["rowbf0"], sb["rowbf1"] = rowbf0, rowbf1
    yFr = lambda: yF[:].rearrange("p (c k) -> p c k", k=2048)

    r0 = p.dma("G", lambda e: e.dma_start(out=sb["rowbf0"][:], in_=io["a_bv"][:]))
    r1 = p.dma("G", lambda e: e.dma_start(out=sb["rowbf1"][:], in_=io["a_bo"][:]))
    ldrows = [r0, r1]

    for b in range(B_LOC):
        ln_s, tr_done = {}, {}
        for t in range(NT):
            x_sb = xt[:, 1024*(t % 4):1024*(t % 4 + 1)]
            w = [(sC, ln_s[t-4][0]), (vC, ln_s[t-4][1])] if t >= 4 else []
            ld = p.dma("Y", lambda e, b=b, t=t, x_sb=x_sb: e.dma_start(
                out=x_sb, in_=src[b, 128*t:128*(t+1), :]), waits=w)
            out_bf = ybf[:, 1024*(t % 2):1024*(t % 2 + 1)]
            pw = [ld]
            if t >= 2:
                pw.append((tC, tr_done[t-2]))
            sN, vN = ln_tile(ctx, x_sb, [out_bf], slot=t % 8, scratch=scr[:], pre_waits=pw)
            ln_s[t] = (sN, vN)
            if DBG_TAP == "attn_a" and b == 0:
                p.dma("Y", lambda e, t=t, out_bf=out_bf: e.dma_start(
                    out=io["dbg2"][:, 1024*t:1024*(t+1)], in_=out_bf), waits=[(vC, vN)])
            tr_done[t] = transpose_tile(
                ctx, out_bf, lambda j0, w_, t=t: yFr()[:, j0:j0+w_, 128*t:128*(t+1)],
                ncol=8, pre_wait=[(vC, vN)])
        p.join(["T", "S", "G"], [vC])
        if DBG_TAP == "attn_a" and b == 0:
            p.wait("Y", [(vC, vC.n)])
            p.dma("Y", lambda e: e.dma_start(out=io["dbg"][:], in_=yF[:]))


        wq_hist = []
        for wi, (Wn, OUT, bw) in enumerate((("aWqT", QF, 0), ("aWkT", KF, 1))):
            for oc in range(8):
                idx = wi * 8 + oc
                wb = wqk[:, 1024*(idx % 2):1024*(idx % 2 + 1)]
                w = [(tC, wq_hist[idx-2])] if idx >= 2 else []
                ldw = p.dma("G", lambda e, Wn=Wn, oc=oc, wb=wb: e.dma_start(
                    out=wb.rearrange("p (c k) -> p c k", k=128),
                    in_=io[Wn][:, 128*oc:128*(oc+1)].rearrange("(c p) k -> p c k", p=128)),
                    waits=w)
                for tg in range(4):
                    bank = tg % 2
                    for kc in range(8):
                        w2 = [ldw, (vC, vC.n)] if kc == 0 else []
                        p.op("T", lambda e, wb=wb, tg=tg, kc=kc, bank=bank: e.matmul(
                            PS(bank), wb[:, 128*kc:128*(kc+1)],
                            yFr()[:, kc, 512*tg:512*(tg+1)],
                            start=(kc == 0), stop=(kc == 7)), waits=w2)
                    p.grp("T", tC)
                    mmv = tC.n
                    p.op("V", lambda e, OUT=OUT, oc=oc, tg=tg, bank=bank, bw=bw:
                         e.tensor_scalar_add(
                            OUT[:].rearrange("p (c k) -> p c k", k=2048)[:, oc,
                                                                        512*tg:512*(tg+1)],
                            PS(bank), sb["a_bqk"][:, 2*oc+bw:2*oc+bw+1]),
                        waits=[(tC, mmv)], inc=vC)
                wq_hist.append(tC.n)
        p.join(["T"], [vC])


        ldv = p.dma("G", lambda e: e.dma_start(
            out=wvo[:].rearrange("p (c k) -> p c k", k=1024),
            in_=io["aWvT"][:].rearrange("(c p) k -> p c k", p=128)))
        for t in range(NT):
            for half in range(2):
                bank = (2 * t + half) % 2
                for kc in range(8):
                    w2 = [ldv, (vC, vC.n)] if kc == 0 else []
                    p.op("T", lambda e, t=t, half=half, kc=kc, bank=bank: e.matmul(
                        PS(bank), yFr()[:, kc, 128*t:128*(t+1)],
                        wvo[:, 1024*kc + 512*half: 1024*kc + 512*(half+1)],
                        start=(kc == 0), stop=False), waits=w2)
                p.op("T", lambda e, half=half, bank=bank: e.matmul(
                    PS(bank), sb["onesbf"][:, 0:128], sb["rowbf0"][:, 512*half:512*(half+1)],
                    start=False, stop=True),
                    waits=(ldrows if (t == 0 and half == 0) else []))
                p.grp("T", tC)
                mmv = tC.n
                p.op("V", lambda e, t=t, half=half, bank=bank: e.tensor_copy(
                    VT[:, 1024*t + 512*half: 1024*t + 512*(half+1)], PS(bank)),
                    waits=[(tC, mmv)], inc=vC)
        p.join(["T"], [vC])
        if DBG_TAP == "attn_a" and b == 0:
            p.wait("Y", [(vC, vC.n)])
            p.dma("Y", lambda e: e.dma_start(out=io["dbg3"][:], in_=VT[:]))

        ldwo = p.dma("G", lambda e: e.dma_start(
            out=wvo[:].rearrange("p (c k) -> p c k", k=1024),
            in_=io["aWoT"][:].rearrange("(c p) k -> p c k", p=128)),
            waits=[(vC, vC.n)])
        sin_done, av_done = {}, {}
        for hp in range(8):
            for qg in range(4):
                for kc in range(16):
                    par = kc % 2
                    for which in range(2):
                        bank = 2 * par + which
                        base = 64 * which
                        w = ([(sC, sin_done[(hp, qg, kc - 2)])]
                             if (hp, qg, kc - 2) in sin_done else [])
                        if kc == 0 and which == 0 and (hp, qg) == (0, 0):
                            w = [(sC, sC.n)]
                        p.op("T", lambda e, hp=hp, qg=qg, kc=kc, bank=bank, base=base: e.matmul(
                            PS(bank),
                            KF[base:base+64, 2048*hp + 128*kc: 2048*hp + 128*(kc+1)],
                            QF[base:base+64, 2048*hp + 512*qg: 2048*hp + 512*(qg+1)],
                            start=True, stop=True), waits=w)
                    p.grp("T", tC)
                    smm = tC.n
                    slot = par
                    w = [(tC, smm)]
                    if (hp, qg, kc - 2) in av_done:
                        w.append((tC, av_done[(hp, qg, kc - 2)]))
                    elif kc < 2 and (hp, qg) != (0, 0):
                        w.append((tC, av_prev))
                    p.op("S", lambda e, par=par, slot=slot: e.activation(
                        sinS[:, 1024*slot:1024*(slot+1)], PS(2*par, 2), AF.Sin, scale=3.75),
                        waits=w, inc=sC)
                    sin_done[(hp, qg, kc)] = sC.n
                    for which in range(2):
                        w = [(sC, sin_done[(hp, qg, kc)])] if which == 0 else []
                        if kc == 0 and which == 0:
                            w.append((vC, vC.n))
                        p.op("T", lambda e, hp=hp, kc=kc, which=which, slot=slot: e.matmul(
                            PS(4)[64*which:64*(which+1), :],
                            VT[:, 1024*kc + 128*hp + 64*which: 1024*kc + 128*hp + 64*(which+1)],
                            sinS[:, 1024*slot + 512*which: 1024*slot + 512*(which+1)],
                            start=(kc == 0), stop=(kc == 15)), waits=w)
                    p.grp("T", tC)
                    av_done[(hp, qg, kc)] = tC.n
                av_prev = av_done[(hp, qg, 15)]
                p.op("V", lambda e, hp=hp, qg=qg, v=av_prev: e.tensor_copy(
                    yF[:, 2048*hp + 512*qg: 2048*hp + 512*(qg+1)], PS(4)),
                    waits=[(tC, av_prev)], inc=vC)
        p.join(["T"], [vC])

        add_done = 0
        st_tok = {}
        for t in range(NT):
            x_sb = xt[:, 1024*(t % 2):1024*(t % 2 + 1)]
            w2 = [(vC, add_done)] if t >= 2 else [(vC, vC.n), (sC, sC.n)]
            ldx = p.dma("Y", lambda e, b=b, t=t, x_sb=x_sb: e.dma_start(
                out=x_sb, in_=src[b, 128*t:128*(t+1), :]), waits=w2)
            rslot = t % 2
            for half in range(2):
                bank = 5 + half
                for kc in range(8):
                    w3 = [(vC, vC.n)] if kc == 0 else []
                    if kc == 0 and t == 0 and half == 0:
                        w3.append(ldwo)
                    p.op("T", lambda e, t=t, kc=kc, half=half, bank=bank: e.matmul(
                        PS(bank), yF[:, 2048*kc + 128*t: 2048*kc + 128*(t+1)],
                        wvo[:, 1024*kc + 512*half: 1024*kc + 512*(half+1)],
                        start=(kc == 0), stop=False), waits=w3)
                p.op("T", lambda e, half=half, bank=bank: e.matmul(
                    PS(bank), sb["onesbf"][:, 0:128], sb["rowbf1"][:, 512*half:512*(half+1)],
                    start=False, stop=True))
                p.grp("T", tC)
                mmv = tC.n
                wv = [(tC, mmv), ldx]
                if half == 0 and t >= 2:
                    wv.append(st_tok[t-2])
                p.op("V", lambda e, rslot=rslot, half=half, bank=bank, x_sb=x_sb: e.tensor_add(
                    res[:, 1024*rslot + 512*half: 1024*rslot + 512*(half+1)],
                    x_sb[:, 512*half:512*(half+1)], PS(bank)),
                    waits=wv, inc=vC)
            add_done = vC.n
            st_tok[t] = p.dma("Y", lambda e, b=b, t=t, rslot=rslot: e.dma_start(
                out=dst[b, 128*t:128*(t+1), :], in_=res[:, 1024*rslot:1024*(rslot+1)]),
                waits=[(vC, add_done)])
        p.join(["T", "S", "V", "G", "Y"], [vC, sC, tC])

def emit_ffn(ctx, stream, dst):
    nc, io, p, C, sb, mem, PS = (ctx[k] for k in ["nc", "io", "p", "C", "sb", "mem", "PS"])
    dY, dG, tC, sC, vC, gC = (C[k] for k in ["dY", "dG", "tC", "sC", "vC", "gC"])
    zs = io["zs"]
    NHC = 22
    CLIP = [(0, 128), (0, 256), (0, 384), (0, 512), (0, 512), (128, 512), (256, 512), (384, 512)]

    mem.reset(ctx["PHASE0"])
    hF = mem.seq("f_hF", [128, 8 * 1024], BF16)
    hF_off = mem.mark() - 8 * 1024 * 2
    qpk = mem.seq("f_qpk", [128, 4 * 2048], BF16)
    kpk = mem.seq("f_kpk", [128, 4 * 2048], BF16)
    kpk_off = mem.mark() - 4 * 2048 * 2
    vaug = mem.seq("f_vaug", [128, 16 * 520], BF16)
    wbuf = mem.seq("f_wbuf", [128, 8 * 1536], BF16)
    tabs = mem.seq("f_tabs", [128, 4 * 1024], F32)
    pexp = mem.seq("f_pexp", [128, 2 * 640], BF16)
    ptb = mem.seq("f_ptb", [128, 2 * 640], BF16)
    expSoff_ = mem.mark() - 4 * 640 * 2
    hbf = mem.seq("f_hbf", [128, 2 * 1024], BF16)
    hbf_off = mem.mark() - 2 * 1024 * 2
    hf32 = mem.seq("f_hf32", [128, 3 * 1024], F32)
    hf32_off = mem.mark() - 3 * 1024 * 4
    qkT = mem.seq("f_qkT", [128, 4 * 512], F32)
    qkT_off = mem.mark() - 4 * 512 * 4
    qkbf = mem.seq("f_qkbf", [128, 512], BF16)
    qkbf_off = mem.mark() - 512 * 2
    zupd = mem.seq("f_zupd", [128, 2 * 1024], F32)
    zupd_off = mem.mark() - 2 * 1024 * 4
    scr = mem.seq("f_scr", [128, 1024], F32)
    nrm = mem.seq("f_nrm", [128, 128], F32)
    oF = mem.at("f_oF", [128, 4 * 2048], BF16, hF_off)
    hlnF = mem.at("fB_hlnF", [128, 8 * 2048], BF16, hF_off)
    ohalf = mem.at("fB_ohalf", [128, 11 * 2048], BF16, kpk_off)
    w2buf = mem.at("fB_w2buf", [128, 11 * 1024], BF16, kpk_off + 45056)
    w1bufB = mem.at("fB_w1buf", [128, 2 * 2048], BF16, kpk_off + 45056 + 22528)
    gelt = mem.at("fB_gelt", [128, 512], F32, kpk_off + 45056 + 22528 + 8192)
    hlnbf = mem.at("fB_hlnbf", [128, 1024], BF16, kpk_off + 45056 + 22528 + 10240)

    lngB = mem.at("fB_lng", [128, D], F32, qkT_off)
    lnbB = mem.at("fB_lnb", [128, D], F32, qkT_off + 4096)
    b1cB = mem.at("fB_b1c", [128, 44], F32, qkbf_off)
    row2bfB = sb["row2bf"]
    assert kpk_off + 45056 + 22528 + 10240 + 2048 <= mem.offs["f_hf32"][0], "B alias overlap"
    hFr = lambda: hF[:].rearrange("p (c k) -> p c k", k=1024)
    hlnFr = lambda: hlnF[:].rearrange("p (c k) -> p c k", k=2048)
    tbr = lambda ti: tabs[:, 1024*ti:1024*(ti+1)].rearrange("p (t k) -> p t k", k=64)

    final_phase = dst is not zs and dst is not None

    for b in range(B_LOC):
        zwr, ln_s = {}, {}
        for t in range(NT):
            x_sb = hf32[:, 1024*(t % 3):1024*(t % 3 + 1)]
            w = [(sC, ln_s[t-3][0]), (vC, ln_s[t-3][1])] if t >= 3 else []
            ld = p.dma("Y", lambda e, b=b, t=t, x_sb=x_sb: e.dma_start(
                out=x_sb, in_=stream[b, 128*t:128*(t+1), :]), waits=w)
            z_sb = zupd[:, 1024*(t % 2):1024*(t % 2 + 1)]
            pw = [ld]
            if t >= 2:
                pw.append(zwr[t-2])
            sN, vN = ln_tile(ctx, x_sb, [z_sb], slot=t % 8, scratch=scr[:], pre_waits=pw)
            ln_s[t] = (sN, vN)
            zwr[t] = p.dma("Y", lambda e, b=b, t=t, z_sb=z_sb: e.dma_start(
                out=zs[b, 128*t:128*(t+1), :], in_=z_sb), waits=[(vC, vN)])

        for layer in range(2):
            final = final_phase and layer == 1
            tabtoks = []
            tabtoks.append(p.dma("Y", lambda e, layer=layer: e.dma_start(
                out=sb["qs8"][:], in_=io["f_qs8"][layer:layer+1, :].broadcast_to([128, 512])),
                waits=[(vC, vC.n)]))
            tabtoks.append(p.dma("Y", lambda e, layer=layer: e.dma_start(
                out=sb["ks8"][:], in_=io["f_ks8"][layer:layer+1, :].broadcast_to([128, 512]))))
            for (ti, rawname) in ((0, "rot_qc"), (1, "rot_qs"), (2, "rot_kc"), (3, "rot_ks")):
                tabtoks.append(p.dma("Y", lambda e, rawname=rawname, ti=ti: e.dma_start(
                    out=tbr(ti), in_=io[rawname][:].rearrange("(t p) k -> p t k", p=128)),
                    waits=[(vC, vC.n)] if ti == 0 else []))
            for ti, sc in ((0, "qs8"), (2, "ks8")):
                p.op("V", lambda e, ti=ti, sc=sc: e.tensor_tensor(
                    tbr(ti), tbr(ti),
                    sb[sc][:, 0:64].unsqueeze(1).broadcast_to([128, 16, 64]), ALU.mult),
                    waits=tabtoks, inc=vC)
            for ti, sc in ((1, "qs8"), (3, "ks8")):
                p.op("V", lambda e, ti=ti, sc=sc: e.tensor_tensor(
                    tbr(ti)[:, :, 0:32], tbr(ti)[:, :, 0:32],
                    sb[sc][:, 32:64].unsqueeze(1).broadcast_to([128, 16, 32]), ALU.mult),
                    waits=tabtoks)
                p.op("V", lambda e, ti=ti, sc=sc: e.tensor_tensor(
                    tbr(ti)[:, :, 32:64], tbr(ti)[:, :, 32:64],
                    sb[sc][:, 0:32].unsqueeze(1).broadcast_to([128, 16, 32]), ALU.mult),
                    inc=vC)
            tab_done = vC.n
            ld_wqkv = p.dma("G", lambda e, layer=layer: e.dma_start(
                out=wbuf[:].rearrange("p (c k) -> p c k", k=1536),
                in_=io["f_wqkv"][layer].rearrange("(c p) k -> p c k", p=128)),
                waits=[(tC, tC.n)])
            for t in range(NT):
                p.op("G", lambda e, t=t: e.memset(
                    vaug[:, 520*t:520*(t+1)].rearrange("p (g k) -> p g k", k=65)[:, :, 64:65],
                    1.0), inc=gC)
            neg_done = gC.n

            tr_done = {}
            for half in range(2):
                for tl in range(8):
                    t = 8 * half + tl
                    h_sb = hbf[:, 1024*(t % 2):1024*(t % 2 + 1)]
                    w = [(tC, tr_done[t-2])] if t >= 2 else [(vC, tab_done)]
                    w += [zwr[t], zwr[max(t-1, 0)]]
                    ldh1 = p.dma("G", lambda e, b=b, t=t, h_sb=h_sb: e.dma_start(
                        out=h_sb[:, 0:512], in_=zs[b, 128*t:128*(t+1), 0:512]), waits=w)
                    if t == 0:
                        p.op("G", lambda e, h_sb=h_sb: e.memset(h_sb[0:1, 512:1024], 0.0),
                             inc=gC)
                        ldh2 = p.dma("G", lambda e, b=b, h_sb=h_sb: e.dma_start(
                            out=h_sb[1:128, 512:1024], in_=zs[b, 0:127, 512:1024]))
                    else:
                        ldh2 = p.dma("G", lambda e, b=b, t=t, h_sb=h_sb: e.dma_start(
                            out=h_sb[:, 512:1024], in_=zs[b, 128*t-1:128*(t+1)-1, 512:1024]))
                    tr_done[t] = transpose_tile(
                        ctx, h_sb, lambda j0, w_, tl=tl: hFr()[:, j0:j0+w_, 128*tl:128*(tl+1)],
                        ncol=8, pre_wait=[ldh1, ldh2, (gC, gC.n)])
                if FFN_DBG_STOP == f"A1h{half}":
                    p.join(["T", "S", "V", "G", "Y"], [tC, sC, vC, gC])
                    return
                for tl in range(8):
                    t = 8 * half + tl
                    for pi in range(3):
                        bank = pi
                        for kc in range(8):
                            w2 = []
                            if kc == 0:
                                w2 = [(vC, vC.n), (sC, sC.n)]
                                if tl == 0 and pi == 0:
                                    w2 += [ld_wqkv, (tC, tr_done[8*half+7])]
                            p.op("T", lambda e, tl=tl, pi=pi, kc=kc, bank=bank: e.matmul(
                                PS(bank), hFr()[:, kc, 128*tl:128*(tl+1)],
                                wbuf[:, 1536*kc + 512*pi: 1536*kc + 512*(pi+1)],
                                start=(kc == 0), stop=(kc == 7)), waits=w2)
                        p.grp("T", tC)
                        mmv = tC.n
                        if pi == 2:
                            p.op("V", lambda e, t=t, bank=bank: e.tensor_copy(
                                vaug[:, 520*t:520*(t+1)].rearrange(
                                    "p (g k) -> p g k", k=65)[:, :, 0:64],
                                PS(bank).rearrange("p (g k) -> p g k", k=64)),
                                waits=[(tC, mmv)], inc=vC)
                        else:
                            p.op("V", lambda e, pi=pi, bank=bank: e.tensor_copy(
                                qkT[:, 512*pi:512*(pi+1)], PS(bank)),
                                waits=[(tC, mmv)], inc=vC)
                    qk_ready = vC.n
                    if FFN_DBG_STOP == f"A2h{half}t{tl}":
                        p.join(["T", "S", "V", "G", "Y"], [tC, sC, vC, gC])
                        return
                    for (pi, TA, TB, dstc) in ((0, 0, 1, qpk), (1, 2, 3, kpk)):
                        src3 = qkT[:, 512*pi:512*(pi+1)].rearrange("p (h k) -> p h k", k=64)
                        u3 = qkT[:, 1024:1536].rearrange("p (h k) -> p h k", k=64)
                        w3 = qkT[:, 1536:2048].rearrange("p (h k) -> p h k", k=64)
                        ss = nrm[:, (t % 4) * 16 + 8 * pi:(t % 4) * 16 + 8 * pi + 8]
                        p.op("S", lambda e, pi=pi: e.activation(
                            scr[:, 0:512], qkT[:, 512*pi:512*(pi+1)], AF.Square),
                            waits=[(vC, qk_ready)], inc=sC)
                        sqv = sC.n
                        p.op("V", lambda e, ss=ss: e.reduce_sum(
                            ss, scr[:, 0:512].rearrange("p (h k) -> p h k", k=64), axis=AX.X),
                            waits=[(sC, sqv)])
                        p.op("V", lambda e, ss=ss: e.tensor_scalar_max(ss, ss, 1e-24), inc=vC)
                        ssv = vC.n
                        p.op("S", lambda e, ss=ss: e.activation(ss, ss, AF.Ln),
                             waits=[(vC, ssv)])
                        p.op("S", lambda e, ss=ss: e.activation(
                            ss, ss, AF.Exp, scale=-0.5), inc=sC)
                        sexp = sC.n
                        if pi == 0:
                            p.op("V", lambda e, ss=ss: e.tensor_scalar_mul(ss, ss, 8.0),
                                 waits=[(sC, sexp)], inc=vC)
                            inv_ctr, inv_done = vC, vC.n
                        else:
                            inv_ctr, inv_done = sC, sexp
                        TAv = tbr(TA)[:, t % 16, :]
                        TBv = tbr(TB)[:, t % 16, :]
                        p.op("V", lambda e, u3=u3, src3=src3, TAv=TAv: e.tensor_tensor(
                            u3, src3, TAv.unsqueeze(1).broadcast_to([128, 8, 64]), ALU.mult))
                        p.op("V", lambda e, w3=w3, src3=src3, TBv=TBv: e.tensor_tensor(
                            w3[:, :, 0:32], src3[:, :, 32:64],
                            TBv[:, 0:32].unsqueeze(1).broadcast_to([128, 8, 32]), ALU.mult))
                        p.op("V", lambda e, w3=w3, src3=src3, TBv=TBv: e.tensor_tensor(
                            w3[:, :, 32:64], src3[:, :, 0:32],
                            TBv[:, 32:64].unsqueeze(1).broadcast_to([128, 8, 32]), ALU.mult))
                        p.op("V", lambda e, u3=u3, w3=w3: e.tensor_sub(
                            u3[:, :, 0:32], u3[:, :, 0:32], w3[:, :, 0:32]))
                        p.op("V", lambda e, u3=u3, w3=w3: e.tensor_add(
                            u3[:, :, 32:64], u3[:, :, 32:64], w3[:, :, 32:64]))
                        p.op("V", lambda e, u3=u3, ss=ss: e.tensor_tensor(
                            u3, u3, ss.unsqueeze(2).broadcast_to([128, 8, 64]), ALU.mult),
                            waits=[(inv_ctr, inv_done)], inc=vC)
                        rot_v = vC.n
                        p.op("V", lambda e: e.tensor_copy(qkbf[:], qkT[:, 1024:1536]),
                             waits=[(vC, rot_v), (tC, tC.n)], inc=vC)
                        cpv = vC.n
                        for j in range(4):
                            w4 = [(vC, cpv), (sC, sC.n)] if j == 0 else []
                            p.op("T", lambda e, j=j: e.matmul(
                                PS(6)[:, 128*j:128*(j+1)], qkbf[:, 128*j:128*(j+1)],
                                sb["ident"][:], start=True, stop=True), waits=w4)
                        p.grp("T", tC)
                        trv = tC.n
                        p.op("S", lambda e, dstc=dstc, t=t: e.activation(
                            dstc[:].rearrange("p (c k) -> p c k", k=2048)[:, :, 128*t:128*(t+1)],
                            PS(6)[:].rearrange("p (c k) -> p c k", k=128), AF.Copy),
                            waits=[(tC, trv)], inc=sC)
            p.join(["T"], [sC, vC, gC])
            if FFN_DBG_STOP == "A3":
                return

            for h in range(8):
                hb, hc2 = 64 * (h % 2), h // 2
                prev_exp = 0
                state = {}
                ptev_hist = {-1: (vC, 0)}
                oev_hist = {-1: 0}

                def emit_S(qc, h=h, hb=hb, hc2=hc2):
                    kcs = list(range(max(0, qc - 4), qc + 1))
                    nk = len(kcs)
                    W_ = 128 * nk
                    sb0 = 2 * (qc % 2)
                    pslot = qc % 2
                    stat = nrm[:, 16*pslot:16*pslot+4]
                    st_ = state[qc] = dict(kcs=kcs, nk=nk, W_=W_, sb0=sb0, pslot=pslot,
                                           stat=stat)
                    w = [(sC, state[qc-2]["exp"])] if qc >= 2 else [(sC, sC.n)]
                    for ji, kc in enumerate(kcs):
                        p.op("T", lambda e, hb=hb, hc2=hc2, qc=qc, kc=kc, ji=ji,
                             sb0=sb0: e.matmul(
                            PS(sb0, 2)[:, 128*ji:128*(ji+1)],
                            qpk[hb:hb+64, 2048*hc2 + 128*qc: 2048*hc2 + 128*(qc+1)],
                            kpk[hb:hb+64, 2048*hc2 + 128*kc: 2048*hc2 + 128*(kc+1)],
                            start=True, stop=True),
                            waits=(w if ji == 0 else []))
                        p.grp("T", tC)
                    st_["smm"] = tC.n

                def emit_softmax(qc):
                    st_ = state[qc]
                    sb0, W_, nk, pslot, stat = (st_[k] for k in
                                                ("sb0", "W_", "nk", "pslot", "stat"))
                    smm = st_["smm"]
                    if qc >= 4:
                        p.op("V", lambda e, sb0=sb0: e.tensor_add(
                            PS(sb0, 2)[:, 0:128], PS(sb0, 2)[:, 0:128],
                            sb["maskadd"][:, 0:128]), waits=[(tC, smm)])
                    p.op("V", lambda e, sb0=sb0, nk=nk: e.tensor_add(
                        PS(sb0, 2)[:, 128*(nk-1):128*nk], PS(sb0, 2)[:, 128*(nk-1):128*nk],
                        sb["maskadd"][:, 128:256]), waits=[(tC, smm)])
                    p.op("V", lambda e, sb0=sb0, W_=W_, stat=stat: e.reduce_max(
                        stat[:, 0:1], PS(sb0, 2)[:, 0:W_], axis=AX.X),
                         waits=[(sC, sC.n)])
                    p.op("V", lambda e, stat=stat: e.tensor_scalar_mul(
                        stat[:, 1:2], stat[:, 0:1], -1.0), inc=vC)
                    nmv = vC.n
                    ptw = state[qc-2]["ptv"] if qc >= 2 else 0
                    p.op("S", lambda e, sb0=sb0, W_=W_, stat=stat, pslot=pslot: e.activation(
                        pexp[:, 640*pslot:640*pslot + W_], PS(sb0, 2)[:, 0:W_], AF.Exp,
                        bias=stat[:, 1:2], accum_out=stat[:, 2:3]),
                        waits=[(vC, nmv), (tC, ptw)], inc=sC)
                    st_["exp"] = sC.n
                    p.op("V", lambda e, stat=stat: e.reciprocal(stat[:, 3:4], stat[:, 2:3]),
                         waits=[(sC, st_["exp"])], inc=vC)
                    rcv = vC.n
                    p.op("S", lambda e, W_=W_, stat=stat, pslot=pslot: e.activation(
                        pexp[:, 640*pslot:640*pslot + W_], pexp[:, 640*pslot:640*pslot + W_],
                        AF.Copy, scale=stat[:, 3:4]), waits=[(vC, rcv)], inc=sC)
                    st_["pnorm"] = (sC, sC.n)

                def emit_tail(qc, h=h):
                    st_ = state[qc]
                    nk, W_, pslot, kcs = (st_[k] for k in ("nk", "W_", "pslot", "kcs"))
                    for ji in range(nk):
                        w5 = ([st_["pnorm"], ptev_hist[qc-1]]
                              if ji == 0 else [])
                        p.op("T", lambda e, ji=ji, pslot=pslot: e.matmul(
                            PS(6, 2)[:, 128*ji:128*(ji+1)],
                            pexp[:, 640*pslot + 128*ji: 640*pslot + 128*(ji+1)],
                            sb["ident"][:], start=True, stop=True), waits=w5)
                        p.grp("T", tC)
                    st_["ptv"] = tC.n
                    if qc % 2 == 0:
                        p.op("V", lambda e, W_=W_, pslot=pslot: e.tensor_copy(
                            ptb[:, 640*pslot:640*pslot + W_], PS(6, 2)[:, 0:W_]),
                            waits=[(tC, st_["ptv"])], inc=vC)
                        ptev_hist[qc] = (vC, vC.n)
                    else:
                        p.op("S", lambda e, W_=W_, pslot=pslot: e.activation(
                            ptb[:, 640*pslot:640*pslot + W_], PS(6, 2)[:, 0:W_], AF.Copy),
                            waits=[(tC, st_["ptv"])], inc=sC)
                        ptev_hist[qc] = (sC, sC.n)
                    for ji, kc in enumerate(kcs):
                        w6 = ([ptev_hist[qc], (vC, oev_hist[qc-1])]
                              if ji == 0 else [])
                        p.op("T", lambda e, kc=kc, h=h, ji=ji, pslot=pslot,
                             first=(ji == 0), last=(ji == nk - 1): e.matmul(
                            PS(5)[0:64, 0:128],
                            vaug[:, 520*kc + 65*h: 520*kc + 65*h + 64],
                            ptb[:, 640*pslot + 128*ji: 640*pslot + 128*(ji+1)],
                            start=first, stop=last), waits=w6)
                        p.grp("T", tC)
                    avv = tC.n
                    p.op("V", lambda e, h=h, qc=qc: e.tensor_copy(
                        oF[64*(h % 2):64*(h % 2)+64,
                           2048*(h // 2) + 128*qc: 2048*(h // 2) + 128*(qc+1)],
                        PS(5)[0:64, 0:128]), waits=[(tC, avv)], inc=vC)
                    oev_hist[qc] = vC.n

                for qc in range(NT):
                    emit_S(qc)
                    emit_softmax(qc)
                    if qc >= 1:
                        emit_tail(qc - 1)
                emit_tail(NT - 1)
            p.join(["T"], [vC, sC])

            ldwo = p.dma("G", lambda e, layer=layer: e.dma_start(
                out=wbuf[:, 0:4*1024].rearrange("p (c k) -> p c k", k=1024),
                in_=io["f_wo"][layer].rearrange("(c p) k -> p c k", p=128)),
                waits=[(tC, tC.n)])
            zw2 = {}
            upd_v = 0
            for t in range(NT):
                z_sb = zupd[:, 1024*(t % 2):1024*(t % 2 + 1)]
                w = ([(vC, upd_v), zw2[t-2]] if t >= 2
                     else [(sC, sC.n)] + p.dma_all()) + [zwr[t]]
                ldz = p.dma("Y", lambda e, b=b, t=t, z_sb=z_sb: e.dma_start(
                    out=z_sb, in_=zs[b, 128*t:128*(t+1), :]), waits=w)
                for half in range(2):
                    bank = half
                    for kc in range(4):
                        w2 = [(vC, vC.n)] if kc == 0 else []
                        if kc == 0 and t == 0 and half == 0:
                            w2.append(ldwo)
                        p.op("T", lambda e, t=t, kc=kc, half=half, bank=bank: e.matmul(
                            PS(bank), oF[:, 2048*kc + 128*t: 2048*kc + 128*(t+1)],
                            wbuf[:, 1024*kc + 512*half: 1024*kc + 512*(half+1)],
                            start=(kc == 0), stop=(kc == 3)), waits=w2)
                    p.grp("T", tC)
                    mmv = tC.n
                    p.op("V", lambda e, z_sb=z_sb, half=half, bank=bank: e.tensor_add(
                        z_sb[:, 512*half:512*(half+1)], z_sb[:, 512*half:512*(half+1)],
                        PS(bank)), waits=[(tC, mmv), ldz], inc=vC)
                upd_v = vC.n
                zw2[t] = p.dma("Y", lambda e, b=b, t=t, z_sb=z_sb: e.dma_start(
                    out=zs[b, 128*t:128*(t+1), :], in_=z_sb), waits=[(vC, upd_v)])
            p.join(["T", "S", "V", "G"], [])

            lnctoks = []
            lnctoks.append(p.dma("Y", lambda e, layer=layer: e.dma_start(
                out=lngB[:], in_=io["f_lng"][layer].broadcast_to([128, D])),
                waits=[(sC, sC.n), (vC, vC.n)]))
            lnctoks.append(p.dma("Y", lambda e, layer=layer: e.dma_start(
                out=lnbB[:], in_=io["f_lnb"][layer].broadcast_to([128, D]))))
            lnctoks.append(p.dma("Y", lambda e, layer=layer: e.dma_start(
                out=b1cB[:], in_=io["f_b1c"][layer])))
            ldb2 = p.dma("G", lambda e, layer=layer: e.dma_start(
                out=row2bfB[:], in_=io["f_b2"][layer]), waits=[(tC, tC.n)])
            trB = {}
            hlv = 0
            for t in range(NT):
                h_sb = hf32[:, 1024*(t % 3):1024*(t % 3 + 1)]
                w = [(tC, trB[t-3])] if t >= 3 else []
                w += [zw2[t], zw2[max(t-1, 0)]]
                ldh1 = p.dma("G", lambda e, b=b, t=t, h_sb=h_sb: e.dma_start(
                    out=h_sb[:, 0:512], in_=zs[b, 128*t:128*(t+1), 0:512]), waits=w)
                if t == 0:
                    p.op("G", lambda e, h_sb=h_sb: e.memset(h_sb[0:1, 512:1024], 0.0), inc=gC)
                    ldh2 = p.dma("G", lambda e, b=b, h_sb=h_sb: e.dma_start(
                        out=h_sb[1:128, 512:1024], in_=zs[b, 0:127, 512:1024]))
                else:
                    ldh2 = p.dma("G", lambda e, b=b, t=t, h_sb=h_sb: e.dma_start(
                        out=h_sb[:, 512:1024], in_=zs[b, 128*t-1:128*(t+1)-1, 512:1024]))
                pw = [ldh1, ldh2, (gC, gC.n), (vC, hlv)] + lnctoks
                sN, vN = ln_tile(ctx, h_sb, [scr[:]], slot=t % 8,
                                 scratch=zupd[:, 0:1024], pre_waits=pw)
                p.op("V", lambda e: e.tensor_mul(scr[:], scr[:], lngB[:]))
                p.op("V", lambda e: e.tensor_add(scr[:], scr[:], lnbB[:]))
                p.op("V", lambda e: e.tensor_copy(hlnbf[:], scr[:]), inc=vC)
                hlv = vC.n
                trB[t] = transpose_tile(
                    ctx, hlnbf, lambda j0, w_, t=t: hlnFr()[:, j0:j0+w_, 128*t:128*(t+1)],
                    ncol=8, pre_wait=[(vC, hlv)])
            p.join(["T"], [vC])

            zw3 = dict(zw2)
            for hhalf in range(2):
                chunks = list(range(11 * hhalf, min(11 * (hhalf + 1), NHC)))
                g_done = 0
                for ci, c in enumerate(chunks):
                    w1b = w1bufB[:, 2048*(ci % 2):2048*(ci % 2 + 1)]
                    w = [(tC, g_done)] if ci >= 2 else [(tC, tC.n)]
                    ldw1 = p.dma("G", lambda e, c=c, layer=layer, w1b=w1b: e.dma_start(
                        out=w1b.rearrange("p (ic k) -> p ic k", k=256),
                        in_=io["f_w1p"][layer][:, 256*c:256*(c+1)].rearrange(
                            "(ic p) k -> p ic k", p=128)), waits=w)
                    for tg in range(4):
                        abank = tg % 2
                        gbank = 2 + tg % 2
                        for kc in range(8):
                            w2 = [(vC, vC.n), ldw1] if kc == 0 else []
                            p.op("T", lambda e, kc=kc, tg=tg, abank=abank, w1b=w1b: e.matmul(
                                PS(abank), w1b[:, 256*kc:256*kc+128],
                                hlnFr()[:, kc, 512*tg:512*(tg+1)],
                                start=(kc == 0), stop=(kc == 7)), waits=w2)
                        for kc in range(8):
                            w2 = [(sC, sC.n)] if kc == 0 else []
                            p.op("T", lambda e, kc=kc, tg=tg, gbank=gbank, w1b=w1b: e.matmul(
                                PS(gbank), w1b[:, 256*kc+128:256*kc+256],
                                hlnFr()[:, kc, 512*tg:512*(tg+1)],
                                start=(kc == 0), stop=(kc == 7)), waits=w2)
                        p.grp("T", tC)
                        mmv = tC.n
                        p.op("S", lambda e, gbank=gbank, c=c: e.activation(
                            gelt[:], PS(gbank), AF.Gelu, bias=b1cB[:, 2*c+1:2*c+2]),
                            waits=[(tC, mmv)], inc=sC)
                        gel = sC.n
                        p.op("V", lambda e, abank=abank, c=c: e.tensor_scalar_add(
                            scr[:, 0:512], PS(abank), b1cB[:, 2*c:2*c+1]),
                            waits=[(tC, mmv)])
                        p.op("V", lambda e, ci=ci, tg=tg: e.tensor_mul(
                            ohalf[:, 2048*ci + 512*tg: 2048*ci + 512*(tg+1)],
                            scr[:, 0:512], gelt[:]), waits=[(sC, gel)], inc=vC)
                    g_done = tC.n
                ldw2 = p.dma("G", lambda e, layer=layer, hhalf=hhalf, nch=len(chunks):
                             e.dma_start(
                    out=w2buf[:, 0:1024*nch].rearrange("p (c k) -> p c k", k=1024),
                    in_=io["f_w2p"][layer, 11*hhalf:11*hhalf+nch].rearrange("c p k -> p c k")),
                    waits=[(tC, tC.n)])
                p.join(["T"], [vC])
                updB = 0
                st3 = {}
                for t in range(NT):
                    z_sb = zupd[:, 1024*(t % 2):1024*(t % 2 + 1)]
                    w = ([(vC, updB), st3[t-2]] if t >= 2
                         else [(sC, sC.n)] + p.dma_all())
                    ldz = p.dma("Y", lambda e, b=b, t=t, z_sb=z_sb: e.dma_start(
                        out=z_sb, in_=zs[b, 128*t:128*(t+1), :]),
                        waits=w + [zw3[t]])
                    ldx = None
                    if final and hhalf == 1:
                        x_sb = hf32[:, 1024*(t % 2):1024*(t % 2 + 1)]
                        ldx = p.dma("Y", lambda e, b=b, t=t, x_sb=x_sb: e.dma_start(
                            out=x_sb, in_=stream[b, 128*t:128*(t+1), :]),
                            waits=([(sC, sC.n)] if t < 3 else []))
                    for half in range(2):
                        bank = 4 + half
                        nch = len(chunks)
                        for ci in range(nch):
                            w2 = [(vC, vC.n), ldw2] if ci == 0 else []
                            p.op("T", lambda e, ci=ci, t=t, half=half, bank=bank,
                                 last=(ci == nch - 1 and hhalf == 0): e.matmul(
                                PS(bank), ohalf[:, 2048*ci + 128*t: 2048*ci + 128*(t+1)],
                                w2buf[:, 1024*ci + 512*half: 1024*ci + 512*(half+1)],
                                start=(ci == 0), stop=last), waits=w2)
                        if hhalf == 1:
                            p.op("T", lambda e, half=half, bank=bank: e.matmul(
                                PS(bank), sb["onesbf"][:, 0:128],
                                row2bfB[:, 512*half:512*(half+1)],
                                start=False, stop=True),
                                waits=([ldb2] if (t == 0 and half == 0) else []))
                        p.grp("T", tC)
                        mmv = tC.n
                        if final and hhalf == 1:
                            p.op("V", lambda e, z_sb=z_sb, half=half, bank=bank: e.tensor_add(
                                z_sb[:, 512*half:512*(half+1)],
                                z_sb[:, 512*half:512*(half+1)], PS(bank)),
                                waits=[(tC, mmv), ldz])
                            p.op("V", lambda e, z_sb=z_sb, half=half, x_sb=x_sb: e.tensor_add(
                                z_sb[:, 512*half:512*(half+1)],
                                z_sb[:, 512*half:512*(half+1)],
                                x_sb[:, 512*half:512*(half+1)]),
                                waits=[ldx], inc=vC)
                        else:
                            p.op("V", lambda e, z_sb=z_sb, half=half, bank=bank: e.tensor_add(
                                z_sb[:, 512*half:512*(half+1)],
                                z_sb[:, 512*half:512*(half+1)], PS(bank)),
                                waits=[(tC, mmv), ldz], inc=vC)
                    updB = vC.n
                    outdst = dst if (final and hhalf == 1) else zs
                    zw3[t] = st3[t] = p.dma("Y", lambda e, b=b, t=t, z_sb=z_sb,
                                            outdst=outdst: e.dma_start(
                        out=outdst[b, 128*t:128*(t+1), :], in_=z_sb),
                        waits=[(vC, updB)])
                p.join(["T", "S", "V", "G"], [])
            zwr = dict(zw3)
        p.join(["T", "S", "V", "G", "Y"], [vC, sC, tC, gC])

def emit_attn_b(ctx, src, dst):
    nc, io, p, C, sb, mem, PS = (ctx[k] for k in ["nc", "io", "p", "C", "sb", "mem", "PS"])
    dY, dG, tC, sC, vC, gC = (C[k] for k in ["dY", "dG", "tC", "sC", "vC", "gC"])
    es = io["zs"]
    SCALE = 0.25

    mem.reset(ctx["PHASE0"])
    eF = mem.seq("b_eF", [128, 8 * 2048], BF16)
    eF_off = mem.mark() - 32 * 1024
    qpad = mem.seq("b_qpad", [128, 4 * 2048], BF16)
    kpad = mem.seq("b_kpad", [128, 4 * 2048], BF16)
    vaug = mem.seq("b_vaug", [128, 16 * 1040], BF16)
    wvp = mem.seq("b_wvp", [128, 8 * 1024], BF16)
    qkraw = mem.seq("b_qkraw", [128, 4 * 2048], BF16)
    et = mem.seq("b_et", [128, 4 * 1024], F32)
    qn = mem.seq("b_qn", [8, 4096], BF16)
    kn = mem.seq("b_kn", [8, 4096], BF16)
    sqb = mem.seq("b_sqb", [128, 2048], BF16)
    scr = mem.seq("b_scr", [128, 1024], F32)
    maxk1 = mem.seq("b_maxk1", [1, 16], F32)
    ebf = mem.seq("b_ebf", [128, 1024], BF16)
    wq1 = mem.seq("b_wq1", [128, 2 * 1024], BF16)
    w1buf = mem.seq("b_w1buf", [128, 8 * 64], BF16)
    expS = mem.at("b_expS", [128, 2 * 2048], BF16, eF_off)
    outF = mem.at("b_outF", [128, 8 * 512], BF16, eF_off + 8 * 1024)
    resF = mem.at("b_resF", [128, 8 * 512], BF16, eF_off + 16 * 1024)
    obf = mem.at("b_obf", [128, 2048], BF16, eF_off + 24 * 1024)
    hW = mem.at("b_hW", [65, 512], BF16, eF_off + 28 * 1024)
    rowstb = mem.at("b_rowst", [1, 2048], BF16, eF_off)
    rowstb2 = mem.at("b_rowst2", [1, 2048], BF16, eF_off + 4096)
    eFr = lambda: eF[:].rearrange("p (c k) -> p c k", k=2048)

    for b in range(B_LOC):
        ewr, ln_s = {}, {}
        for t in range(NT):
            x_sb = et[:, 1024*(t % 4):1024*(t % 4 + 1)]
            w = [(sC, ln_s[t-4][0]), (vC, ln_s[t-4][1])] if t >= 4 else []
            ld = p.dma("Y", lambda e, b=b, t=t, x_sb=x_sb: e.dma_start(
                out=x_sb, in_=src[b, 128*t:128*(t+1), :]), waits=w)
            z_sb = scr[:]
            pw = [ld]
            if t >= 1:
                pw.append(ewr[t-1])
            sN, vN = ln_tile(ctx, x_sb, [z_sb], slot=t % 8,
                             scratch=sqb[:].bitcast(F32), pre_waits=pw)
            ln_s[t] = (sN, vN)
            ewr[t] = p.dma("Y", lambda e, b=b, t=t, z_sb=z_sb: e.dma_start(
                out=es[b, 128*t:128*(t+1), :], in_=z_sb), waits=[(vC, vN)])

        negd = []
        for cch in range(16):
            g_, si = cch // 4, cch % 4
            negd.append(p.dma("Y", lambda e, g_=g_, si=si: e.dma_start(
                out=kpad[32*si+16:32*si+17, 2048*g_:2048*(g_+1)], in_=io["negrow"][:])))

        for it in range(2):
            tr_done = {}
            for t in range(NT):
                w = [(tC, tr_done[t-1])] if t >= 1 else list(negd)
                w += [ewr[t]]
                ldt = p.dma("G", lambda e, b=b, t=t: e.dma_start(
                    out=ebf[:], in_=es[b, 128*t:128*(t+1), :]), waits=w)
                tr_done[t] = transpose_tile(
                    ctx, ebf, lambda j0, w_, t=t: eFr()[:, j0:j0+w_, 128*t:128*(t+1)],
                    ncol=8, pre_wait=[ldt])
            p.join(["T"], [vC])

            for wi, Wn in enumerate(("bWqT", "bWkT")):
                for oc in range(2):
                    idx = 2 * wi + oc
                    wb = wq1[:, 1024*(idx % 2):1024*(idx % 2 + 1)]
                    w = [(tC, qk_hist)] if idx >= 2 else [(tC, tC.n)]
                    ldw = p.dma("G", lambda e, Wn=Wn, oc=oc, wb=wb: e.dma_start(
                        out=wb.rearrange("p (c k) -> p c k", k=128),
                        in_=io[Wn][:, 128*oc:128*(oc+1)].rearrange("(c p) k -> p c k", p=128)),
                        waits=w)
                    for tg in range(4):
                        bank = tg % 2
                        for kc in range(8):
                            w2 = [ldw, (sC, sC.n)] if kc == 0 else []
                            p.op("T", lambda e, wb=wb, tg=tg, kc=kc, bank=bank: e.matmul(
                                PS(bank), wb[:, 128*kc:128*(kc+1)],
                                eFr()[:, kc, 512*tg:512*(tg+1)],
                                start=(kc == 0), stop=(kc == 7)), waits=w2)
                        p.grp("T", tC)
                        mmv = tC.n
                        sc_ = SCALE if wi == 0 else 1.0
                        p.op("S", lambda e, wi=wi, oc=oc, tg=tg, bank=bank, sc_=sc_: e.activation(
                            qkraw[:, 2048*(2*wi+oc) + 512*tg: 2048*(2*wi+oc) + 512*(tg+1)],
                            PS(bank), AF.Copy, scale=sc_), waits=[(tC, mmv)], inc=sC)
                    qk_hist = tC.n
            ldv = p.dma("G", lambda e: e.dma_start(
                out=wvp[:].rearrange("p (c k) -> p c k", k=1024),
                in_=io["bWvT"][:].rearrange("(c p) k -> p c k", p=128)),
                waits=[(tC, tC.n)])
            for t in range(NT):
                for half in range(2):
                    bank = (2 * t + half) % 2
                    for kc in range(8):
                        w2 = [ldv, (vC, vC.n), (sC, sC.n)] if kc == 0 else []
                        p.op("T", lambda e, t=t, half=half, kc=kc, bank=bank: e.matmul(
                            PS(bank), eFr()[:, kc, 128*t:128*(t+1)],
                            wvp[:, 1024*kc + 512*half: 1024*kc + 512*(half+1)],
                            start=(kc == 0), stop=(kc == 7)), waits=w2)
                    p.grp("T", tC)
                    mmv = tC.n
                    p.op("V", lambda e, t=t, half=half, bank=bank: e.tensor_copy(
                        vaug[:, 1040*t:1040*(t+1)].rearrange(
                            "p (g k) -> p g k", k=65)[:, 8*half:8*half+8, 0:64],
                        PS(bank).rearrange("p (g k) -> p g k", k=64)),
                        waits=[(tC, mmv)], inc=vC)
            for t in range(NT):
                p.op("G", lambda e, t=t: e.memset(
                    vaug[:, 1040*t:1040*(t+1)].rearrange("p (g k) -> p g k", k=65)[:, :, 64:65],
                    1.0), inc=gC)
            vones = gC.n
            p.join(["T"], [vC, sC])

            for wi, (rawbase, dstn) in enumerate(((0, qn), (2, kn))):
                for ch in range(2):
                    p.op("S", lambda e, rawbase=rawbase, ch=ch: e.activation(
                        sqb[:], qkraw[:, 2048*(rawbase+ch):2048*(rawbase+ch+1)], AF.Square),
                        waits=[(vC, vC.n)], inc=sC)
                    sqv = sC.n
                    for ng in range(4):
                        w4 = [(sC, sqv), (sC, sC.n)] if ng == 0 else []
                        p.op("T", lambda e, ng=ng: e.matmul(
                            PS(2, 4)[0:8, 512*ng:512*(ng+1)], sb["ind8"][:],
                            sqb[:, 512*ng:512*(ng+1)], start=True, stop=True), waits=w4)
                    p.grp("T", tC)
                    mmn = tC.n
                    p.op("S", lambda e, dstn=dstn, ch=ch: e.activation(
                        dstn[0:8, 2048*ch:2048*(ch+1)], PS(2, 4)[0:8, 0:2048], AF.Copy),
                        waits=[(tC, mmn)], inc=sC)
            nrm_s = sC.n
            p.op("V", lambda e: e.reduce_max(
                scr[0:8, 0:2], kn[0:8, :].rearrange("p (c k) -> p c k", k=2048), axis=AX.X),
                waits=[(sC, nrm_s)], inc=vC)
            mkv = vC.n
            mkd = p.dma("Y", lambda e: e.dma_start(out=maxk1[0:1, 0:16], in_=scr[0:8, 0:2],
                                                   allow_slow_non_contiguous=True),
                        waits=[(vC, mkv)])
            for cch in range(16):
                g_, si = cch // 4, cch % 4
                src_ch, src_r = cch // 8, 16 * (cch % 8)
                for (rawbase, dpad) in ((0, qpad), (2, kpad)):
                    p.dma("Y", lambda e, rawbase=rawbase, src_ch=src_ch, src_r=src_r,
                          g_=g_, si=si, dpad=dpad: e.dma_start(
                        out=dpad[32*si:32*si+16, 2048*g_:2048*(g_+1)],
                        in_=qkraw[src_r:src_r+16,
                                  2048*(rawbase+src_ch):2048*(rawbase+src_ch+1)]),
                        waits=[(sC, nrm_s)] if cch == 0 else [])
                rowd = p.dma("Y", lambda e, cch=cch, src_ch=src_ch: e.dma_start(
                    out=rowstb[:], in_=qn[cch % 8:cch % 8 + 1,
                                          2048*src_ch:2048*(src_ch+1)]),
                    waits=[(sC, bsts)] if cch > 0 else [(sC, nrm_s)])
                wS = [rowd, mkd]
                if cch > 0:
                    wS.append(brow_tok)
                p.op("S", lambda e, cch=cch: e.activation(
                    rowstb2[:], rowstb[:], AF.Sqrt,
                    scale=maxk1[0:1, cch:cch+1]), waits=wS, inc=sC)
                bsts = sC.n
                brow_tok = p.dma("Y", lambda e, g_=g_, si=si: e.dma_start(
                    out=qpad[32*si+16:32*si+17, 2048*g_:2048*(g_+1)], in_=rowstb2[:]),
                    waits=[(sC, bsts)])
            p.join(["T"], [vC, gC])

            ldwp = p.dma("G", lambda e: e.dma_start(
                out=wvp[:].rearrange("p (c k) -> p c k", k=1024),
                in_=io["bWpT"][:].rearrange("(c p) k -> p c k", p=128)),
                waits=[(tC, tC.n)])
            ldw1 = p.dma("G", lambda e: e.dma_start(
                out=w1buf[:].rearrange("p (c k) -> p c k", k=64),
                in_=io["bW1T"][:].rearrange("(c p) k -> p c k", p=128)))
            p.op("G", lambda e: e.memset(hW[64:65, :], 1.0), inc=gC)
            hwones = gC.n

            PSF = PS(0, 8)

            def AVAP(i):
                off = 512 * (4 + i // 7) + 65 * (i % 7)
                return PSF[:, off:off + 65]
            av_war = vC.n
            prev_av_last = tC.n
            exp15_prev = sC.n
            wp_v_prev = vC.n
            ptv_hist = {}
            bank_tok = {3: (vC, 0), 7: (vC, 0)}
            for ng in range(4):
                for cg in range(4):
                    gcg = 4 * ng + cg
                    obslot = 1024 * (gcg % 2)
                    avd = {}
                    exp_done = {}
                    for mg in range(16):
                        if mg >= 1:
                            w = [(sC, exp_done[mg - 1])]
                        elif gcg == 0:
                            w = [(sC, sC.n), (vC, vC.n)]
                        else:
                            w = [(sC, exp15_prev), bank_tok[3]]
                        for j in range(4):
                            for s in range(4):
                                cc = 32 * ((s + j) % 4)
                                p.op("T", lambda e, cg=cg, s=s, cc=cc, mg=mg, ng=ng:
                                     e.matmul(
                                    PSF[cc:cc+32, 512*s:512*s+512],
                                    kpad[32*s:32*s+17,
                                         2048*cg + 128*mg + cc: 2048*cg + 128*mg + cc + 32],
                                    qpad[32*s:32*s+17,
                                         2048*cg + 512*ng: 2048*cg + 512*(ng+1)],
                                    start=True, stop=True, tile_position=(32*s, cc)),
                                    waits=(w if (j == 0 and s == 0) else []))
                        p.grp("T", tC)
                        smm = tC.n
                        eslot = mg % 2
                        w = [(tC, smm)]
                        if mg >= 2:
                            w.append((tC, avd[mg - 2]))
                        elif gcg > 0 or it > 0 or b > 0:
                            w.append((tC, prev_av_last))
                        p.op("S", lambda e, eslot=eslot: e.activation(
                            expS[:, 2048*eslot:2048*(eslot+1)], PS(0, 4), AF.Exp),
                            waits=w, inc=sC)
                        exp_done[mg] = sC.n
                        for s in range(4):
                            for qq in range(4):
                                i = 4 * s + qq
                                w = []
                                if i == 0:
                                    w = [(sC, exp_done[mg])]
                                    if mg == 0:
                                        w += [(gC, vones), (vC, av_war),
                                              (vC, wp_v_prev)]
                                p.op("T", lambda e, i=i, s=s, qq=qq, mg=mg, cg=cg,
                                     eslot=eslot,
                                     first=(mg == 0 and i in (0, 7, 14)),
                                     last=(mg == 15 and i in (6, 13, 15)):
                                     e.matmul(
                                    AVAP(i),
                                    expS[:, 2048*eslot + 512*s + 128*qq:
                                         2048*eslot + 512*s + 128*(qq+1)],
                                    vaug[:, 1040*mg + 65*(4*cg+s):
                                         1040*mg + 65*(4*cg+s) + 65],
                                    start=first, stop=last), waits=w)
                        p.grp("T", tC)
                        avd[mg] = tC.n
                    av_last = prev_av_last = avd[15]
                    exp15_prev = exp_done[15]
                    for jb in range(3):
                        nsl = (7, 7, 2)[jb]
                        p.op("V", lambda e, jb=jb, nsl=nsl: e.reciprocal(
                            scr[:, 8*jb:8*jb+nsl].rearrange("p (g k) -> p g k", k=1),
                            PSF[:, 512*(4+jb):512*(4+jb) + 65*nsl].rearrange(
                                "p (g k) -> p g k", k=65)[:, :, 64:65]),
                            waits=([(tC, av_last)] if jb == 0 else []))
                    rcv = vC.n
                    for i in range(16):
                        w = []
                        if i == 0:
                            w = [(vC, rcv), (tC, ptv_hist.get(gcg - 2, 0))]
                        p.op("V", lambda e, i=i, obslot=obslot: e.tensor_scalar_mul(
                            obf[:, obslot + 64*i: obslot + 64*(i+1)],
                            AVAP(i)[:, 0:64], scr[:, 8*(i//7) + (i % 7):
                                                  8*(i//7) + (i % 7) + 1]), waits=w)
                    nrmv = vC.n
                    av_war = nrmv
                    for s in range(4):
                        tb = 3 if s % 2 == 0 else 7
                        for qq in range(4):
                            w = []
                            if qq == 0:
                                w = [(vC, nrmv), bank_tok[tb]]
                            p.op("T", lambda e, s=s, qq=qq, tb=tb, obslot=obslot:
                                 e.matmul(
                                PS(tb)[0:64, 128*qq:128*(qq+1)],
                                obf[:, obslot + 256*s + 64*qq:
                                    obslot + 256*s + 64*(qq+1)],
                                sb["ident"][:], start=True, stop=True), waits=w)
                        p.grp("T", tC)
                        ptv_s = tC.n
                        cch = 4 * cg + s
                        if s % 2 == 0:
                            p.op("V", lambda e, cch=cch, tb=tb: e.tensor_copy(
                                outF[64*(cch % 2):64*(cch % 2)+64,
                                     512*(cch // 2):512*(cch // 2 + 1)],
                                PS(tb)[0:64, 0:512]), waits=[(tC, ptv_s)], inc=vC)
                            bank_tok[tb] = (vC, vC.n)
                        else:
                            p.op("S", lambda e, cch=cch, tb=tb: e.activation(
                                outF[64*(cch % 2):64*(cch % 2)+64,
                                     512*(cch // 2):512*(cch // 2 + 1)],
                                PS(tb)[0:64, 0:512], AF.Copy), waits=[(tC, ptv_s)], inc=sC)
                            bank_tok[tb] = (sC, sC.n)
                    ptv_hist[gcg] = tC.n
                out_done = vC.n
                wp_v = 0
                for tc in range(4):
                    t = 4 * ng + tc
                    e_sb = et[:, 1024*tc:1024*(tc+1)]
                    prev_t = (4 * (ng - 1)) % 16 + tc
                    lde = p.dma("Y", lambda e, b=b, t=t, e_sb=e_sb: e.dma_start(
                        out=e_sb, in_=es[b, 128*t:128*(t+1), :]),
                        waits=[(vC, out_done), ewr[t], ewr[prev_t]])
                    for half in range(2):
                        bank = 6 + half
                        for kc in range(8):
                            w2 = [(vC, vC.n)] if kc == 0 else []
                            if tc == 0 and half == 0 and kc == 0:
                                w2 += [ldwp, (vC, out_done), (sC, sC.n)]
                            p.op("T", lambda e, kc=kc, tc=tc, half=half, bank=bank: e.matmul(
                                PS(bank), outF[:, 512*kc + 128*tc: 512*kc + 128*(tc+1)],
                                wvp[:, 1024*kc + 512*half: 1024*kc + 512*(half+1)],
                                start=(kc == 0), stop=(kc == 7)), waits=w2)
                        p.grp("T", tC)
                        mmv = tC.n
                        p.op("V", lambda e, e_sb=e_sb, half=half, bank=bank: e.tensor_add(
                            e_sb[:, 512*half:512*(half+1)], e_sb[:, 512*half:512*(half+1)],
                            PS(bank)), waits=[(tC, mmv), lde], inc=vC)
                    wp_v = wp_v_prev = vC.n
                rtr = 0
                for tc in range(4):
                    e_sb = et[:, 1024*tc:1024*(tc+1)]
                    p.op("V", lambda e, e_sb=e_sb: e.tensor_copy(ebf[:], e_sb),
                         waits=[(vC, wp_v)] if tc == 0 else [(tC, rtr)], inc=vC)
                    cpv = vC.n
                    rtr = transpose_tile(
                        ctx, ebf, lambda j0, w_, tc=tc: resF[:].rearrange(
                            "p (c k) -> p c k", k=512)[:, j0:j0+w_, 128*tc:128*(tc+1)],
                        ncol=8, pre_wait=[(vC, cpv)], bank=7)
                for kc in range(8):
                    w2 = [ldw1, (sC, sC.n), (vC, vC.n)] if kc == 0 else []
                    p.op("T", lambda e, kc=kc: e.matmul(
                        PS(7)[0:64, 0:512], w1buf[:, 64*kc:64*(kc+1)],
                        resF[:].rearrange("p (c k) -> p c k", k=512)[:, kc, :],
                        start=(kc == 0), stop=(kc == 7)), waits=w2)
                p.grp("T", tC)
                w1v = tC.n
                p.op("S", lambda e: e.activation(
                    hW[0:64, :], PS(7)[0:64, 0:512], AF.Relu, bias=sb["b_b1"][:]),
                    waits=[(tC, w1v)], inc=sC)
                hWs = sC.n
                lastit = it == 1
                for tc in range(4):
                    t = 4 * ng + tc
                    e_sb = et[:, 1024*tc:1024*(tc+1)]
                    ldx = None
                    if lastit:
                        ldx = p.dma("Y", lambda e, b=b, t=t: e.dma_start(
                            out=scr[:, 0:1024], in_=src[b, 128*t:128*(t+1), :]),
                            waits=[(vC, vC.n)])
                    for half in range(2):
                        bank = 7
                        w2 = [(sC, hWs), (gC, hwones), (vC, vC.n)]
                        p.op("T", lambda e, tc=tc, half=half, bank=bank: e.matmul(
                            PS(bank), hW[0:65, 128*tc:128*(tc+1)],
                            sb["bW2aug"][0:65, 512*half:512*(half+1)],
                            start=True, stop=True), waits=w2, inc=tC)
                        mmv = tC.n
                        if lastit:
                            p.op("V", lambda e, e_sb=e_sb, half=half, bank=bank: e.tensor_add(
                                e_sb[:, 512*half:512*(half+1)],
                                e_sb[:, 512*half:512*(half+1)], PS(bank)),
                                waits=[(tC, mmv)])
                            p.op("V", lambda e, e_sb=e_sb, half=half: e.tensor_add(
                                e_sb[:, 512*half:512*(half+1)],
                                e_sb[:, 512*half:512*(half+1)], scr[:, 512*half:512*(half+1)]),
                                waits=[ldx], inc=vC)
                        else:
                            p.op("V", lambda e, e_sb=e_sb, half=half, bank=bank: e.tensor_add(
                                e_sb[:, 512*half:512*(half+1)],
                                e_sb[:, 512*half:512*(half+1)], PS(bank)),
                                waits=[(tC, mmv)], inc=vC)
                    uv = vC.n
                    outdst = dst if lastit else es
                    ewr[t] = p.dma("Y", lambda e, b=b, t=t, e_sb=e_sb, outdst=outdst:
                                   e.dma_start(
                        out=outdst[b, 128*t:128*(t+1), :], in_=e_sb),
                        waits=[(vC, uv)])
                bank_tok[7] = (vC, vC.n)
            p.join(["T", "S", "V", "G"], [])
        p.join(["T", "S", "V", "G", "Y"], [vC, sC, tC, gC])

def build_program():
    nc = bass.Bass("TRN2", target_bir_lowering=False)
    io = {}
    def din(name, shape, dtype=F32):
        io[name] = nc.dram_tensor(name, list(shape), dtype, kind="ExternalInput")
    din("x", [B_LOC, S, D])
    for n, s_ in (("aWqT", [D, D]), ("aWkT", [D, D]), ("aWvT", [D, D]), ("aWoT", [D, D]),
                  ("a_bqk", [128, 16]), ("a_bv", [1, D]), ("a_bo", [1, D]),
                  ("bWqT", [D, 256]), ("bWkT", [D, 256]), ("bWvT", [D, D]), ("bWpT", [D, D]),
                  ("bW1T", [D, 64]), ("b_b1", [64, 1]), ("bW2aug", [65, D]),
                  ("f_wqkv", [2, D, 1536]), ("f_wo", [2, 512, D]),
                  ("f_w1p", [2, D, 22 * 256]), ("f_b1c", [2, 128, 44]),
                  ("f_w2p", [2, 22, 128, D]), ("f_b2", [2, 1, D]),
                  ("f_lng", [2, 1, D]), ("f_lnb", [2, 1, D]),
                  ("f_qs8", [2, 512]), ("f_ks8", [2, 512]), ("onesf32", [1, 128]),
                  ("maskadd", [128, 256]),
                  ("rot_qc", [S, 64]), ("rot_qs", [S, 64]), ("rot_kc", [S, 64]),
                  ("rot_ks", [S, 64])):
        din(n, s_)
    for n, s_ in (("ident", [128, 128]), ("maskP12", [128, 256]), ("onesbf", [1, 128]),
                  ("ind8", [128, 8]), ("negrow", [1, S])):
        din(n, s_, BF16)
    io["dbg"] = nc.dram_tensor("dbg", [128, 16384], BF16, kind="ExternalOutput")
    io["dbg2"] = nc.dram_tensor("dbg2", [128, 16384], BF16, kind="ExternalOutput")
    io["dbg3"] = nc.dram_tensor("dbg3", [128, 16384], BF16, kind="ExternalOutput")
    io["xs"] = nc.dram_tensor("xs", [B_LOC, S, D], F32, kind="Internal")
    io["zs"] = nc.dram_tensor("zs", [B_LOC, S, D], F32, kind="Internal")
    io["y"] = nc.dram_tensor("y", [B_LOC, S, D], F32, kind="ExternalOutput")

    p = Plan()
    C = {n: p.ctr(n) for n in ["tC", "sC", "vC", "gC", "dY", "dG"]}
    p.auto = {"V": C["vC"], "S": C["sC"], "G": C["gC"]}
    p.mkqueue("Y", 10)
    p.mkqueue("G", 8)
    mem = Mem(nc)
    sb = {}
    sb["ident"] = mem.seq("c_ident", [128, 128], BF16)
    sb["maskP12"] = mem.seq("c_maskP12", [128, 256], BF16)
    sb["maskadd"] = mem.seq("c_maskadd", [128, 256], F32)
    sb["row2bf"] = mem.seq("c_row2bf", [1, D], BF16)
    sb["onesbf"] = mem.seq("c_onesbf", [1, 128], BF16)
    sb["onesf32"] = mem.seq("c_onesf32", [1, 128], F32)
    sb["ind8"] = mem.seq("c_ind8", [128, 8], BF16)
    sb["a_bqk"] = mem.seq("c_a_bqk", [128, 16], F32)
    sb["stats"] = mem.seq("c_stats", [128, 64], F32)
    sb["b_b1"] = mem.seq("c_b_b1", [64, 1], F32)
    sb["bW2aug"] = mem.seq("c_bW2aug", [65, D], BF16)
    sb["qs8"] = mem.seq("c_qs8", [128, 512], F32)
    sb["ks8"] = mem.seq("c_ks8", [128, 512], F32)
    PHASE0 = mem.mark()

    psum = nc.alloc_psum_tensor("ps", [128, 4096], F32)
    def PS(b_, w=1):
        return psum[:, 512*b_:512*(b_+w)]

    for n in ["ident", "maskP12", "maskadd", "onesbf", "onesf32", "ind8", "a_bqk",
              "b_b1"]:
        p.dma("Y", lambda e, n=n: e.dma_start(out=sb[n][:], in_=io[n][:]))
    p.dma("G", lambda e: e.dma_start(out=sb["bW2aug"][:], in_=io["bW2aug"][:]))
    p.wait("T", p.dma_all())
    p.wait("S", p.dma_all())
    p.wait("V", p.dma_all())
    ctx = dict(nc=nc, io=io, p=p, C=C, sb=sb, mem=mem, PS=PS, PHASE0=PHASE0)

    emit_attn_a(ctx, src=io["x"], dst=(io["xs"] if PHASES > 1 else io["y"]))
    if PHASES > 1:
        emit_ffn(ctx, stream=io["xs"], dst=(io["xs"] if PHASES > 2 else io["y"]))
    if PHASES > 2:
        emit_attn_b(ctx, src=io["xs"], dst=(io["xs"] if PHASES > 3 else io["y"]))
    if PHASES > 3:
        emit_ffn(ctx, stream=io["xs"], dst=io["y"])

    p.wait("Y", p.dma_all())

    import contextlib
    stack = contextlib.ExitStack()
    for c in p.ctrs:
        c.h = stack.enter_context(nc.semaphore(c.name))
    with nc.Block() as block:
        @block.tensor
        def _(e):
            for f in p.items["T"]:
                f(e)
        @block.scalar
        def _(e):
            for f in p.items["S"]:
                f(e)
        @block.vector
        def _(e):
            for f in p.items["V"]:
                f(e)
        @block.gpsimd
        def _(e):
            for f in p.items["G"]:
                f(e)
        @block.sync
        def _(e):
            for f in p.items["Y"]:
                f(e)
    stack.close()
    nc.dbg_mem = mem
    return nc

def kernel(**inputs):
    d = prepare_host(inputs)
    x = np.asarray(inputs["L0_attn_a_x"], dtype=np.float32)
    nc = build_program()
    in_maps = []
    for c in range(N_CORES):
        m = dict(d)
        m["x"] = np.ascontiguousarray(x[2*c:2*c+2])
        in_maps.append(m)
    res = run_bass_kernel_spmd(nc, in_maps, core_ids=list(range(N_CORES)))
    return np.concatenate([r["y"] for r in res.results], axis=0)


# version 10
# speedup vs baseline: 1.1817x; 1.0036x over previous
import numpy as np
import ml_dtypes
import concourse.bass as bass
import concourse.mybir as mybir
from concourse.bass_utils import run_bass_kernel_spmd

F32 = mybir.dt.float32
BF16 = mybir.dt.bfloat16
AF = mybir.ActivationFunctionType
ALU = mybir.AluOpType
AX = mybir.AxisListType
bfloat16 = ml_dtypes.bfloat16

N_CORES = 8
B_LOC = 2
S = 2048
D = 1024
NT = S // 128
PHASES = 4
DBG_TAP = None
FFN_DBG_STOP = None

class Ctr:
    def __init__(self, name):
        self.name, self.n, self.h = name, 0, None

class Plan:
    def __init__(self):
        self.items = {k: [] for k in "TSVGY"}
        self.ctrs = []
        self.queues = {}
    def ctr(self, name):
        c = Ctr(name); self.ctrs.append(c); return c
    def mkqueue(self, eng, n=8):
        self.queues[eng] = [self.ctr(f"d{eng}{i}") for i in range(n)], [0]
    def dma(self, eng, fn, waits=()):
        cs, kbox = self.queues[eng]
        c = cs[kbox[0] % len(cs)]
        kbox[0] += 1
        w = list(waits)
        if c.n > 0:
            w.append((c, c.n))
        self.op(eng, fn, waits=w, inc=c, by=16, _dma=True)
        return (c, c.n)
    def dma_all(self):
        toks = []
        for cs, _ in self.queues.values():
            toks += [(c, c.n) for c in cs if c.n > 0]
        return toks
    def op(self, eng, fn, waits=(), inc=None, by=1, _dma=False):
        if eng in ("V", "S", "G") and not _dma:
            c = self.auto[eng]
            waits = list(waits) + [(c, c.n)]
            if inc is None:
                inc, by = c, 1
            else:
                assert inc is c, f"{eng} op with foreign inc"
        for (c, v) in waits:
            assert v <= c.n, f"wait on future: {c.name} {v} > {c.n}"
        waits = tuple(waits)
        if inc is not None:
            inc.n += by
        def run(e, fn=fn, waits=waits, inc=inc, by=by):
            for (c, v) in waits:
                if v > 0:
                    e.wait_ge(c.h, v)
            ins = fn(e)
            if inc is not None:
                ins.then_inc(inc.h, by)
            return ins
        self.items[eng].append(run)
        return inc.n if inc is not None else None
    def grp(self, eng, ctr, by=1):
        ctr.n += by
        prev_run = self.items[eng].pop()
        def run2(e, prev_run=prev_run, ctr=ctr, by=by):
            ins = prev_run(e)
            ins.then_inc(ctr.h, by)
            return ins
        self.items[eng].append(run2)
        return ctr.n
    def wait(self, eng, waits):
        for (c, v) in waits:
            assert v <= c.n, f"wait on future: {c.name} {v} > {c.n}"
        waits = tuple(waits)
        def run(e, waits=waits):
            for (c, v) in waits:
                if v > 0:
                    e.wait_ge(c.h, v)
        self.items[eng].append(run)
    def join(self, engs, ctrs, dmas=True):
        for eng in engs:
            w = [(c, c.n) for c in ctrs]
            if dmas:
                w += self.dma_all()
            self.wait(eng, w)

class Mem:
    def __init__(self, nc):
        self.size = (nc.sbuf_bytes_remaining - 4096) & ~63
        self.nc = nc
        self.handle = nc.alloc_sbuf_tensor("pool", [128, self.size], mybir.dt.uint8)
        self.pap = self.handle.ap() if hasattr(self.handle, "ap") else self.handle[:]
        self.cur = 0
        self.reg = {}
        self.offs = {}
    def at(self, name, shape, dtype, off):
        nbytes = int(np.prod(shape[1:])) * mybir.dt.size(dtype)
        assert off + nbytes <= self.size, f"{name}: {off}+{nbytes} > {self.size}"
        ap = self.pap[:, off:off + nbytes].bitcast(dtype)
        if shape[0] != 128:
            ap = ap[0:shape[0], :]
        self.reg[name] = ap
        self.offs[name] = (off, tuple(shape), dtype)
        return ap
    def seq(self, name, shape, dtype):
        off = (self.cur + 63) & ~63
        t = self.at(name, shape, dtype, off)
        self.cur = off + int(np.prod(shape[1:])) * mybir.dt.size(dtype)
        return t
    def mark(self):
        return self.cur
    def reset(self, mark):
        self.cur = mark

def _rotary_tables():
    DIM_HEAD, WINDOW = 64, 512
    half_idx = np.arange(0, DIM_HEAD, 2, dtype=np.float64)
    inv_freq = 1.0 / 10000.0 ** (half_idx / DIM_HEAD)
    t = np.arange(S, dtype=np.float64)
    freqs = t[:, None] * inv_freq[None, :]
    freqs = np.concatenate([freqs, freqs], axis=-1)
    base = (half_idx + 0.4 * DIM_HEAD) / (1.4 * DIM_HEAD)
    power = (t - S // 2) / float(WINDOW // 2)
    scale = base[None, :] ** power[:, None]
    scale = np.concatenate([scale, scale], axis=-1)
    cos, sin = np.cos(freqs), np.sin(freqs)
    return ((cos * scale).astype(np.float32), (sin * scale).astype(np.float32),
            (cos / scale).astype(np.float32), (sin / scale).astype(np.float32))

def prepare_host(inp):
    f32 = np.float32
    g = lambda k: np.asarray(inp[k])
    d = {}
    d["aWqT"] = np.ascontiguousarray(g("L0_attn_a_Wq").T.astype(f32))
    d["aWkT"] = np.ascontiguousarray(g("L0_attn_a_Wk").T.astype(f32))
    d["aWvT"] = np.ascontiguousarray(g("L0_attn_a_Wv").T.astype(f32))
    d["aWoT"] = np.ascontiguousarray(g("L0_attn_a_Wo").T.astype(f32))
    a_bqk = np.zeros((128, 16), f32)
    a_bqk[:, 0::2] = g("L0_attn_a_bq").astype(f32).reshape(8, 128).T
    a_bqk[:, 1::2] = g("L0_attn_a_bk").astype(f32).reshape(8, 128).T
    d["a_bqk"] = a_bqk
    d["a_bv"] = g("L0_attn_a_bv").astype(f32).reshape(1, D)
    d["a_bo"] = g("L0_attn_a_bo").astype(f32).reshape(1, D)
    d["bWqT"] = np.ascontiguousarray(g("L0_attn_b_Wq").T.astype(f32))
    d["bWkT"] = np.ascontiguousarray(g("L0_attn_b_Wk").T.astype(f32))
    d["bWvT"] = np.ascontiguousarray(g("L0_attn_b_Wv").T.astype(f32))
    d["bWpT"] = np.ascontiguousarray(g("L0_attn_b_Wp").T.astype(f32))
    d["bW1T"] = np.ascontiguousarray(g("L0_attn_b_W1").T.astype(f32))
    d["b_b1"] = g("L0_attn_b_b1").astype(f32).reshape(64, 1)
    d["bW2aug"] = np.ascontiguousarray(np.concatenate(
        [g("L0_attn_b_W2").T.astype(f32), g("L0_attn_b_b2").astype(f32).reshape(1, D)], 0))
    HID, NHC = 2730, 22
    d["f_wqkv"] = np.ascontiguousarray(g("L0_ffn_wqkv").astype(f32))
    d["f_wo"] = np.ascontiguousarray(g("L0_ffn_wo").astype(f32))
    w1 = g("L0_ffn_w1").astype(f32)
    w1p = np.zeros((2, D, NHC, 256), f32)
    for c in range(NHC):
        w = min(128, HID - c * 128)
        w1p[:, :, c, 0:w] = w1[:, :, c*128:c*128+w]
        w1p[:, :, c, 128:128+w] = w1[:, :, HID + c*128:HID + c*128+w]
    d["f_w1p"] = np.ascontiguousarray(w1p.reshape(2, D, NHC * 256))
    b1 = g("L0_ffn_b1").astype(f32)
    b1c = np.zeros((2, 128, 2 * NHC), f32)
    for c in range(NHC):
        w = min(128, HID - c * 128)
        b1c[:, 0:w, 2*c] = b1[:, c*128:c*128+w]
        b1c[:, 0:w, 2*c+1] = b1[:, HID + c*128:HID + c*128+w]
    d["f_b1c"] = np.ascontiguousarray(b1c)
    w2 = g("L0_ffn_w2").astype(f32)
    w2p = np.zeros((2, NHC, 128, D), f32)
    for c in range(NHC):
        w = min(128, HID - c * 128)
        w2p[:, c, 0:w, :] = w2[:, c*128:c*128+w, :]
    d["f_w2p"] = np.ascontiguousarray(w2p)
    d["f_b2"] = g("L0_ffn_b2").astype(f32).reshape(2, 1, D)
    d["f_lng"] = g("L0_ffn_ln_g").astype(f32).reshape(2, 1, D)
    d["f_lnb"] = g("L0_ffn_ln_b").astype(f32).reshape(2, 1, D)
    d["f_qs8"] = np.ascontiguousarray(np.tile(g("L0_ffn_q_scale").astype(f32), (1, 8)))
    d["f_ks8"] = np.ascontiguousarray(np.tile(g("L0_ffn_k_scale").astype(f32), (1, 8)))
    d["ident"] = np.eye(128, dtype=bfloat16)
    r = np.arange(128)
    d["maskP12"] = np.concatenate(
        [(r[:, None] <= r[None, :]).astype(bfloat16),
         (r[:, None] >= r[None, :]).astype(bfloat16)], axis=1)
    d["onesbf"] = np.ones((1, 128), dtype=bfloat16)
    r2 = np.arange(128)
    maskadd = np.zeros((128, 256), np.float32)
    maskadd[:, 0:128][r2[:, None] > r2[None, :]] = -1e30
    maskadd[:, 128:256][r2[:, None] < r2[None, :]] = -1e30
    d["maskadd"] = maskadda = maskadd
    d["onesf32"] = np.ones((1, 128), dtype=np.float32)
    d["negrow"] = np.full((1, S), -1.0, dtype=bfloat16)
    ind8 = np.zeros((128, 8), dtype=bfloat16)
    for rr in range(128):
        ind8[rr, rr // 16] = 1
    d["ind8"] = ind8
    d["rot_qc"], d["rot_qs"], d["rot_kc"], d["rot_ks"] = _rotary_tables()
    return d

def ln_tile(ctx, x_sb, outs, slot, scratch, pre_waits=()):
    p, C, sb = ctx["p"], ctx["C"], ctx["sb"]
    st = sb["stats"]
    o = slot * 8
    ssum = st[:, o:o+1]; ssq = st[:, o+1:o+2]; mu = st[:, o+2:o+3]
    m2 = st[:, o+3:o+4]; var = st[:, o+4:o+5]; istd = st[:, o+5:o+6]
    nb = st[:, o+6:o+7]; lnv = st[:, o+7:o+8]
    vC, sC = C["vC"], C["sC"]
    p.op("V", lambda e: e.reduce_sum(ssum, x_sb, axis=AX.X), waits=list(pre_waits), inc=vC)
    p.op("S", lambda e: e.activation(scratch, x_sb, AF.Square, accum_out=ssq),
         waits=list(pre_waits), inc=sC)
    svv = sC.n
    p.op("V", lambda e: e.tensor_scalar_mul(mu, ssum, 1.0 / D))
    p.op("V", lambda e: e.tensor_scalar_mul(m2, ssq, 1.0 / D), waits=[(sC, svv)])
    p.op("V", lambda e: e.tensor_mul(var, mu, mu))
    p.op("V", lambda e: e.tensor_sub(var, m2, var))
    p.op("V", lambda e: e.tensor_scalar_add(var, var, 1e-5), inc=vC)
    vstat = vC.n
    p.op("S", lambda e: e.activation(lnv, var, AF.Ln), waits=[(vC, vstat)])
    p.op("S", lambda e: e.activation(istd, lnv, AF.Exp, scale=-0.5), inc=sC)
    sistd = sC.n
    p.op("V", lambda e: e.tensor_mul(nb, mu, istd), waits=[(sC, sistd)])
    p.op("V", lambda e: e.tensor_scalar_mul(nb, nb, -1.0), inc=vC)
    vnb = vC.n
    for ap in outs:
        p.op("V", lambda e, ap=ap: e.tensor_scalar(
            ap, x_sb, istd, nb, ALU.mult, ALU.add),
            waits=[(vC, vnb), (sC, sistd)], inc=vC)
    return sC.n, vC.n

def transpose_tile(ctx, src_bf, dst_fn, ncol, pre_wait=(), bank=7):
    p, C, sb, PS = ctx["p"], ctx["C"], ctx["sb"], ctx["PS"]
    tC, vC = C["tC"], C["vC"]
    done = 0
    j0 = 0
    while j0 < ncol:
        w = min(4, ncol - j0)
        for j in range(w):
            jj = j0 + j
            wl = list(pre_wait) if jj == 0 else []
            if j == 0:
                wl.append((vC, vC.n))
            p.op("T", lambda e, jj=jj, j=j: e.matmul(
                PS(bank)[:, 128*j:128*(j+1)], src_bf[:, 128*jj:128*(jj+1)],
                sb["ident"][:], start=True, stop=True), waits=wl)
        p.grp("T", tC)
        tv = tC.n
        p.op("V", lambda e, j0=j0, w=w, tv=tv: e.tensor_copy(
            dst_fn(j0, w), PS(bank)[:, 0:128*w].rearrange("p (c k) -> p c k", k=128)),
            waits=[(tC, tv)], inc=vC)
        done = tC.n
        j0 += w
    return done

def emit_attn_a(ctx, src, dst):
    nc, io, p, C, sb, mem, PS = (ctx[k] for k in ["nc", "io", "p", "C", "sb", "mem", "PS"])
    dY, dG, tC, sC, vC, gC = (C[k] for k in ["dY", "dG", "tC", "sC", "vC", "gC"])
    mem.reset(ctx["PHASE0"])
    yF = mem.seq("a_yF", [128, 8 * 2048], BF16)
    QF = mem.seq("a_QF", [128, 8 * 2048], BF16)
    KF = mem.seq("a_KF", [128, 8 * 2048], BF16)
    VT = mem.seq("a_VT", [128, 16 * 1024], BF16)
    wqk = mem.seq("a_wqk", [128, 2 * 1024], BF16)
    wvo = mem.seq("a_wvo", [128, 8 * 1024], BF16)
    xt = mem.seq("a_xt", [128, 4 * 1024], F32)
    ybf = mem.seq("a_ybf", [128, 2 * 1024], BF16)
    sinS = mem.seq("a_sinS", [128, 2 * 1024], BF16)
    scr = mem.seq("a_scr", [128, 1024], F32)
    res = mem.seq("a_res", [128, 2 * 1024], F32)
    rowbf0 = mem.seq("a_rowbf0", [1, D], BF16)
    rowbf1 = mem.seq("a_rowbf1", [1, D], BF16)
    sb["rowbf0"], sb["rowbf1"] = rowbf0, rowbf1
    yFr = lambda: yF[:].rearrange("p (c k) -> p c k", k=2048)

    r0 = p.dma("G", lambda e: e.dma_start(out=sb["rowbf0"][:], in_=io["a_bv"][:]))
    r1 = p.dma("G", lambda e: e.dma_start(out=sb["rowbf1"][:], in_=io["a_bo"][:]))
    ldrows = [r0, r1]

    for b in range(B_LOC):
        ln_s, tr_done = {}, {}
        for t in range(NT):
            x_sb = xt[:, 1024*(t % 4):1024*(t % 4 + 1)]
            w = [(sC, ln_s[t-4][0]), (vC, ln_s[t-4][1])] if t >= 4 else []
            ld = p.dma("Y", lambda e, b=b, t=t, x_sb=x_sb: e.dma_start(
                out=x_sb, in_=src[b, 128*t:128*(t+1), :]), waits=w)
            out_bf = ybf[:, 1024*(t % 2):1024*(t % 2 + 1)]
            pw = [ld]
            if t >= 2:
                pw.append((tC, tr_done[t-2]))
            sN, vN = ln_tile(ctx, x_sb, [out_bf], slot=t % 8, scratch=scr[:], pre_waits=pw)
            ln_s[t] = (sN, vN)
            if DBG_TAP == "attn_a" and b == 0:
                p.dma("Y", lambda e, t=t, out_bf=out_bf: e.dma_start(
                    out=io["dbg2"][:, 1024*t:1024*(t+1)], in_=out_bf), waits=[(vC, vN)])
            tr_done[t] = transpose_tile(
                ctx, out_bf, lambda j0, w_, t=t: yFr()[:, j0:j0+w_, 128*t:128*(t+1)],
                ncol=8, pre_wait=[(vC, vN)])
        p.join(["T", "S", "G"], [vC])
        if DBG_TAP == "attn_a" and b == 0:
            p.wait("Y", [(vC, vC.n)])
            p.dma("Y", lambda e: e.dma_start(out=io["dbg"][:], in_=yF[:]))


        wq_hist = []
        for wi, (Wn, OUT, bw) in enumerate((("aWqT", QF, 0), ("aWkT", KF, 1))):
            for oc in range(8):
                idx = wi * 8 + oc
                wb = wqk[:, 1024*(idx % 2):1024*(idx % 2 + 1)]
                w = [(tC, wq_hist[idx-2])] if idx >= 2 else []
                ldw = p.dma("G", lambda e, Wn=Wn, oc=oc, wb=wb: e.dma_start(
                    out=wb.rearrange("p (c k) -> p c k", k=128),
                    in_=io[Wn][:, 128*oc:128*(oc+1)].rearrange("(c p) k -> p c k", p=128)),
                    waits=w)
                for tg in range(4):
                    bank = tg % 2
                    for kc in range(8):
                        w2 = [ldw, (vC, vC.n)] if kc == 0 else []
                        p.op("T", lambda e, wb=wb, tg=tg, kc=kc, bank=bank: e.matmul(
                            PS(bank), wb[:, 128*kc:128*(kc+1)],
                            yFr()[:, kc, 512*tg:512*(tg+1)],
                            start=(kc == 0), stop=(kc == 7)), waits=w2)
                    p.grp("T", tC)
                    mmv = tC.n
                    p.op("V", lambda e, OUT=OUT, oc=oc, tg=tg, bank=bank, bw=bw:
                         e.tensor_scalar_add(
                            OUT[:].rearrange("p (c k) -> p c k", k=2048)[:, oc,
                                                                        512*tg:512*(tg+1)],
                            PS(bank), sb["a_bqk"][:, 2*oc+bw:2*oc+bw+1]),
                        waits=[(tC, mmv)], inc=vC)
                wq_hist.append(tC.n)
        p.join(["T"], [vC])


        ldv = p.dma("G", lambda e: e.dma_start(
            out=wvo[:].rearrange("p (c k) -> p c k", k=1024),
            in_=io["aWvT"][:].rearrange("(c p) k -> p c k", p=128)))
        for t in range(NT):
            for half in range(2):
                bank = (2 * t + half) % 2
                for kc in range(8):
                    w2 = [ldv, (vC, vC.n)] if kc == 0 else []
                    p.op("T", lambda e, t=t, half=half, kc=kc, bank=bank: e.matmul(
                        PS(bank), yFr()[:, kc, 128*t:128*(t+1)],
                        wvo[:, 1024*kc + 512*half: 1024*kc + 512*(half+1)],
                        start=(kc == 0), stop=False), waits=w2)
                p.op("T", lambda e, half=half, bank=bank: e.matmul(
                    PS(bank), sb["onesbf"][:, 0:128], sb["rowbf0"][:, 512*half:512*(half+1)],
                    start=False, stop=True),
                    waits=(ldrows if (t == 0 and half == 0) else []))
                p.grp("T", tC)
                mmv = tC.n
                p.op("V", lambda e, t=t, half=half, bank=bank: e.tensor_copy(
                    VT[:, 1024*t + 512*half: 1024*t + 512*(half+1)], PS(bank)),
                    waits=[(tC, mmv)], inc=vC)
        p.join(["T"], [vC])
        if DBG_TAP == "attn_a" and b == 0:
            p.wait("Y", [(vC, vC.n)])
            p.dma("Y", lambda e: e.dma_start(out=io["dbg3"][:], in_=VT[:]))

        ldwo = p.dma("G", lambda e: e.dma_start(
            out=wvo[:].rearrange("p (c k) -> p c k", k=1024),
            in_=io["aWoT"][:].rearrange("(c p) k -> p c k", p=128)),
            waits=[(vC, vC.n)])
        sin_done, av_done = {}, {}
        for hp in range(8):
            for qg in range(4):
                for kc in range(16):
                    par = kc % 2
                    for which in range(2):
                        bank = 2 * par + which
                        base = 64 * which
                        w = ([(sC, sin_done[(hp, qg, kc - 2)])]
                             if (hp, qg, kc - 2) in sin_done else [])
                        if kc == 0 and which == 0 and (hp, qg) == (0, 0):
                            w = [(sC, sC.n)]
                        p.op("T", lambda e, hp=hp, qg=qg, kc=kc, bank=bank, base=base: e.matmul(
                            PS(bank),
                            KF[base:base+64, 2048*hp + 128*kc: 2048*hp + 128*(kc+1)],
                            QF[base:base+64, 2048*hp + 512*qg: 2048*hp + 512*(qg+1)],
                            start=True, stop=True), waits=w)
                    p.grp("T", tC)
                    smm = tC.n
                    slot = par
                    w = [(tC, smm)]
                    if (hp, qg, kc - 2) in av_done:
                        w.append((tC, av_done[(hp, qg, kc - 2)]))
                    elif kc < 2 and (hp, qg) != (0, 0):
                        w.append((tC, av_prev))
                    p.op("S", lambda e, par=par, slot=slot: e.activation(
                        sinS[:, 1024*slot:1024*(slot+1)], PS(2*par, 2), AF.Sin, scale=3.75),
                        waits=w, inc=sC)
                    sin_done[(hp, qg, kc)] = sC.n
                    for which in range(2):
                        w = [(sC, sin_done[(hp, qg, kc)])] if which == 0 else []
                        if kc == 0 and which == 0:
                            w.append((vC, vC.n))
                        p.op("T", lambda e, hp=hp, kc=kc, which=which, slot=slot: e.matmul(
                            PS(4)[64*which:64*(which+1), :],
                            VT[:, 1024*kc + 128*hp + 64*which: 1024*kc + 128*hp + 64*(which+1)],
                            sinS[:, 1024*slot + 512*which: 1024*slot + 512*(which+1)],
                            start=(kc == 0), stop=(kc == 15)), waits=w)
                    p.grp("T", tC)
                    av_done[(hp, qg, kc)] = tC.n
                av_prev = av_done[(hp, qg, 15)]
                p.op("V", lambda e, hp=hp, qg=qg, v=av_prev: e.tensor_copy(
                    yF[:, 2048*hp + 512*qg: 2048*hp + 512*(qg+1)], PS(4)),
                    waits=[(tC, av_prev)], inc=vC)
        p.join(["T"], [vC])

        add_done = 0
        st_tok = {}
        for t in range(NT):
            x_sb = xt[:, 1024*(t % 2):1024*(t % 2 + 1)]
            w2 = [(vC, add_done)] if t >= 2 else [(vC, vC.n), (sC, sC.n)]
            ldx = p.dma("Y", lambda e, b=b, t=t, x_sb=x_sb: e.dma_start(
                out=x_sb, in_=src[b, 128*t:128*(t+1), :]), waits=w2)
            rslot = t % 2
            for half in range(2):
                bank = 5 + half
                for kc in range(8):
                    w3 = [(vC, vC.n)] if kc == 0 else []
                    if kc == 0 and t == 0 and half == 0:
                        w3.append(ldwo)
                    p.op("T", lambda e, t=t, kc=kc, half=half, bank=bank: e.matmul(
                        PS(bank), yF[:, 2048*kc + 128*t: 2048*kc + 128*(t+1)],
                        wvo[:, 1024*kc + 512*half: 1024*kc + 512*(half+1)],
                        start=(kc == 0), stop=False), waits=w3)
                p.op("T", lambda e, half=half, bank=bank: e.matmul(
                    PS(bank), sb["onesbf"][:, 0:128], sb["rowbf1"][:, 512*half:512*(half+1)],
                    start=False, stop=True))
                p.grp("T", tC)
                mmv = tC.n
                wv = [(tC, mmv), ldx]
                if half == 0 and t >= 2:
                    wv.append(st_tok[t-2])
                p.op("V", lambda e, rslot=rslot, half=half, bank=bank, x_sb=x_sb: e.tensor_add(
                    res[:, 1024*rslot + 512*half: 1024*rslot + 512*(half+1)],
                    x_sb[:, 512*half:512*(half+1)], PS(bank)),
                    waits=wv, inc=vC)
            add_done = vC.n
            st_tok[t] = p.dma("Y", lambda e, b=b, t=t, rslot=rslot: e.dma_start(
                out=dst[b, 128*t:128*(t+1), :], in_=res[:, 1024*rslot:1024*(rslot+1)]),
                waits=[(vC, add_done)])
        p.join(["T", "S", "V", "G", "Y"], [vC, sC, tC])

def emit_ffn(ctx, stream, dst):
    nc, io, p, C, sb, mem, PS = (ctx[k] for k in ["nc", "io", "p", "C", "sb", "mem", "PS"])
    dY, dG, tC, sC, vC, gC = (C[k] for k in ["dY", "dG", "tC", "sC", "vC", "gC"])
    zs = io["zs"]
    NHC = 22
    CLIP = [(0, 128), (0, 256), (0, 384), (0, 512), (0, 512), (128, 512), (256, 512), (384, 512)]

    mem.reset(ctx["PHASE0"])
    hF = mem.seq("f_hF", [128, 8 * 1024], BF16)
    hF_off = mem.mark() - 8 * 1024 * 2
    qpk = mem.seq("f_qpk", [128, 4 * 2048], BF16)
    kpk = mem.seq("f_kpk", [128, 4 * 2048], BF16)
    kpk_off = mem.mark() - 4 * 2048 * 2
    vaug = mem.seq("f_vaug", [128, 16 * 520], BF16)
    wbuf = mem.seq("f_wbuf", [128, 8 * 1536], BF16)
    tabs = mem.seq("f_tabs", [128, 4 * 1024], F32)
    pexp = mem.seq("f_pexp", [128, 2 * 640], BF16)
    ptb = mem.seq("f_ptb", [128, 2 * 640], BF16)
    expSoff_ = mem.mark() - 4 * 640 * 2
    hbf = mem.seq("f_hbf", [128, 2 * 1024], BF16)
    hbf_off = mem.mark() - 2 * 1024 * 2
    hf32 = mem.seq("f_hf32", [128, 3 * 1024], F32)
    hf32_off = mem.mark() - 3 * 1024 * 4
    qkT = mem.seq("f_qkT", [128, 4 * 512], F32)
    qkT_off = mem.mark() - 4 * 512 * 4
    qkbf = mem.seq("f_qkbf", [128, 512], BF16)
    qkbf_off = mem.mark() - 512 * 2
    zupd = mem.seq("f_zupd", [128, 2 * 1024], F32)
    zupd_off = mem.mark() - 2 * 1024 * 4
    scr = mem.seq("f_scr", [128, 1024], F32)
    nrm = mem.seq("f_nrm", [128, 128], F32)
    oF = mem.at("f_oF", [128, 4 * 2048], BF16, hF_off)
    hlnF = mem.at("fB_hlnF", [128, 8 * 2048], BF16, hF_off)
    ohalf = mem.at("fB_ohalf", [128, 11 * 2048], BF16, kpk_off)
    w2buf = mem.at("fB_w2buf", [128, 11 * 1024], BF16, kpk_off + 45056)
    w1bufB = mem.at("fB_w1buf", [128, 2 * 2048], BF16, kpk_off + 45056 + 22528)
    gelt = mem.at("fB_gelt", [128, 512], F32, kpk_off + 45056 + 22528 + 8192)
    hlnbf = mem.at("fB_hlnbf", [128, 1024], BF16, kpk_off + 45056 + 22528 + 10240)

    lngB = mem.at("fB_lng", [128, D], F32, qkT_off)
    lnbB = mem.at("fB_lnb", [128, D], F32, qkT_off + 4096)
    b1cB = mem.at("fB_b1c", [128, 44], F32, qkbf_off)
    row2bfB = sb["row2bf"]
    assert kpk_off + 45056 + 22528 + 10240 + 2048 <= mem.offs["f_hf32"][0], "B alias overlap"
    hFr = lambda: hF[:].rearrange("p (c k) -> p c k", k=1024)
    hlnFr = lambda: hlnF[:].rearrange("p (c k) -> p c k", k=2048)
    tbr = lambda ti: tabs[:, 1024*ti:1024*(ti+1)].rearrange("p (t k) -> p t k", k=64)

    final_phase = dst is not zs and dst is not None

    for b in range(B_LOC):
        zwr, ln_s = {}, {}
        for t in range(NT):
            x_sb = hf32[:, 1024*(t % 3):1024*(t % 3 + 1)]
            w = [(sC, ln_s[t-3][0]), (vC, ln_s[t-3][1])] if t >= 3 else []
            ld = p.dma("Y", lambda e, b=b, t=t, x_sb=x_sb: e.dma_start(
                out=x_sb, in_=stream[b, 128*t:128*(t+1), :]), waits=w)
            z_sb = zupd[:, 1024*(t % 2):1024*(t % 2 + 1)]
            pw = [ld]
            if t >= 2:
                pw.append(zwr[t-2])
            sN, vN = ln_tile(ctx, x_sb, [z_sb], slot=t % 8, scratch=scr[:], pre_waits=pw)
            ln_s[t] = (sN, vN)
            zwr[t] = p.dma("Y", lambda e, b=b, t=t, z_sb=z_sb: e.dma_start(
                out=zs[b, 128*t:128*(t+1), :], in_=z_sb), waits=[(vC, vN)])

        for layer in range(2):
            final = final_phase and layer == 1
            tabtoks = []
            tabtoks.append(p.dma("Y", lambda e, layer=layer: e.dma_start(
                out=sb["qs8"][:], in_=io["f_qs8"][layer:layer+1, :].broadcast_to([128, 512])),
                waits=[(vC, vC.n)]))
            tabtoks.append(p.dma("Y", lambda e, layer=layer: e.dma_start(
                out=sb["ks8"][:], in_=io["f_ks8"][layer:layer+1, :].broadcast_to([128, 512]))))
            for (ti, rawname) in ((0, "rot_qc"), (1, "rot_qs"), (2, "rot_kc"), (3, "rot_ks")):
                tabtoks.append(p.dma("Y", lambda e, rawname=rawname, ti=ti: e.dma_start(
                    out=tbr(ti), in_=io[rawname][:].rearrange("(t p) k -> p t k", p=128)),
                    waits=[(vC, vC.n)] if ti == 0 else []))
            for ti, sc in ((0, "qs8"), (2, "ks8")):
                p.op("V", lambda e, ti=ti, sc=sc: e.tensor_tensor(
                    tbr(ti), tbr(ti),
                    sb[sc][:, 0:64].unsqueeze(1).broadcast_to([128, 16, 64]), ALU.mult),
                    waits=tabtoks, inc=vC)
            for ti, sc in ((1, "qs8"), (3, "ks8")):
                p.op("V", lambda e, ti=ti, sc=sc: e.tensor_tensor(
                    tbr(ti)[:, :, 0:32], tbr(ti)[:, :, 0:32],
                    sb[sc][:, 32:64].unsqueeze(1).broadcast_to([128, 16, 32]), ALU.mult),
                    waits=tabtoks)
                p.op("V", lambda e, ti=ti, sc=sc: e.tensor_tensor(
                    tbr(ti)[:, :, 32:64], tbr(ti)[:, :, 32:64],
                    sb[sc][:, 0:32].unsqueeze(1).broadcast_to([128, 16, 32]), ALU.mult),
                    inc=vC)
            tab_done = vC.n
            ld_wqkv = p.dma("G", lambda e, layer=layer: e.dma_start(
                out=wbuf[:].rearrange("p (c k) -> p c k", k=1536),
                in_=io["f_wqkv"][layer].rearrange("(c p) k -> p c k", p=128)),
                waits=[(tC, tC.n)])
            for t in range(NT):
                p.op("G", lambda e, t=t: e.memset(
                    vaug[:, 520*t:520*(t+1)].rearrange("p (g k) -> p g k", k=65)[:, :, 64:65],
                    1.0), inc=gC)
            neg_done = gC.n

            tr_done = {}
            for half in range(2):
                for tl in range(8):
                    t = 8 * half + tl
                    h_sb = hbf[:, 1024*(t % 2):1024*(t % 2 + 1)]
                    w = [(tC, tr_done[t-2])] if t >= 2 else [(vC, tab_done)]
                    w += [zwr[t], zwr[max(t-1, 0)]]
                    ldh1 = p.dma("G", lambda e, b=b, t=t, h_sb=h_sb: e.dma_start(
                        out=h_sb[:, 0:512], in_=zs[b, 128*t:128*(t+1), 0:512]), waits=w)
                    if t == 0:
                        p.op("G", lambda e, h_sb=h_sb: e.memset(h_sb[0:1, 512:1024], 0.0),
                             inc=gC)
                        ldh2 = p.dma("G", lambda e, b=b, h_sb=h_sb: e.dma_start(
                            out=h_sb[1:128, 512:1024], in_=zs[b, 0:127, 512:1024]))
                    else:
                        ldh2 = p.dma("G", lambda e, b=b, t=t, h_sb=h_sb: e.dma_start(
                            out=h_sb[:, 512:1024], in_=zs[b, 128*t-1:128*(t+1)-1, 512:1024]))
                    tr_done[t] = transpose_tile(
                        ctx, h_sb, lambda j0, w_, tl=tl: hFr()[:, j0:j0+w_, 128*tl:128*(tl+1)],
                        ncol=8, pre_wait=[ldh1, ldh2, (gC, gC.n)])
                if FFN_DBG_STOP == f"A1h{half}":
                    p.join(["T", "S", "V", "G", "Y"], [tC, sC, vC, gC])
                    return
                for tl in range(8):
                    t = 8 * half + tl
                    for pi in range(3):
                        bank = pi
                        for kc in range(8):
                            w2 = []
                            if kc == 0:
                                w2 = [(vC, vC.n), (sC, sC.n)]
                                if tl == 0 and pi == 0:
                                    w2 += [ld_wqkv, (tC, tr_done[8*half+7])]
                            p.op("T", lambda e, tl=tl, pi=pi, kc=kc, bank=bank: e.matmul(
                                PS(bank), hFr()[:, kc, 128*tl:128*(tl+1)],
                                wbuf[:, 1536*kc + 512*pi: 1536*kc + 512*(pi+1)],
                                start=(kc == 0), stop=(kc == 7)), waits=w2)
                        p.grp("T", tC)
                        mmv = tC.n
                        if pi == 2:
                            p.op("V", lambda e, t=t, bank=bank: e.tensor_copy(
                                vaug[:, 520*t:520*(t+1)].rearrange(
                                    "p (g k) -> p g k", k=65)[:, :, 0:64],
                                PS(bank).rearrange("p (g k) -> p g k", k=64)),
                                waits=[(tC, mmv)], inc=vC)
                        else:
                            p.op("V", lambda e, pi=pi, bank=bank: e.tensor_copy(
                                qkT[:, 512*pi:512*(pi+1)], PS(bank)),
                                waits=[(tC, mmv)], inc=vC)
                    qk_ready = vC.n
                    if FFN_DBG_STOP == f"A2h{half}t{tl}":
                        p.join(["T", "S", "V", "G", "Y"], [tC, sC, vC, gC])
                        return
                    for (pi, TA, TB, dstc) in ((0, 0, 1, qpk), (1, 2, 3, kpk)):
                        src3 = qkT[:, 512*pi:512*(pi+1)].rearrange("p (h k) -> p h k", k=64)
                        u3 = qkT[:, 1024:1536].rearrange("p (h k) -> p h k", k=64)
                        w3 = qkT[:, 1536:2048].rearrange("p (h k) -> p h k", k=64)
                        ss = nrm[:, (t % 4) * 16 + 8 * pi:(t % 4) * 16 + 8 * pi + 8]
                        p.op("S", lambda e, pi=pi: e.activation(
                            scr[:, 0:512], qkT[:, 512*pi:512*(pi+1)], AF.Square),
                            waits=[(vC, qk_ready)], inc=sC)
                        sqv = sC.n
                        p.op("V", lambda e, ss=ss: e.reduce_sum(
                            ss, scr[:, 0:512].rearrange("p (h k) -> p h k", k=64), axis=AX.X),
                            waits=[(sC, sqv)])
                        p.op("V", lambda e, ss=ss: e.tensor_scalar_max(ss, ss, 1e-24), inc=vC)
                        ssv = vC.n
                        p.op("S", lambda e, ss=ss: e.activation(ss, ss, AF.Ln),
                             waits=[(vC, ssv)])
                        p.op("S", lambda e, ss=ss: e.activation(
                            ss, ss, AF.Exp, scale=-0.5), inc=sC)
                        sexp = sC.n
                        if pi == 0:
                            p.op("V", lambda e, ss=ss: e.tensor_scalar_mul(ss, ss, 8.0),
                                 waits=[(sC, sexp)], inc=vC)
                            inv_ctr, inv_done = vC, vC.n
                        else:
                            inv_ctr, inv_done = sC, sexp
                        TAv = tbr(TA)[:, t % 16, :]
                        TBv = tbr(TB)[:, t % 16, :]
                        p.op("V", lambda e, u3=u3, src3=src3, TAv=TAv: e.tensor_tensor(
                            u3, src3, TAv.unsqueeze(1).broadcast_to([128, 8, 64]), ALU.mult))
                        p.op("V", lambda e, w3=w3, src3=src3, TBv=TBv: e.tensor_tensor(
                            w3[:, :, 0:32], src3[:, :, 32:64],
                            TBv[:, 0:32].unsqueeze(1).broadcast_to([128, 8, 32]), ALU.mult))
                        p.op("V", lambda e, w3=w3, src3=src3, TBv=TBv: e.tensor_tensor(
                            w3[:, :, 32:64], src3[:, :, 0:32],
                            TBv[:, 32:64].unsqueeze(1).broadcast_to([128, 8, 32]), ALU.mult))
                        p.op("V", lambda e, u3=u3, w3=w3: e.tensor_sub(
                            u3[:, :, 0:32], u3[:, :, 0:32], w3[:, :, 0:32]))
                        p.op("V", lambda e, u3=u3, w3=w3: e.tensor_add(
                            u3[:, :, 32:64], u3[:, :, 32:64], w3[:, :, 32:64]))
                        p.op("V", lambda e, u3=u3, ss=ss: e.tensor_tensor(
                            u3, u3, ss.unsqueeze(2).broadcast_to([128, 8, 64]), ALU.mult),
                            waits=[(inv_ctr, inv_done)], inc=vC)
                        rot_v = vC.n
                        p.op("V", lambda e: e.tensor_copy(qkbf[:], qkT[:, 1024:1536]),
                             waits=[(vC, rot_v), (tC, tC.n)], inc=vC)
                        cpv = vC.n
                        for j in range(4):
                            w4 = [(vC, cpv), (sC, sC.n)] if j == 0 else []
                            p.op("T", lambda e, j=j: e.matmul(
                                PS(6)[:, 128*j:128*(j+1)], qkbf[:, 128*j:128*(j+1)],
                                sb["ident"][:], start=True, stop=True), waits=w4)
                        p.grp("T", tC)
                        trv = tC.n
                        p.op("S", lambda e, dstc=dstc, t=t: e.activation(
                            dstc[:].rearrange("p (c k) -> p c k", k=2048)[:, :, 128*t:128*(t+1)],
                            PS(6)[:].rearrange("p (c k) -> p c k", k=128), AF.Copy),
                            waits=[(tC, trv)], inc=sC)
            p.join(["T"], [sC, vC, gC])
            if FFN_DBG_STOP == "A3":
                return

            phexp = [None, None]
            for h in range(8):
                hb, hc2 = 64 * (h % 2), h // 2
                prev_exp = 0
                state = {}
                ptev_hist = {-1: (vC, 0)}
                oev_hist = {-1: 0}

                def emit_S(qc, h=h, hb=hb, hc2=hc2):
                    kcs = list(range(max(0, qc - 4), qc + 1))
                    nk = len(kcs)
                    W_ = 128 * nk
                    sb0 = 2 * (qc % 2)
                    pslot = qc % 2
                    stat = nrm[:, 16*pslot:16*pslot+4]
                    st_ = state[qc] = dict(kcs=kcs, nk=nk, W_=W_, sb0=sb0, pslot=pslot,
                                           stat=stat)
                    if qc >= 2:
                        w = [(sC, state[qc-2]["exp"])]
                    elif phexp[0] is not None:
                        w = [(sC, phexp[qc])]
                    else:
                        w = [(sC, sC.n)]
                    for ji, kc in enumerate(kcs):
                        p.op("T", lambda e, hb=hb, hc2=hc2, qc=qc, kc=kc, ji=ji,
                             sb0=sb0: e.matmul(
                            PS(sb0, 2)[:, 128*ji:128*(ji+1)],
                            qpk[hb:hb+64, 2048*hc2 + 128*qc: 2048*hc2 + 128*(qc+1)],
                            kpk[hb:hb+64, 2048*hc2 + 128*kc: 2048*hc2 + 128*(kc+1)],
                            start=True, stop=True),
                            waits=(w if ji == 0 else []))
                        p.grp("T", tC)
                    st_["smm"] = tC.n

                def emit_softmax(qc):
                    st_ = state[qc]
                    sb0, W_, nk, pslot, stat = (st_[k] for k in
                                                ("sb0", "W_", "nk", "pslot", "stat"))
                    smm = st_["smm"]
                    if qc >= 4:
                        p.op("V", lambda e, sb0=sb0: e.tensor_add(
                            PS(sb0, 2)[:, 0:128], PS(sb0, 2)[:, 0:128],
                            sb["maskadd"][:, 0:128]), waits=[(tC, smm)])
                    p.op("V", lambda e, sb0=sb0, nk=nk: e.tensor_add(
                        PS(sb0, 2)[:, 128*(nk-1):128*nk], PS(sb0, 2)[:, 128*(nk-1):128*nk],
                        sb["maskadd"][:, 128:256]), waits=[(tC, smm)])
                    p.op("V", lambda e, sb0=sb0, W_=W_, stat=stat: e.reduce_max(
                        stat[:, 0:1], PS(sb0, 2)[:, 0:W_], axis=AX.X),
                         waits=([state[qc-2]["pnorm"]] if qc >= 2 else [(sC, sC.n)]))
                    p.op("V", lambda e, stat=stat: e.tensor_scalar_mul(
                        stat[:, 1:2], stat[:, 0:1], -1.0), inc=vC)
                    nmv = vC.n
                    ptw = state[qc-2]["ptv"] if qc >= 2 else 0
                    p.op("S", lambda e, sb0=sb0, W_=W_, stat=stat, pslot=pslot: e.activation(
                        pexp[:, 640*pslot:640*pslot + W_], PS(sb0, 2)[:, 0:W_], AF.Exp,
                        bias=stat[:, 1:2], accum_out=stat[:, 2:3]),
                        waits=[(vC, nmv), (tC, ptw)], inc=sC)
                    st_["exp"] = sC.n
                    p.op("V", lambda e, stat=stat: e.reciprocal(stat[:, 3:4], stat[:, 2:3]),
                         waits=[(sC, st_["exp"])], inc=vC)
                    rcv = vC.n
                    p.op("S", lambda e, W_=W_, stat=stat, pslot=pslot: e.activation(
                        pexp[:, 640*pslot:640*pslot + W_], pexp[:, 640*pslot:640*pslot + W_],
                        AF.Copy, scale=stat[:, 3:4]), waits=[(vC, rcv)], inc=sC)
                    st_["pnorm"] = (sC, sC.n)

                def emit_tail(qc, h=h):
                    st_ = state[qc]
                    nk, W_, pslot, kcs = (st_[k] for k in ("nk", "W_", "pslot", "kcs"))
                    for ji in range(nk):
                        w5 = ([st_["pnorm"], ptev_hist[qc-1]]
                              if ji == 0 else [])
                        p.op("T", lambda e, ji=ji, pslot=pslot: e.matmul(
                            PS(6, 2)[:, 128*ji:128*(ji+1)],
                            pexp[:, 640*pslot + 128*ji: 640*pslot + 128*(ji+1)],
                            sb["ident"][:], start=True, stop=True), waits=w5)
                        p.grp("T", tC)
                    st_["ptv"] = tC.n
                    if qc % 2 == 0:
                        p.op("V", lambda e, W_=W_, pslot=pslot: e.tensor_copy(
                            ptb[:, 640*pslot:640*pslot + W_], PS(6, 2)[:, 0:W_]),
                            waits=[(tC, st_["ptv"])], inc=vC)
                        ptev_hist[qc] = (vC, vC.n)
                    else:
                        p.op("S", lambda e, W_=W_, pslot=pslot: e.activation(
                            ptb[:, 640*pslot:640*pslot + W_], PS(6, 2)[:, 0:W_], AF.Copy),
                            waits=[(tC, st_["ptv"])], inc=sC)
                        ptev_hist[qc] = (sC, sC.n)
                    for ji, kc in enumerate(kcs):
                        w6 = ([ptev_hist[qc], (vC, oev_hist[qc-1])]
                              if ji == 0 else [])
                        p.op("T", lambda e, kc=kc, h=h, ji=ji, pslot=pslot,
                             first=(ji == 0), last=(ji == nk - 1): e.matmul(
                            PS(5)[0:64, 0:128],
                            vaug[:, 520*kc + 65*h: 520*kc + 65*h + 64],
                            ptb[:, 640*pslot + 128*ji: 640*pslot + 128*(ji+1)],
                            start=first, stop=last), waits=w6)
                        p.grp("T", tC)
                    avv = tC.n
                    p.op("V", lambda e, h=h, qc=qc: e.tensor_copy(
                        oF[64*(h % 2):64*(h % 2)+64,
                           2048*(h // 2) + 128*qc: 2048*(h // 2) + 128*(qc+1)],
                        PS(5)[0:64, 0:128]), waits=[(tC, avv)], inc=vC)
                    oev_hist[qc] = vC.n

                for qc in range(NT):
                    emit_S(qc)
                    emit_softmax(qc)
                    if qc >= 1:
                        emit_tail(qc - 1)
                emit_tail(NT - 1)
                phexp = [state[14]["exp"], state[15]["exp"]]
            p.join(["T"], [vC, sC])

            ldwo = p.dma("G", lambda e, layer=layer: e.dma_start(
                out=wbuf[:, 0:4*1024].rearrange("p (c k) -> p c k", k=1024),
                in_=io["f_wo"][layer].rearrange("(c p) k -> p c k", p=128)),
                waits=[(tC, tC.n)])
            zw2 = {}
            upd_v = 0
            for t in range(NT):
                z_sb = zupd[:, 1024*(t % 2):1024*(t % 2 + 1)]
                w = ([(vC, upd_v), zw2[t-2]] if t >= 2
                     else [(sC, sC.n)] + p.dma_all()) + [zwr[t]]
                ldz = p.dma("Y", lambda e, b=b, t=t, z_sb=z_sb: e.dma_start(
                    out=z_sb, in_=zs[b, 128*t:128*(t+1), :]), waits=w)
                for half in range(2):
                    bank = half
                    for kc in range(4):
                        w2 = [(vC, vC.n)] if kc == 0 else []
                        if kc == 0 and t == 0 and half == 0:
                            w2.append(ldwo)
                        p.op("T", lambda e, t=t, kc=kc, half=half, bank=bank: e.matmul(
                            PS(bank), oF[:, 2048*kc + 128*t: 2048*kc + 128*(t+1)],
                            wbuf[:, 1024*kc + 512*half: 1024*kc + 512*(half+1)],
                            start=(kc == 0), stop=(kc == 3)), waits=w2)
                    p.grp("T", tC)
                    mmv = tC.n
                    p.op("V", lambda e, z_sb=z_sb, half=half, bank=bank: e.tensor_add(
                        z_sb[:, 512*half:512*(half+1)], z_sb[:, 512*half:512*(half+1)],
                        PS(bank)), waits=[(tC, mmv), ldz], inc=vC)
                upd_v = vC.n
                zw2[t] = p.dma("Y", lambda e, b=b, t=t, z_sb=z_sb: e.dma_start(
                    out=zs[b, 128*t:128*(t+1), :], in_=z_sb), waits=[(vC, upd_v)])
            p.join(["T", "S", "V", "G"], [])

            lnctoks = []
            lnctoks.append(p.dma("Y", lambda e, layer=layer: e.dma_start(
                out=lngB[:], in_=io["f_lng"][layer].broadcast_to([128, D])),
                waits=[(sC, sC.n), (vC, vC.n)]))
            lnctoks.append(p.dma("Y", lambda e, layer=layer: e.dma_start(
                out=lnbB[:], in_=io["f_lnb"][layer].broadcast_to([128, D]))))
            lnctoks.append(p.dma("Y", lambda e, layer=layer: e.dma_start(
                out=b1cB[:], in_=io["f_b1c"][layer])))
            ldb2 = p.dma("G", lambda e, layer=layer: e.dma_start(
                out=row2bfB[:], in_=io["f_b2"][layer]), waits=[(tC, tC.n)])
            trB = {}
            hlv = 0
            for t in range(NT):
                h_sb = hf32[:, 1024*(t % 3):1024*(t % 3 + 1)]
                w = [(tC, trB[t-3])] if t >= 3 else []
                w += [zw2[t], zw2[max(t-1, 0)]]
                ldh1 = p.dma("G", lambda e, b=b, t=t, h_sb=h_sb: e.dma_start(
                    out=h_sb[:, 0:512], in_=zs[b, 128*t:128*(t+1), 0:512]), waits=w)
                if t == 0:
                    p.op("G", lambda e, h_sb=h_sb: e.memset(h_sb[0:1, 512:1024], 0.0), inc=gC)
                    ldh2 = p.dma("G", lambda e, b=b, h_sb=h_sb: e.dma_start(
                        out=h_sb[1:128, 512:1024], in_=zs[b, 0:127, 512:1024]))
                else:
                    ldh2 = p.dma("G", lambda e, b=b, t=t, h_sb=h_sb: e.dma_start(
                        out=h_sb[:, 512:1024], in_=zs[b, 128*t-1:128*(t+1)-1, 512:1024]))
                pw = [ldh1, ldh2, (gC, gC.n), (vC, hlv)] + lnctoks
                sN, vN = ln_tile(ctx, h_sb, [scr[:]], slot=t % 8,
                                 scratch=zupd[:, 0:1024], pre_waits=pw)
                p.op("V", lambda e: e.tensor_mul(scr[:], scr[:], lngB[:]))
                p.op("V", lambda e: e.tensor_add(scr[:], scr[:], lnbB[:]))
                p.op("V", lambda e: e.tensor_copy(hlnbf[:], scr[:]), inc=vC)
                hlv = vC.n
                trB[t] = transpose_tile(
                    ctx, hlnbf, lambda j0, w_, t=t: hlnFr()[:, j0:j0+w_, 128*t:128*(t+1)],
                    ncol=8, pre_wait=[(vC, hlv)])
            p.join(["T"], [vC])

            zw3 = dict(zw2)
            for hhalf in range(2):
                chunks = list(range(11 * hhalf, min(11 * (hhalf + 1), NHC)))
                g_done = 0
                for ci, c in enumerate(chunks):
                    w1b = w1bufB[:, 2048*(ci % 2):2048*(ci % 2 + 1)]
                    w = [(tC, g_done)] if ci >= 2 else [(tC, tC.n)]
                    ldw1 = p.dma("G", lambda e, c=c, layer=layer, w1b=w1b: e.dma_start(
                        out=w1b.rearrange("p (ic k) -> p ic k", k=256),
                        in_=io["f_w1p"][layer][:, 256*c:256*(c+1)].rearrange(
                            "(ic p) k -> p ic k", p=128)), waits=w)
                    for tg in range(4):
                        abank = tg % 2
                        gbank = 2 + tg % 2
                        for kc in range(8):
                            w2 = [(vC, vC.n), ldw1] if kc == 0 else []
                            p.op("T", lambda e, kc=kc, tg=tg, abank=abank, w1b=w1b: e.matmul(
                                PS(abank), w1b[:, 256*kc:256*kc+128],
                                hlnFr()[:, kc, 512*tg:512*(tg+1)],
                                start=(kc == 0), stop=(kc == 7)), waits=w2)
                        for kc in range(8):
                            w2 = [(sC, sC.n)] if kc == 0 else []
                            p.op("T", lambda e, kc=kc, tg=tg, gbank=gbank, w1b=w1b: e.matmul(
                                PS(gbank), w1b[:, 256*kc+128:256*kc+256],
                                hlnFr()[:, kc, 512*tg:512*(tg+1)],
                                start=(kc == 0), stop=(kc == 7)), waits=w2)
                        p.grp("T", tC)
                        mmv = tC.n
                        p.op("S", lambda e, gbank=gbank, c=c: e.activation(
                            gelt[:], PS(gbank), AF.Gelu, bias=b1cB[:, 2*c+1:2*c+2]),
                            waits=[(tC, mmv)], inc=sC)
                        gel = sC.n
                        p.op("V", lambda e, abank=abank, c=c: e.tensor_scalar_add(
                            scr[:, 0:512], PS(abank), b1cB[:, 2*c:2*c+1]),
                            waits=[(tC, mmv)])
                        p.op("V", lambda e, ci=ci, tg=tg: e.tensor_mul(
                            ohalf[:, 2048*ci + 512*tg: 2048*ci + 512*(tg+1)],
                            scr[:, 0:512], gelt[:]), waits=[(sC, gel)], inc=vC)
                    g_done = tC.n
                ldw2 = p.dma("G", lambda e, layer=layer, hhalf=hhalf, nch=len(chunks):
                             e.dma_start(
                    out=w2buf[:, 0:1024*nch].rearrange("p (c k) -> p c k", k=1024),
                    in_=io["f_w2p"][layer, 11*hhalf:11*hhalf+nch].rearrange("c p k -> p c k")),
                    waits=[(tC, tC.n)])
                p.join(["T"], [vC])
                updB = 0
                st3 = {}
                for t in range(NT):
                    z_sb = zupd[:, 1024*(t % 2):1024*(t % 2 + 1)]
                    w = ([(vC, updB), st3[t-2]] if t >= 2
                         else [(sC, sC.n)] + p.dma_all())
                    ldz = p.dma("Y", lambda e, b=b, t=t, z_sb=z_sb: e.dma_start(
                        out=z_sb, in_=zs[b, 128*t:128*(t+1), :]),
                        waits=w + [zw3[t]])
                    ldx = None
                    if final and hhalf == 1:
                        x_sb = hf32[:, 1024*(t % 2):1024*(t % 2 + 1)]
                        ldx = p.dma("Y", lambda e, b=b, t=t, x_sb=x_sb: e.dma_start(
                            out=x_sb, in_=stream[b, 128*t:128*(t+1), :]),
                            waits=([(sC, sC.n)] if t < 3 else []))
                    for half in range(2):
                        bank = 4 + half
                        nch = len(chunks)
                        for ci in range(nch):
                            w2 = [(vC, vC.n), ldw2] if ci == 0 else []
                            p.op("T", lambda e, ci=ci, t=t, half=half, bank=bank,
                                 last=(ci == nch - 1 and hhalf == 0): e.matmul(
                                PS(bank), ohalf[:, 2048*ci + 128*t: 2048*ci + 128*(t+1)],
                                w2buf[:, 1024*ci + 512*half: 1024*ci + 512*(half+1)],
                                start=(ci == 0), stop=last), waits=w2)
                        if hhalf == 1:
                            p.op("T", lambda e, half=half, bank=bank: e.matmul(
                                PS(bank), sb["onesbf"][:, 0:128],
                                row2bfB[:, 512*half:512*(half+1)],
                                start=False, stop=True),
                                waits=([ldb2] if (t == 0 and half == 0) else []))
                        p.grp("T", tC)
                        mmv = tC.n
                        if final and hhalf == 1:
                            p.op("V", lambda e, z_sb=z_sb, half=half, bank=bank: e.tensor_add(
                                z_sb[:, 512*half:512*(half+1)],
                                z_sb[:, 512*half:512*(half+1)], PS(bank)),
                                waits=[(tC, mmv), ldz])
                            p.op("V", lambda e, z_sb=z_sb, half=half, x_sb=x_sb: e.tensor_add(
                                z_sb[:, 512*half:512*(half+1)],
                                z_sb[:, 512*half:512*(half+1)],
                                x_sb[:, 512*half:512*(half+1)]),
                                waits=[ldx], inc=vC)
                        else:
                            p.op("V", lambda e, z_sb=z_sb, half=half, bank=bank: e.tensor_add(
                                z_sb[:, 512*half:512*(half+1)],
                                z_sb[:, 512*half:512*(half+1)], PS(bank)),
                                waits=[(tC, mmv), ldz], inc=vC)
                    updB = vC.n
                    outdst = dst if (final and hhalf == 1) else zs
                    zw3[t] = st3[t] = p.dma("Y", lambda e, b=b, t=t, z_sb=z_sb,
                                            outdst=outdst: e.dma_start(
                        out=outdst[b, 128*t:128*(t+1), :], in_=z_sb),
                        waits=[(vC, updB)])
                p.join(["T", "S", "V", "G"], [])
            zwr = dict(zw3)
        p.join(["T", "S", "V", "G", "Y"], [vC, sC, tC, gC])

def emit_attn_b(ctx, src, dst):
    nc, io, p, C, sb, mem, PS = (ctx[k] for k in ["nc", "io", "p", "C", "sb", "mem", "PS"])
    dY, dG, tC, sC, vC, gC = (C[k] for k in ["dY", "dG", "tC", "sC", "vC", "gC"])
    es = io["zs"]
    SCALE = 0.25

    mem.reset(ctx["PHASE0"])
    eF = mem.seq("b_eF", [128, 8 * 2048], BF16)
    eF_off = mem.mark() - 32 * 1024
    qpad = mem.seq("b_qpad", [128, 4 * 2048], BF16)
    kpad = mem.seq("b_kpad", [128, 4 * 2048], BF16)
    vaug = mem.seq("b_vaug", [128, 16 * 1040], BF16)
    wvp = mem.seq("b_wvp", [128, 8 * 1024], BF16)
    qkraw = mem.seq("b_qkraw", [128, 4 * 2048], BF16)
    et = mem.seq("b_et", [128, 4 * 1024], F32)
    qn = mem.seq("b_qn", [8, 4096], BF16)
    kn = mem.seq("b_kn", [8, 4096], BF16)
    sqb = mem.seq("b_sqb", [128, 2048], BF16)
    scr = mem.seq("b_scr", [128, 1024], F32)
    maxk1 = mem.seq("b_maxk1", [1, 16], F32)
    ebf = mem.seq("b_ebf", [128, 1024], BF16)
    wq1 = mem.seq("b_wq1", [128, 2 * 1024], BF16)
    w1buf = mem.seq("b_w1buf", [128, 8 * 64], BF16)
    expS = mem.at("b_expS", [128, 2 * 2048], BF16, eF_off)
    outF = mem.at("b_outF", [128, 8 * 512], BF16, eF_off + 8 * 1024)
    resF = mem.at("b_resF", [128, 8 * 512], BF16, eF_off + 16 * 1024)
    obf = mem.at("b_obf", [128, 2048], BF16, eF_off + 24 * 1024)
    hW = mem.at("b_hW", [65, 512], BF16, eF_off + 28 * 1024)
    rowstb = mem.at("b_rowst", [1, 2048], BF16, eF_off)
    rowstb2 = mem.at("b_rowst2", [1, 2048], BF16, eF_off + 4096)
    eFr = lambda: eF[:].rearrange("p (c k) -> p c k", k=2048)

    for b in range(B_LOC):
        ewr, ln_s = {}, {}
        for t in range(NT):
            x_sb = et[:, 1024*(t % 4):1024*(t % 4 + 1)]
            w = [(sC, ln_s[t-4][0]), (vC, ln_s[t-4][1])] if t >= 4 else []
            ld = p.dma("Y", lambda e, b=b, t=t, x_sb=x_sb: e.dma_start(
                out=x_sb, in_=src[b, 128*t:128*(t+1), :]), waits=w)
            z_sb = scr[:]
            pw = [ld]
            if t >= 1:
                pw.append(ewr[t-1])
            sN, vN = ln_tile(ctx, x_sb, [z_sb], slot=t % 8,
                             scratch=sqb[:].bitcast(F32), pre_waits=pw)
            ln_s[t] = (sN, vN)
            ewr[t] = p.dma("Y", lambda e, b=b, t=t, z_sb=z_sb: e.dma_start(
                out=es[b, 128*t:128*(t+1), :], in_=z_sb), waits=[(vC, vN)])

        negd = []
        for cch in range(16):
            g_, si = cch // 4, cch % 4
            negd.append(p.dma("Y", lambda e, g_=g_, si=si: e.dma_start(
                out=kpad[32*si+16:32*si+17, 2048*g_:2048*(g_+1)], in_=io["negrow"][:])))

        for it in range(2):
            tr_done = {}
            for t in range(NT):
                w = [(tC, tr_done[t-1])] if t >= 1 else list(negd)
                w += [ewr[t]]
                ldt = p.dma("G", lambda e, b=b, t=t: e.dma_start(
                    out=ebf[:], in_=es[b, 128*t:128*(t+1), :]), waits=w)
                tr_done[t] = transpose_tile(
                    ctx, ebf, lambda j0, w_, t=t: eFr()[:, j0:j0+w_, 128*t:128*(t+1)],
                    ncol=8, pre_wait=[ldt])
            p.join(["T"], [vC])

            for wi, Wn in enumerate(("bWqT", "bWkT")):
                for oc in range(2):
                    idx = 2 * wi + oc
                    wb = wq1[:, 1024*(idx % 2):1024*(idx % 2 + 1)]
                    w = [(tC, qk_hist)] if idx >= 2 else [(tC, tC.n)]
                    ldw = p.dma("G", lambda e, Wn=Wn, oc=oc, wb=wb: e.dma_start(
                        out=wb.rearrange("p (c k) -> p c k", k=128),
                        in_=io[Wn][:, 128*oc:128*(oc+1)].rearrange("(c p) k -> p c k", p=128)),
                        waits=w)
                    for tg in range(4):
                        bank = tg % 2
                        for kc in range(8):
                            w2 = [ldw, (sC, sC.n)] if kc == 0 else []
                            p.op("T", lambda e, wb=wb, tg=tg, kc=kc, bank=bank: e.matmul(
                                PS(bank), wb[:, 128*kc:128*(kc+1)],
                                eFr()[:, kc, 512*tg:512*(tg+1)],
                                start=(kc == 0), stop=(kc == 7)), waits=w2)
                        p.grp("T", tC)
                        mmv = tC.n
                        sc_ = SCALE if wi == 0 else 1.0
                        p.op("S", lambda e, wi=wi, oc=oc, tg=tg, bank=bank, sc_=sc_: e.activation(
                            qkraw[:, 2048*(2*wi+oc) + 512*tg: 2048*(2*wi+oc) + 512*(tg+1)],
                            PS(bank), AF.Copy, scale=sc_), waits=[(tC, mmv)], inc=sC)
                    qk_hist = tC.n
            ldv = p.dma("G", lambda e: e.dma_start(
                out=wvp[:].rearrange("p (c k) -> p c k", k=1024),
                in_=io["bWvT"][:].rearrange("(c p) k -> p c k", p=128)),
                waits=[(tC, tC.n)])
            for t in range(NT):
                for half in range(2):
                    bank = (2 * t + half) % 2
                    for kc in range(8):
                        w2 = [ldv, (vC, vC.n), (sC, sC.n)] if kc == 0 else []
                        p.op("T", lambda e, t=t, half=half, kc=kc, bank=bank: e.matmul(
                            PS(bank), eFr()[:, kc, 128*t:128*(t+1)],
                            wvp[:, 1024*kc + 512*half: 1024*kc + 512*(half+1)],
                            start=(kc == 0), stop=(kc == 7)), waits=w2)
                    p.grp("T", tC)
                    mmv = tC.n
                    p.op("V", lambda e, t=t, half=half, bank=bank: e.tensor_copy(
                        vaug[:, 1040*t:1040*(t+1)].rearrange(
                            "p (g k) -> p g k", k=65)[:, 8*half:8*half+8, 0:64],
                        PS(bank).rearrange("p (g k) -> p g k", k=64)),
                        waits=[(tC, mmv)], inc=vC)
            for t in range(NT):
                p.op("G", lambda e, t=t: e.memset(
                    vaug[:, 1040*t:1040*(t+1)].rearrange("p (g k) -> p g k", k=65)[:, :, 64:65],
                    1.0), inc=gC)
            vones = gC.n
            p.join(["T"], [vC, sC])

            for wi, (rawbase, dstn) in enumerate(((0, qn), (2, kn))):
                for ch in range(2):
                    p.op("S", lambda e, rawbase=rawbase, ch=ch: e.activation(
                        sqb[:], qkraw[:, 2048*(rawbase+ch):2048*(rawbase+ch+1)], AF.Square),
                        waits=[(vC, vC.n)], inc=sC)
                    sqv = sC.n
                    for ng in range(4):
                        w4 = [(sC, sqv), (sC, sC.n)] if ng == 0 else []
                        p.op("T", lambda e, ng=ng: e.matmul(
                            PS(2, 4)[0:8, 512*ng:512*(ng+1)], sb["ind8"][:],
                            sqb[:, 512*ng:512*(ng+1)], start=True, stop=True), waits=w4)
                    p.grp("T", tC)
                    mmn = tC.n
                    p.op("S", lambda e, dstn=dstn, ch=ch: e.activation(
                        dstn[0:8, 2048*ch:2048*(ch+1)], PS(2, 4)[0:8, 0:2048], AF.Copy),
                        waits=[(tC, mmn)], inc=sC)
            nrm_s = sC.n
            p.op("V", lambda e: e.reduce_max(
                scr[0:8, 0:2], kn[0:8, :].rearrange("p (c k) -> p c k", k=2048), axis=AX.X),
                waits=[(sC, nrm_s)], inc=vC)
            mkv = vC.n
            mkd = p.dma("Y", lambda e: e.dma_start(out=maxk1[0:1, 0:16], in_=scr[0:8, 0:2],
                                                   allow_slow_non_contiguous=True),
                        waits=[(vC, mkv)])
            for cch in range(16):
                g_, si = cch // 4, cch % 4
                src_ch, src_r = cch // 8, 16 * (cch % 8)
                for (rawbase, dpad) in ((0, qpad), (2, kpad)):
                    p.dma("Y", lambda e, rawbase=rawbase, src_ch=src_ch, src_r=src_r,
                          g_=g_, si=si, dpad=dpad: e.dma_start(
                        out=dpad[32*si:32*si+16, 2048*g_:2048*(g_+1)],
                        in_=qkraw[src_r:src_r+16,
                                  2048*(rawbase+src_ch):2048*(rawbase+src_ch+1)]),
                        waits=[(sC, nrm_s)] if cch == 0 else [])
                rowd = p.dma("Y", lambda e, cch=cch, src_ch=src_ch: e.dma_start(
                    out=rowstb[:], in_=qn[cch % 8:cch % 8 + 1,
                                          2048*src_ch:2048*(src_ch+1)]),
                    waits=[(sC, bsts)] if cch > 0 else [(sC, nrm_s)])
                wS = [rowd, mkd]
                if cch > 0:
                    wS.append(brow_tok)
                p.op("S", lambda e, cch=cch: e.activation(
                    rowstb2[:], rowstb[:], AF.Sqrt,
                    scale=maxk1[0:1, cch:cch+1]), waits=wS, inc=sC)
                bsts = sC.n
                brow_tok = p.dma("Y", lambda e, g_=g_, si=si: e.dma_start(
                    out=qpad[32*si+16:32*si+17, 2048*g_:2048*(g_+1)], in_=rowstb2[:]),
                    waits=[(sC, bsts)])
            p.join(["T"], [vC, gC])

            ldwp = p.dma("G", lambda e: e.dma_start(
                out=wvp[:].rearrange("p (c k) -> p c k", k=1024),
                in_=io["bWpT"][:].rearrange("(c p) k -> p c k", p=128)),
                waits=[(tC, tC.n)])
            ldw1 = p.dma("G", lambda e: e.dma_start(
                out=w1buf[:].rearrange("p (c k) -> p c k", k=64),
                in_=io["bW1T"][:].rearrange("(c p) k -> p c k", p=128)))
            p.op("G", lambda e: e.memset(hW[64:65, :], 1.0), inc=gC)
            hwones = gC.n

            PSF = PS(0, 8)

            def AVAP(i):
                off = 512 * (4 + i // 7) + 65 * (i % 7)
                return PSF[:, off:off + 65]
            av_war = vC.n
            prev_av_last = tC.n
            exp15_prev = sC.n
            wp_v_prev = vC.n
            ptv_hist = {}
            bank_tok = {3: (vC, 0), 7: (vC, 0)}
            for ng in range(4):
                for cg in range(4):
                    gcg = 4 * ng + cg
                    obslot = 1024 * (gcg % 2)
                    avd = {}
                    exp_done = {}
                    for mg in range(16):
                        if mg >= 1:
                            w = [(sC, exp_done[mg - 1])]
                        elif gcg == 0:
                            w = [(sC, sC.n), (vC, vC.n)]
                        else:
                            w = [(sC, exp15_prev), bank_tok[3]]
                        for j in range(4):
                            for s in range(4):
                                cc = 32 * ((s + j) % 4)
                                p.op("T", lambda e, cg=cg, s=s, cc=cc, mg=mg, ng=ng:
                                     e.matmul(
                                    PSF[cc:cc+32, 512*s:512*s+512],
                                    kpad[32*s:32*s+17,
                                         2048*cg + 128*mg + cc: 2048*cg + 128*mg + cc + 32],
                                    qpad[32*s:32*s+17,
                                         2048*cg + 512*ng: 2048*cg + 512*(ng+1)],
                                    start=True, stop=True, tile_position=(32*s, cc)),
                                    waits=(w if (j == 0 and s == 0) else []))
                        p.grp("T", tC)
                        smm = tC.n
                        eslot = mg % 2
                        w = [(tC, smm)]
                        if mg >= 2:
                            w.append((tC, avd[mg - 2]))
                        elif gcg > 0 or it > 0 or b > 0:
                            w.append((tC, prev_av_last))
                        p.op("S", lambda e, eslot=eslot: e.activation(
                            expS[:, 2048*eslot:2048*(eslot+1)], PS(0, 4), AF.Exp),
                            waits=w, inc=sC)
                        exp_done[mg] = sC.n
                        for s in range(4):
                            for qq in range(4):
                                i = 4 * s + qq
                                w = []
                                if i == 0:
                                    w = [(sC, exp_done[mg])]
                                    if mg == 0:
                                        w += [(gC, vones), (vC, av_war),
                                              (vC, wp_v_prev)]
                                p.op("T", lambda e, i=i, s=s, qq=qq, mg=mg, cg=cg,
                                     eslot=eslot,
                                     first=(mg == 0 and i in (0, 7, 14)),
                                     last=(mg == 15 and i in (6, 13, 15)):
                                     e.matmul(
                                    AVAP(i),
                                    expS[:, 2048*eslot + 512*s + 128*qq:
                                         2048*eslot + 512*s + 128*(qq+1)],
                                    vaug[:, 1040*mg + 65*(4*cg+s):
                                         1040*mg + 65*(4*cg+s) + 65],
                                    start=first, stop=last), waits=w)
                        p.grp("T", tC)
                        avd[mg] = tC.n
                    av_last = prev_av_last = avd[15]
                    exp15_prev = exp_done[15]
                    for jb in range(3):
                        nsl = (7, 7, 2)[jb]
                        p.op("V", lambda e, jb=jb, nsl=nsl: e.reciprocal(
                            scr[:, 8*jb:8*jb+nsl].rearrange("p (g k) -> p g k", k=1),
                            PSF[:, 512*(4+jb):512*(4+jb) + 65*nsl].rearrange(
                                "p (g k) -> p g k", k=65)[:, :, 64:65]),
                            waits=([(tC, av_last)] if jb == 0 else []))
                    rcv = vC.n
                    for i in range(16):
                        w = []
                        if i == 0:
                            w = [(vC, rcv), (tC, ptv_hist.get(gcg - 2, 0))]
                        p.op("V", lambda e, i=i, obslot=obslot: e.tensor_scalar_mul(
                            obf[:, obslot + 64*i: obslot + 64*(i+1)],
                            AVAP(i)[:, 0:64], scr[:, 8*(i//7) + (i % 7):
                                                  8*(i//7) + (i % 7) + 1]), waits=w)
                    nrmv = vC.n
                    av_war = nrmv
                    for s in range(4):
                        tb = 3 if s % 2 == 0 else 7
                        for qq in range(4):
                            w = []
                            if qq == 0:
                                w = [(vC, nrmv), bank_tok[tb]]
                            p.op("T", lambda e, s=s, qq=qq, tb=tb, obslot=obslot:
                                 e.matmul(
                                PS(tb)[0:64, 128*qq:128*(qq+1)],
                                obf[:, obslot + 256*s + 64*qq:
                                    obslot + 256*s + 64*(qq+1)],
                                sb["ident"][:], start=True, stop=True), waits=w)
                        p.grp("T", tC)
                        ptv_s = tC.n
                        cch = 4 * cg + s
                        if s % 2 == 0:
                            p.op("V", lambda e, cch=cch, tb=tb: e.tensor_copy(
                                outF[64*(cch % 2):64*(cch % 2)+64,
                                     512*(cch // 2):512*(cch // 2 + 1)],
                                PS(tb)[0:64, 0:512]), waits=[(tC, ptv_s)], inc=vC)
                            bank_tok[tb] = (vC, vC.n)
                        else:
                            p.op("S", lambda e, cch=cch, tb=tb: e.activation(
                                outF[64*(cch % 2):64*(cch % 2)+64,
                                     512*(cch // 2):512*(cch // 2 + 1)],
                                PS(tb)[0:64, 0:512], AF.Copy), waits=[(tC, ptv_s)], inc=sC)
                            bank_tok[tb] = (sC, sC.n)
                    ptv_hist[gcg] = tC.n
                out_done = vC.n
                wp_v = 0
                for tc in range(4):
                    t = 4 * ng + tc
                    e_sb = et[:, 1024*tc:1024*(tc+1)]
                    prev_t = (4 * (ng - 1)) % 16 + tc
                    lde = p.dma("Y", lambda e, b=b, t=t, e_sb=e_sb: e.dma_start(
                        out=e_sb, in_=es[b, 128*t:128*(t+1), :]),
                        waits=[(vC, out_done), ewr[t], ewr[prev_t]])
                    for half in range(2):
                        bank = 6 + half
                        for kc in range(8):
                            w2 = [(vC, vC.n)] if kc == 0 else []
                            if tc == 0 and half == 0 and kc == 0:
                                w2 += [ldwp, (vC, out_done), (sC, sC.n)]
                            p.op("T", lambda e, kc=kc, tc=tc, half=half, bank=bank: e.matmul(
                                PS(bank), outF[:, 512*kc + 128*tc: 512*kc + 128*(tc+1)],
                                wvp[:, 1024*kc + 512*half: 1024*kc + 512*(half+1)],
                                start=(kc == 0), stop=(kc == 7)), waits=w2)
                        p.grp("T", tC)
                        mmv = tC.n
                        p.op("V", lambda e, e_sb=e_sb, half=half, bank=bank: e.tensor_add(
                            e_sb[:, 512*half:512*(half+1)], e_sb[:, 512*half:512*(half+1)],
                            PS(bank)), waits=[(tC, mmv), lde], inc=vC)
                    wp_v = wp_v_prev = vC.n
                rtr = 0
                for tc in range(4):
                    e_sb = et[:, 1024*tc:1024*(tc+1)]
                    p.op("V", lambda e, e_sb=e_sb: e.tensor_copy(ebf[:], e_sb),
                         waits=[(vC, wp_v)] if tc == 0 else [(tC, rtr)], inc=vC)
                    cpv = vC.n
                    rtr = transpose_tile(
                        ctx, ebf, lambda j0, w_, tc=tc: resF[:].rearrange(
                            "p (c k) -> p c k", k=512)[:, j0:j0+w_, 128*tc:128*(tc+1)],
                        ncol=8, pre_wait=[(vC, cpv)], bank=7)
                for kc in range(8):
                    w2 = [ldw1, (sC, sC.n), (vC, vC.n)] if kc == 0 else []
                    p.op("T", lambda e, kc=kc: e.matmul(
                        PS(7)[0:64, 0:512], w1buf[:, 64*kc:64*(kc+1)],
                        resF[:].rearrange("p (c k) -> p c k", k=512)[:, kc, :],
                        start=(kc == 0), stop=(kc == 7)), waits=w2)
                p.grp("T", tC)
                w1v = tC.n
                p.op("S", lambda e: e.activation(
                    hW[0:64, :], PS(7)[0:64, 0:512], AF.Relu, bias=sb["b_b1"][:]),
                    waits=[(tC, w1v)], inc=sC)
                hWs = sC.n
                lastit = it == 1
                for tc in range(4):
                    t = 4 * ng + tc
                    e_sb = et[:, 1024*tc:1024*(tc+1)]
                    ldx = None
                    if lastit:
                        ldx = p.dma("Y", lambda e, b=b, t=t: e.dma_start(
                            out=scr[:, 0:1024], in_=src[b, 128*t:128*(t+1), :]),
                            waits=[(vC, vC.n)])
                    for half in range(2):
                        bank = 7
                        w2 = [(sC, hWs), (gC, hwones), (vC, vC.n)]
                        p.op("T", lambda e, tc=tc, half=half, bank=bank: e.matmul(
                            PS(bank), hW[0:65, 128*tc:128*(tc+1)],
                            sb["bW2aug"][0:65, 512*half:512*(half+1)],
                            start=True, stop=True), waits=w2, inc=tC)
                        mmv = tC.n
                        if lastit:
                            p.op("V", lambda e, e_sb=e_sb, half=half, bank=bank: e.tensor_add(
                                e_sb[:, 512*half:512*(half+1)],
                                e_sb[:, 512*half:512*(half+1)], PS(bank)),
                                waits=[(tC, mmv)])
                            p.op("V", lambda e, e_sb=e_sb, half=half: e.tensor_add(
                                e_sb[:, 512*half:512*(half+1)],
                                e_sb[:, 512*half:512*(half+1)], scr[:, 512*half:512*(half+1)]),
                                waits=[ldx], inc=vC)
                        else:
                            p.op("V", lambda e, e_sb=e_sb, half=half, bank=bank: e.tensor_add(
                                e_sb[:, 512*half:512*(half+1)],
                                e_sb[:, 512*half:512*(half+1)], PS(bank)),
                                waits=[(tC, mmv)], inc=vC)
                    uv = vC.n
                    outdst = dst if lastit else es
                    ewr[t] = p.dma("Y", lambda e, b=b, t=t, e_sb=e_sb, outdst=outdst:
                                   e.dma_start(
                        out=outdst[b, 128*t:128*(t+1), :], in_=e_sb),
                        waits=[(vC, uv)])
                bank_tok[7] = (vC, vC.n)
            p.join(["T", "S", "V", "G"], [])
        p.join(["T", "S", "V", "G", "Y"], [vC, sC, tC, gC])

def build_program():
    nc = bass.Bass("TRN2", target_bir_lowering=False)
    io = {}
    def din(name, shape, dtype=F32):
        io[name] = nc.dram_tensor(name, list(shape), dtype, kind="ExternalInput")
    din("x", [B_LOC, S, D])
    for n, s_ in (("aWqT", [D, D]), ("aWkT", [D, D]), ("aWvT", [D, D]), ("aWoT", [D, D]),
                  ("a_bqk", [128, 16]), ("a_bv", [1, D]), ("a_bo", [1, D]),
                  ("bWqT", [D, 256]), ("bWkT", [D, 256]), ("bWvT", [D, D]), ("bWpT", [D, D]),
                  ("bW1T", [D, 64]), ("b_b1", [64, 1]), ("bW2aug", [65, D]),
                  ("f_wqkv", [2, D, 1536]), ("f_wo", [2, 512, D]),
                  ("f_w1p", [2, D, 22 * 256]), ("f_b1c", [2, 128, 44]),
                  ("f_w2p", [2, 22, 128, D]), ("f_b2", [2, 1, D]),
                  ("f_lng", [2, 1, D]), ("f_lnb", [2, 1, D]),
                  ("f_qs8", [2, 512]), ("f_ks8", [2, 512]), ("onesf32", [1, 128]),
                  ("maskadd", [128, 256]),
                  ("rot_qc", [S, 64]), ("rot_qs", [S, 64]), ("rot_kc", [S, 64]),
                  ("rot_ks", [S, 64])):
        din(n, s_)
    for n, s_ in (("ident", [128, 128]), ("maskP12", [128, 256]), ("onesbf", [1, 128]),
                  ("ind8", [128, 8]), ("negrow", [1, S])):
        din(n, s_, BF16)
    io["dbg"] = nc.dram_tensor("dbg", [128, 16384], BF16, kind="ExternalOutput")
    io["dbg2"] = nc.dram_tensor("dbg2", [128, 16384], BF16, kind="ExternalOutput")
    io["dbg3"] = nc.dram_tensor("dbg3", [128, 16384], BF16, kind="ExternalOutput")
    io["xs"] = nc.dram_tensor("xs", [B_LOC, S, D], F32, kind="Internal")
    io["zs"] = nc.dram_tensor("zs", [B_LOC, S, D], F32, kind="Internal")
    io["y"] = nc.dram_tensor("y", [B_LOC, S, D], F32, kind="ExternalOutput")

    p = Plan()
    C = {n: p.ctr(n) for n in ["tC", "sC", "vC", "gC", "dY", "dG"]}
    p.auto = {"V": C["vC"], "S": C["sC"], "G": C["gC"]}
    p.mkqueue("Y", 10)
    p.mkqueue("G", 8)
    mem = Mem(nc)
    sb = {}
    sb["ident"] = mem.seq("c_ident", [128, 128], BF16)
    sb["maskP12"] = mem.seq("c_maskP12", [128, 256], BF16)
    sb["maskadd"] = mem.seq("c_maskadd", [128, 256], F32)
    sb["row2bf"] = mem.seq("c_row2bf", [1, D], BF16)
    sb["onesbf"] = mem.seq("c_onesbf", [1, 128], BF16)
    sb["onesf32"] = mem.seq("c_onesf32", [1, 128], F32)
    sb["ind8"] = mem.seq("c_ind8", [128, 8], BF16)
    sb["a_bqk"] = mem.seq("c_a_bqk", [128, 16], F32)
    sb["stats"] = mem.seq("c_stats", [128, 64], F32)
    sb["b_b1"] = mem.seq("c_b_b1", [64, 1], F32)
    sb["bW2aug"] = mem.seq("c_bW2aug", [65, D], BF16)
    sb["qs8"] = mem.seq("c_qs8", [128, 512], F32)
    sb["ks8"] = mem.seq("c_ks8", [128, 512], F32)
    PHASE0 = mem.mark()

    psum = nc.alloc_psum_tensor("ps", [128, 4096], F32)
    def PS(b_, w=1):
        return psum[:, 512*b_:512*(b_+w)]

    for n in ["ident", "maskP12", "maskadd", "onesbf", "onesf32", "ind8", "a_bqk",
              "b_b1"]:
        p.dma("Y", lambda e, n=n: e.dma_start(out=sb[n][:], in_=io[n][:]))
    p.dma("G", lambda e: e.dma_start(out=sb["bW2aug"][:], in_=io["bW2aug"][:]))
    p.wait("T", p.dma_all())
    p.wait("S", p.dma_all())
    p.wait("V", p.dma_all())
    ctx = dict(nc=nc, io=io, p=p, C=C, sb=sb, mem=mem, PS=PS, PHASE0=PHASE0)

    emit_attn_a(ctx, src=io["x"], dst=(io["xs"] if PHASES > 1 else io["y"]))
    if PHASES > 1:
        emit_ffn(ctx, stream=io["xs"], dst=(io["xs"] if PHASES > 2 else io["y"]))
    if PHASES > 2:
        emit_attn_b(ctx, src=io["xs"], dst=(io["xs"] if PHASES > 3 else io["y"]))
    if PHASES > 3:
        emit_ffn(ctx, stream=io["xs"], dst=io["y"])

    p.wait("Y", p.dma_all())

    import contextlib
    stack = contextlib.ExitStack()
    for c in p.ctrs:
        c.h = stack.enter_context(nc.semaphore(c.name))
    with nc.Block() as block:
        @block.tensor
        def _(e):
            for f in p.items["T"]:
                f(e)
        @block.scalar
        def _(e):
            for f in p.items["S"]:
                f(e)
        @block.vector
        def _(e):
            for f in p.items["V"]:
                f(e)
        @block.gpsimd
        def _(e):
            for f in p.items["G"]:
                f(e)
        @block.sync
        def _(e):
            for f in p.items["Y"]:
                f(e)
    stack.close()
    nc.dbg_mem = mem
    return nc

def kernel(**inputs):
    d = prepare_host(inputs)
    x = np.asarray(inputs["L0_attn_a_x"], dtype=np.float32)
    nc = build_program()
    in_maps = []
    for c in range(N_CORES):
        m = dict(d)
        m["x"] = np.ascontiguousarray(x[2*c:2*c+2])
        in_maps.append(m)
    res = run_bass_kernel_spmd(nc, in_maps, core_ids=list(range(N_CORES)))
    return np.concatenate([r["y"] for r in res.results], axis=0)
